# version 12
# speedup vs baseline: 1.1910x; 1.0023x over previous
import math
import contextlib
import numpy as np
import ml_dtypes
import concourse.bass as bass
import concourse.mybir as mybir
from concourse.bass_utils import run_bass_kernel_spmd

F32 = mybir.dt.float32
BF16 = mybir.dt.bfloat16
I32 = mybir.dt.int32
U8 = mybir.dt.uint8
AF = mybir.ActivationFunctionType
ALU = mybir.AluOpType
AX = mybir.AxisListType

T = 2048
D = 1024
NT = 16
ND = 8
PI = math.pi

INPUT_SPECS = {
    "L0_attn_a_x": ((T, D), F32),
    "L0_attn_a_wq": ((D, D), F32), "L0_attn_a_wk": ((D, D), F32),
    "L0_attn_a_wv": ((D, D), F32), "L0_attn_a_wo": ((D, D), F32),
    "L0_attn_a_w_phase": ((D, 2), F32),
    "L0_attn_a_carrier": ((1, 16), F32), "L0_attn_a_lock": ((1, 16), F32),
    "L0_attn_a_bandwidth": ((1, 16), F32), "L0_attn_a_active_mask": ((1, 16), U8),
    "L0_attn_b_norm_weight": ((1, D), F32),
    "L0_attn_b_w_q": ((D, D), F32), "L0_attn_b_w_k": ((D, D), F32),
    "L0_attn_b_w_v": ((D, D), F32), "L0_attn_b_w_f": ((D, D), F32),
    "L0_attn_b_gnorm_weight": ((1, D), F32), "L0_attn_b_w_o": ((D, D), F32),
    "L0_ffn_weight": ((D, D), F32), "L0_ffn_bias": ((1, D), F32),
    "L1_attn_a_W_in": ((D, 2080), F32), "L1_attn_a_w_conv": ((4, 2080), F32),
    "L1_attn_a_b_conv": ((1, 2080), F32), "L1_attn_a_theta": ((32, 128), F32),
    "L1_attn_a_decay_slopes": ((1, 28), F32), "L1_attn_a_anchor_slopes": ((1, 4), F32),
    "L1_attn_a_score_scale": ((1, 32), F32), "L1_attn_a_deriv_logits": ((1, 3), F32),
    "L1_attn_a_norm_scale": ((1, 256), F32),
    "L1_attn_a_W_re": ((128, 32), F32), "L1_attn_a_W_im": ((128, 32), F32),
    "L1_attn_a_W_out": ((D, D), F32),
    "L1_attn_b_down_w": ((D, 512), F32), "L1_attn_b_down_b": ((1, 512), F32),
    "L1_attn_b_up_w": ((512, D), F32), "L1_attn_b_up_b": ((1, D), F32),
    "L1_attn_b_norm1_w": ((1, 512), F32), "L1_attn_b_norm2_w": ((1, D), F32),
    "L1_attn_b_qkv1_w": ((256, 768), F32), "L1_attn_b_qkv1_b": ((1, 768), F32),
    "L1_attn_b_proj1_w": ((256, 256), F32), "L1_attn_b_proj1_b": ((1, 256), F32),
    "L1_attn_b_rpb1": ((19, 8), F32),
    "L1_attn_b_qkv2_w": ((256, 768), F32), "L1_attn_b_qkv2_b": ((1, 768), F32),
    "L1_attn_b_proj2_w": ((256, 256), F32), "L1_attn_b_proj2_b": ((1, 256), F32),
    "L1_attn_b_rpb2": ((39, 8), F32),
    "L1_attn_b_ffn_w1": ((D, 2048), F32), "L1_attn_b_ffn_b1": ((1, 2048), F32),
    "L1_attn_b_ffn_w2": ((2048, D), F32), "L1_attn_b_ffn_b2": ((1, D), F32),
    "L1_ffn_weight": ((D, D), F32), "L1_ffn_bias": ((1, D), F32),
}

ENG_SECTION = {"S": "scalar", "V": "vector", "G": "gpsimd", "T": "tensor", "Y": "sync"}


class Ctx:
    def __init__(self, nc, es, blk):
        self.nc = nc
        self.es = es
        self.blk = blk
        self.cnt = {}
        self.sem = {}
        self.uid = 0

    def barrier(self):
        g = self.start_guard()
        for eng in ("V", "S", "T", "G"):
            sec = self.section(eng)

            @sec
            def _(e, g=g, eng=eng):
                for tok in g:
                    if tok[0] != eng:
                        self.wait(e, tok)

    def start_guard(self):
        return [("T", self.cnt["T"]), ("S", self.cnt["S"]), ("V", self.cnt["V"]),
                ("G", self.cnt["G"])]

    def guard_tokens(self):
        return [(e, c) for e, c in self.psum_guard.items()]

    def make_sems(self):
        self.psum_guard = {}
        for name in ("LD", "LG", "ST", "V", "S", "T", "G", "CC"):
            self.sem[name] = self.es.enter_context(self.nc.semaphore(name))
            self.cnt[name] = 0

    def sb(self, name, shape, dtype, scope=None):
        self.uid += 1
        return (scope or self.es).enter_context(
            self.nc.sbuf_tensor(f"{name}{self.uid}", shape, dtype))

    def f(self, eng_obj, eng, instr):
        tok = self.inc(instr, eng)
        eng_obj.wait_ge(self.sem[eng], tok[1])
        return tok

    def inc(self, instr, eng, n=1):
        self.cnt[eng] += n
        instr.then_inc(self.sem[eng], n)
        return (eng, self.cnt[eng])

    def tok(self, eng):
        return (eng, self.cnt[eng])

    def wait(self, engine, token):
        if token is not None:
            engine.wait_ge(self.sem[token[0]], token[1])

    def section(self, eng):
        return getattr(self.blk, ENG_SECTION[eng])

    def X(self, tt):
        return self.XRES[:, tt * D:(tt + 1) * D]

    def XL(self, tt):
        return self.XLNB[:, tt * D:(tt + 1) * D]

    def XF(self, d):
        return self.XFM[:, d * T:(d + 1) * T]

    def ON(self, d):
        return self.XLNB[:, d * T:(d + 1) * T]

    def ccol(self, v):
        assert v in self.cvals, f"const {v} not pre-registered"
        return self.CPOOL[:, self.cvals[v]:self.cvals[v] + 1]

    def dump(self, name, ap, wait_tok, hold=None):
        if not getattr(self, "debug", False):
            return
        if getattr(self, "dump_only", None) is not None and name not in self.dump_only:
            return
        self.uid += 1
        t = self.nc.dram_tensor(f"d_{name}_{self.uid}", list(ap.shape), ap.dtype,
                                kind="ExternalOutput").ap()

        @self.section("Y")
        def _(sy):
            self.wait(sy, wait_tok)
            sy.dma_start(t, ap).then_inc(self.sem["ST"], 16)
            self.cnt["ST"] += 16
        if hold:
            st_tok = ("ST", self.cnt["ST"])
            for hh in ([hold] if isinstance(hold, str) else hold):
                @self.section(hh)
                def _(e, hh=hh):
                    self.wait(e, st_tok)

    def dma(self, eng_obj, dst, src):
        nm = "LG" if str(getattr(eng_obj, "engine", "")) == "EngineType.Pool" else "LD"
        eng_obj.dma_start(dst, src).then_inc(self.sem[nm], 16)
        self.cnt[nm] += 16
        return (nm, self.cnt[nm])


def ecopy(e, dst, src):
    return e.copy(dst, src) if hasattr(e, "copy") else e.tensor_copy(dst, src)


def mm_pipeline(ctx, work, evac="S", nbuf=2, t_prewait=None, e_prewait=None):
    n = len(work)
    tbase = ctx.cnt["T"]
    ebase = ctx.cnt[evac]
    guards = ctx.guard_tokens()
    ctx.cnt["T"] = tbase + n
    ctx.cnt[evac] = ebase + n
    ctx.psum_guard[evac] = ebase + n

    @ctx.section("T")
    def _(t):
        for tok in (t_prewait or []):
            ctx.wait(t, tok)
        for tok in guards:
            ctx.wait(t, tok)
        for i, (emit_mms, _e) in enumerate(work):
            if i >= nbuf:
                t.wait_ge(ctx.sem[evac], ebase + i - nbuf + 1)
            emit_mms(t, i).then_inc(ctx.sem["T"], 1)

    @ctx.section(evac)
    def _(e):
        for tok in (e_prewait or []):
            ctx.wait(e, tok)
        for i, (_m, emit_evac) in enumerate(work):
            e.wait_ge(ctx.sem["T"], tbase + i + 1)
            emit_evac(e, i).then_inc(ctx.sem[evac], 1)

    return (evac, ctx.cnt[evac])


def emit_ln(ctx, wait_tok=None, rms=False, eps=1e-5, out_scale_col=None,
            gamma_bcast=None, apply_out=None, apply=True):
    blk = ctx.blk
    ssum, ssq, rstd, negb = ctx.LNsum, ctx.LNsq, ctx.LNrstd, ctx.LNnegb

    @ctx.section("V")
    def _(v):
        ctx.wait(v, wait_tok)
        if rms:
            ctx.f(v, "V", v.memset(ssum[:], 0.0))
        else:
            for tt in range(NT):
                ctx.inc(v.tensor_reduce(ssum[:, tt:tt + 1], ctx.X(tt), AX.X, ALU.add), "V")
            v.wait_ge(ctx.sem["V"], ctx.cnt["V"])
        ctx.inc(v.tensor_scalar(ssum[:], ssum[:], 1.0 / D, None, ALU.mult), "V")
    vt = ctx.tok("V")

    s0 = ctx.cnt["S"]
    v0 = ctx.cnt["V"]

    @ctx.section("S")
    def _(s):
        ctx.wait(s, wait_tok)
        for tt in range(NT):
            if tt >= 2:
                s.wait_ge(ctx.sem["V"], v0 + tt)
            s.activation(ctx.LNJ[:, (tt % 2) * 1024:(tt % 2) * 1024 + D],
                         ctx.X(tt), AF.Square).then_inc(ctx.sem["S"], 1)
    ctx.cnt["S"] = s0 + NT

    @ctx.section("V")
    def _(v):
        ctx.wait(v, vt)
        for tt in range(NT):
            v.wait_ge(ctx.sem["S"], s0 + tt + 1)
            v.tensor_reduce(ssq[:, tt:tt + 1],
                            ctx.LNJ[:, (tt % 2) * 1024:(tt % 2) * 1024 + D],
                            AX.X, ALU.add).then_inc(ctx.sem["V"], 1)
    ctx.cnt["V"] = v0 + NT + 1 - 1
    st = ctx.tok("V")

    @ctx.section("V")
    def _(v):
        ctx.wait(v, st)
        v.wait_ge(ctx.sem["V"], ctx.cnt["V"])
        ctx.f(v, "V", v.tensor_scalar(ssq[:], ssq[:], 1.0 / D, None, ALU.mult))
        ctx.f(v, "V", v.tensor_tensor(negb[:], ssum[:], ssum[:], ALU.mult))
        ctx.inc(v.tensor_tensor(ssq[:], ssq[:], negb[:], ALU.subtract), "V")
    vt2 = ctx.tok("V")

    @ctx.section("S")
    def _(s):
        ctx.wait(s, vt2)
        ctx.f(s, "S", s.activation(rstd[:], ssq[:], AF.Ln, bias=ctx.ccol(eps)))
        ctx.inc(s.activation(rstd[:], rstd[:], AF.Exp, scale=-0.5), "S")
    st2 = ctx.tok("S")

    @ctx.section("V")
    def _(v):
        ctx.wait(v, st2)
        if out_scale_col is not None:
            ctx.f(v, "V", v.tensor_scalar(rstd[:], rstd[:], out_scale_col, None, ALU.mult))
        ctx.inc(v.scalar_tensor_tensor(negb[:], ssum[:], -1.0, rstd[:],
                                       ALU.mult, ALU.mult), "V")
    vt3 = ctx.tok("V")

    done = vt3
    if apply:
        dst = apply_out or ctx.XL
        if gamma_bcast is None:
            @ctx.section("S")
            def _(s):
                ctx.wait(s, vt3)
                for tt in range(NT):
                    ctx.inc(s.activation(dst(tt), ctx.X(tt), AF.Identity,
                                         bias=negb[:, tt:tt + 1],
                                         scale=rstd[:, tt:tt + 1]), "S")
            done = ctx.tok("S")
        else:
            @ctx.section("V")
            def _(v):
                ctx.wait(v, vt3)
                for tt in range(NT):
                    ctx.inc(v.scalar_tensor_tensor(dst(tt), ctx.X(tt),
                                                   rstd[:, tt:tt + 1], gamma_bcast,
                                                   ALU.mult, ALU.mult), "V")
            done = ctx.tok("V")
    return rstd, negb, ssum, done


def emit_transpose_to_xfm(ctx, wait_tok, src=None, dst=None, nd=ND, nt=NT):
    src = src or ctx.XL
    dst = dst or ctx.XF
    work = []
    for tt in range(nt):
        for d in range(nd):
            def mk(tt=tt, d=d):
                def mm(t, i):
                    pb = (ctx.P0 if i % 2 == 0 else ctx.P1)[:].bitcast(BF16)
                    return t.transpose(pb[:, 0:128],
                                       src(tt)[:, d * 128:(d + 1) * 128], ctx.identb[:])

                def ev(e, i):
                    pb = (ctx.P0 if i % 2 == 0 else ctx.P1)[:].bitcast(BF16)
                    return ecopy(e, dst(d)[:, tt * 128:(tt + 1) * 128], pb[:, 0:128])
                return mm, ev
            work.append(mk())
    return mm_pipeline(ctx, work, evac="V", nbuf=2,
                       t_prewait=[wait_tok, ctx.consts_tok])


def load_w(ctx, eng_obj, src_ap, dst, cols=None):
    if cols is not None:
        src_ap = src_ap[:, cols[0]:cols[1]]
    R, C = src_ap.shape
    nr = R // 128
    return ctx.dma(eng_obj, dst[:, 0:nr * C].rearrange("p (a c) -> p a c", a=nr),
                   src_ap.rearrange("(a p) c -> p a c", p=128))


def emit_l0_attn_a(ctx, inp):
    H = 16
    es = contextlib.ExitStack()
    with es:
        _, _, _, lntok = emit_ln(ctx, wait_tok=ctx.x_tok)
        xfm_tok = emit_transpose_to_xfm(ctx, lntok)
        ctx.dump("xln", ctx.XLNB[:], lntok)
        ctx.dump("xfm", ctx.XFM[:], xfm_tok)

        wbuf = ctx.sb("waa", [128, 2 * 4096], BF16, es)
        wph = ctx.sb("wph", [128, 16], BF16, es)
        wphf = ctx.sb("wphf", [128, 16], F32, es)
        rows = ctx.sb("aarow", [1, 96], F32, es)
        am8 = ctx.sb("am8", [1, 16], U8, es)
        CARR = ctx.sb("CARR", [128, 80], F32, es)
        AFR = ctx.sb("afr", [1, 16], F32, es)
        LG = ctx.sb("LG", [128, 16 * NT], F32, es)
        PH0 = ctx.sb("PH0", [128, NT], F32, es)
        PH1 = ctx.sb("PH1", [128, NT], F32, es)
        RINV = ctx.sb("RINV", [128, NT], F32, es)
        gt1 = ctx.sb("gt1", [128, NT], F32, es)
        gt2 = ctx.sb("gt2", [128, NT], F32, es)
        qb = ctx.sb("qb", [128, 2 * T], BF16, es)
        kb = ctx.sb("kb", [128, 2 * T], BF16, es)
        vaug = ctx.sb("vaug", [128, NT * 4 * 65], BF16, es)
        expb = ctx.sb("expb", [128, 1024], BF16, es)
        bcsb = ctx.sb("bcsb", [128, 512], F32, es)
        rnrow = ctx.LNJ[0:1, 0:T]

        @ctx.section("Y")
        def _(sy):
            ctx.dma(sy, rows[:, 0:16], inp["L0_attn_a_carrier"])
            ctx.dma(sy, rows[:, 16:32], inp["L0_attn_a_lock"])
            ctx.dma(sy, rows[:, 32:48], inp["L0_attn_a_bandwidth"])
            ctx.dma(sy, am8[:], inp["L0_attn_a_active_mask"])

        @ctx.section("G")
        def _(g):
            g.memset(vaug[:].rearrange("p (a c) -> p a c", c=65)[:, :, 64:65], 1.0)
            ctx.inc(g.memset(ctx.onesb[:], 1.0), "G")
        wph_ld = None

        @ctx.section("Y")
        def _(sy):
            nonlocal wph_ld
            wph_ld = ctx.dma(sy, wphf[:].rearrange("p (a c) -> p a c", a=8),
                             inp["L0_attn_a_w_phase"].rearrange("(a p) c -> p a c", p=128))
        rows_tok = ctx.tok("LD")
        ones_tok = ctx.tok("G")

        @ctx.section("S")
        def _(s_):
            ctx.wait(s_, wph_ld)
            ctx.inc(s_.copy(wph[:], wphf[:]), "S")
        wph_tok = ctx.tok("S")
        ctx.dump("wphf", wphf[:], wph_ld)
        ctx.dump("wph", wph[:], wph_tok)

        rawb = ctx.sb("rawb", [1, 48], F32, es)

        @ctx.section("V")
        def _(v):
            ctx.wait(v, rows_tok)
            v.tensor_copy(rawb[:, 0:16], rows[:, 0:16])
            v.tensor_copy(rawb[:, 16:32], rows[:, 16:32])
            v.tensor_copy(rawb[:, 32:48], rows[:, 32:48])
            ctx.inc(v.tensor_copy(AFR[:], am8[:]), "V")
        vt0 = ctx.tok("V")

        def cb_mm(t, i):
            return t.matmul(ctx.PO[0:128, 0:48], ctx.onesf[0:1, :], rawb[:],
                            start=True, stop=True)

        def cb_ev(e, i):
            e.copy(CARR[:, 0:16], ctx.PO[0:128, 0:16])
            e.copy(CARR[:, 48:64], ctx.PO[0:128, 16:32])
            return e.copy(CARR[:, 32:48], ctx.PO[0:128, 32:48])
        carr0_tok = mm_pipeline(ctx, [(cb_mm, cb_ev)], evac="S",
                                t_prewait=[vt0, ctx.consts_tok])

        dbg4 = ctx.sb("dbg4", [128, 64], F32, es)

        @ctx.section("V")
        def _(v):
            ctx.wait(v, carr0_tok)
            v.memset(ctx.LNJ[:, 0:512], 0.0)
            ctx.inc(v.tensor_scalar(ctx.LNJ[:, 0:16], CARR[:, 0:16], 1.0 / (2 * PI), None, ALU.mult), "V")
            v.wait_ge(ctx.sem["V"], ctx.cnt["V"])
            v.tensor_copy(ctx.II[:, 0:512], ctx.LNJ[:, 0:512])
            v.wait_ge(ctx.sem["V"], ctx.cnt["V"]) if False else None
            ctx.inc(v.tensor_copy(ctx.LNJ[:, 512:1024], ctx.II[:, 0:512]), "V")
            v.wait_ge(ctx.sem["V"], ctx.cnt["V"])
            ctx.inc(v.tensor_copy(gt1[:], ctx.LNJ[:, 512:528]), "V")
            v.wait_ge(ctx.sem["V"], ctx.cnt["V"])
            v.scalar_tensor_tensor(gt2[:], gt1[:], -2 * PI, CARR[:, 0:16],
                                   ALU.mult, ALU.add)
            v.tensor_copy(dbg4[:, 0:16], ctx.LNJ[:, 0:16])
            v.tensor_copy(dbg4[:, 16:32], gt1[:])
            v.tensor_copy(dbg4[:, 32:48], gt2[:])
            v.tensor_copy(dbg4[:, 48:64], CARR[:, 0:16])
            ctx.f(v, "V", v.reciprocal(CARR[:, 32:48], CARR[:, 32:48]))
            ctx.inc(v.tensor_scalar(CARR[:, 64:80], CARR[:, 48:64], -1.0,
                                    1.0 + 1e-6, ALU.mult, ALU.add), "V")
        vtc = ctx.tok("V")

        ctx.dump("dbg4", dbg4[:], vtc)

        @ctx.section("S")
        def _(s):
            ctx.wait(s, vtc)
            ctx.f(s, "S", s.activation(gt1[:], gt2[:], AF.Abs))
            s.activation(CARR[:, 16:32], gt2[:], AF.Sin)
            ctx.inc(s.activation(CARR[:, 0:16], gt1[:], AF.Sin,
                                 scale=-1.0, bias=ctx.ccol(PI / 2)), "S")
        carr_tok = ctx.tok("S")

        ph_work = []
        for tt in range(NT):
            def mk(tt=tt):
                def mm(t, i):
                    pb = ctx.P0 if i % 2 == 0 else ctx.P1
                    for d in range(ND):
                        r = t.matmul(pb[:, 0:2], ctx.XF(d)[:, tt * 128:(tt + 1) * 128],
                                     wph[:, d * 2:(d + 1) * 2],
                                     start=(d == 0), stop=(d == ND - 1))
                    return r

                def ev(e, i):
                    pb = ctx.P0 if i % 2 == 0 else ctx.P1
                    e.copy(PH0[:, tt:tt + 1], pb[:, 0:1])
                    return e.copy(PH1[:, tt:tt + 1], pb[:, 1:2])
                return mm, ev
            ph_work.append(mk())
        ph_tok = mm_pipeline(ctx, ph_work, evac="S",
                             t_prewait=[xfm_tok, wph_tok, ctx.tok("LD")])

        ctx.dump("ph0", PH0[:], ph_tok)
        ctx.dump("ph1", PH1[:], ph_tok)
        ctx.dump("carr", CARR[:], carr_tok)
        @ctx.section("V")
        def _(v):
            ctx.wait(v, ph_tok)
            ctx.wait(v, carr_tok)
            v.tensor_tensor(gt1[:], PH0[:], PH0[:], ALU.mult)
            ctx.f(v, "V", v.tensor_tensor(gt2[:], PH1[:], PH1[:], ALU.mult))
            ctx.inc(v.tensor_tensor(RINV[:], gt1[:], gt2[:], ALU.add), "V")
        vt3 = ctx.tok("V")

        @ctx.section("S")
        def _(s):
            ctx.wait(s, vt3)
            ctx.f(s, "S", s.activation(RINV[:], RINV[:], AF.Ln, bias=ctx.ccol(1e-30)))
            ctx.inc(s.activation(RINV[:], RINV[:], AF.Exp, scale=-0.5), "S")
        st3 = ctx.tok("S")

        for h in range(H):
            cc, sc = CARR[:, h:h + 1], CARR[:, 16 + h:17 + h]
            ib = CARR[:, 32 + h:33 + h]
            lk, ol = CARR[:, 48 + h:49 + h], CARR[:, 64 + h:65 + h]
            lgh = LG[:, NT * h:NT * (h + 1)]

            @ctx.section("V")
            def _(v, cc=cc, sc=sc, ib=ib, lgh=lgh, h=h):
                if h == 0:
                    ctx.wait(v, st3)
                ctx.f(v, "V", v.tensor_scalar(gt1[:], PH1[:], cc, None, ALU.mult))
                ctx.f(v, "V", v.scalar_tensor_tensor(gt2[:], PH0[:], sc, gt1[:], ALU.mult, ALU.add))
                ctx.f(v, "V", v.tensor_tensor(gt2[:], gt2[:], RINV[:], ALU.mult))
                ctx.inc(v.tensor_scalar(lgh[:], gt2[:], ib, ib,
                                        ALU.mult, ALU.subtract), "V")
            vth = ctx.tok("V")

            @ctx.section("S")
            def _(s, lgh=lgh, vth=vth):
                ctx.wait(s, vth)
                ctx.inc(s.activation(lgh[:], lgh[:], AF.Exp), "S")
            sth = ctx.tok("S")

            @ctx.section("V")
            def _(v, lk=lk, ol=ol, lgh=lgh, sth=sth):
                ctx.wait(v, sth)
                ctx.inc(v.tensor_scalar(lgh[:], lgh[:], lk, ol, ALU.mult, ALU.add), "V")
            vth2 = ctx.tok("V")

            @ctx.section("S")
            def _(s, lgh=lgh, vth2=vth2):
                ctx.wait(s, vth2)
                ctx.inc(s.activation(lgh[:], lgh[:], AF.Ln), "S")
        lg_tok = ctx.tok("S")

        ctx.dump("lg", LG[:], lg_tok)
        GH = 4
        grp_T = [None, None]
        for grp in range(4):
            cols = (256 * grp, 256 * grp + 256)

            wb0 = (grp % 2) * 4096

            @ctx.section("G")
            def _(g, cols=cols, grp=grp, wb0=wb0, pv=grp_T[grp % 2]):
                if pv is not None:
                    ctx.wait(g, pv)
                load_w(ctx, g, inp["L0_attn_a_wq"], wbuf[:, wb0:wb0 + 2048], cols)
                load_w(ctx, g, inp["L0_attn_a_wk"], wbuf[:, wb0 + 2048:wb0 + 4096], cols)
            qk_ld = ctx.tok("LG")

            for wi in range(2):
                dstw = wbuf[:, wb0 + wi * 2048:wb0 + (wi + 1) * 2048]
                qdst = qb if wi == 0 else kb
                work = []
                for dqt in range(2):
                    for tch in range(4):
                        def mk(dqt=dqt, tch=tch, dstw=dstw, qdst=qdst):
                            def mm(t, i):
                                pb = ctx.P0 if i % 2 == 0 else ctx.P1
                                for d in range(ND):
                                    r = t.matmul(
                                        pb[:],
                                        dstw[:, d * 256 + dqt * 128:d * 256 + dqt * 128 + 128],
                                        ctx.XF(d)[:, tch * 512:(tch + 1) * 512],
                                        start=(d == 0), stop=(d == ND - 1))
                                return r

                            def ev(e, i):
                                pb = ctx.P0 if i % 2 == 0 else ctx.P1
                                return e.copy(
                                    qdst[:, dqt * T + tch * 512:dqt * T + tch * 512 + 512],
                                    pb[:])
                            return mm, ev
                        work.append(mk())
                pre = [qk_ld] + ([xfm_tok] if grp == 0 and wi == 0 else [])
                mm_pipeline(ctx, work, evac="S", t_prewait=pre)
            qk_done_T = ctx.tok("T")

            @ctx.section("G")
            def _(g, cols=cols, qk_done_T=qk_done_T, wb0=wb0):
                ctx.wait(g, qk_done_T)
                load_w(ctx, g, inp["L0_attn_a_wv"], wbuf[:, wb0:wb0 + 2048], cols)
            wv_ld = ctx.tok("LG")

            vwork = []
            for tt in range(NT):
                def mk(tt=tt):
                    def mm(t, i):
                        pb = ctx.P0 if i % 2 == 0 else ctx.P1
                        for d in range(ND):
                            r = t.matmul(pb[:, 0:256], ctx.XF(d)[:, tt * 128:(tt + 1) * 128],
                                         wbuf[:, wb0 + d * 256:wb0 + (d + 1) * 256],
                                         start=(d == 0), stop=(d == ND - 1))
                        return r

                    def ev(e, i):
                        pb = ctx.P0 if i % 2 == 0 else ctx.P1
                        for hl in range(GH):
                            r = e.copy(vaug[:, (tt * GH + hl) * 65:(tt * GH + hl) * 65 + 64],
                                       pb[:, hl * 64:(hl + 1) * 64])
                        return r
                    return mm, ev
                vwork.append(mk())
            mm_pipeline(ctx, vwork, evac="S", t_prewait=[wv_ld])
            v_done_S = ctx.tok("S")
            grp_T[grp % 2] = ctx.tok("T")

            for hl in range(GH):
                Hh = grp * GH + hl
                til, base = hl // 2, (hl % 2) * 64
                qh = lambda c0, c1, til=til, base=base: qb[base:base + 64, til * T + c0:til * T + c1]
                kh = lambda c0, c1, til=til, base=base: kb[base:base + 64, til * T + c0:til * T + c1]

                segs = []
                for kt in range(NT):
                    for c in range(kt // 4, 4):
                        q0 = max(kt * 128, c * 512)
                        segs.append((kt, c, q0, (c + 1) * 512, q0 == kt * 128))

                t0, s0, v0 = ctx.cnt["T"], ctx.cnt["S"], ctx.cnt["V"]
                ndiag = sum(1 for sgg in segs if sgg[4])
                ctx.cnt["T"] += 2 * len(segs)
                ctx.cnt["S"] += len(segs)
                ctx.cnt["V"] += ndiag

                head_guards = ctx.guard_tokens()

                @ctx.section("T")
                def _(t, segs=segs, t0=t0, s0=s0, v0=v0, hl=hl, grp=grp, Hh=Hh,
                      qh=qh, kh=kh, v_done_S=v_done_S, head_guards=head_guards):
                    ctx.wait(t, v_done_S)
                    t.wait_ge(ctx.sem["S"], s0)
                    for tokg in head_guards:
                        ctx.wait(t, tokg)
                    if hl == 0 and grp == 0:
                        ctx.wait(t, lg_tok)
                    dcount = 0
                    for i, (kt, c, q0, q1, diag) in enumerate(segs):
                        pb = ctx.P0 if i % 2 == 0 else ctx.P1
                        if i >= 2:
                            t.wait_ge(ctx.sem["S"], s0 + i - 1)
                        t.matmul(pb[:, 0:q1 - q0], kh(kt * 128, kt * 128 + 128),
                                 qh(q0, q1), start=True, stop=True)\
                            .then_inc(ctx.sem["T"], 1)
                        if diag:
                            dcount += 1
                            t.wait_ge(ctx.sem["V"], v0 + dcount)
                        else:
                            t.wait_ge(ctx.sem["S"], s0 + i + 1)
                        t.matmul(ctx.PO[0:65, q0:q1],
                                 vaug[:, (kt * GH + hl) * 65:(kt * GH + hl) * 65 + 65],
                                 expb[:, (i % 2) * 512:(i % 2) * 512 + (q1 - q0)],
                                 start=(kt == 0), stop=(kt == min(15, 4 * c + 3)))\
                            .then_inc(ctx.sem["T"], 1)

                @ctx.section("S")
                def _(s, segs=segs, t0=t0, Hh=Hh):
                    for i, (kt, c, q0, q1, diag) in enumerate(segs):
                        pb = ctx.P0 if i % 2 == 0 else ctx.P1
                        s.wait_ge(ctx.sem["T"], t0 + 2 * i + 1)
                        s.activation(expb[:, (i % 2) * 512:(i % 2) * 512 + (q1 - q0)],
                                     pb[:, 0:q1 - q0], AF.Exp, scale=0.125,
                                     bias=LG[:, NT * Hh + kt:NT * Hh + kt + 1])\
                            .then_inc(ctx.sem["S"], 1)

                @ctx.section("V")
                def _(v, segs=segs, s0=s0):
                    for i, (kt, c, q0, q1, diag) in enumerate(segs):
                        if diag:
                            v.wait_ge(ctx.sem["S"], s0 + i + 1)
                            sl = expb[:, (i % 2) * 512:(i % 2) * 512 + 128]
                            v.tensor_tensor(sl, sl, ctx.tri01[:], ALU.mult)\
                                .then_inc(ctx.sem["V"], 1)

                attn_T = ctx.tok("T")

                @ctx.section("V")
                def _(v, attn_T=attn_T, Hh=Hh):
                    ctx.wait(v, attn_T)
                    ctx.f(v, "V", v.reciprocal(rnrow, ctx.PO[64:65, 0:T]))
                    ctx.inc(v.tensor_scalar(rnrow, rnrow,
                                            AFR[0:1, Hh:Hh + 1], None, ALU.mult), "V")
                rn_tok = ctx.tok("V")

                dtil, dbase = Hh // 2, (Hh % 2) * 64
                for c in range(4):
                    @ctx.section("T")
                    def _(t, c=c, rn_tok=rn_tok):
                        ctx.wait(t, rn_tok)
                        if c > 0:
                            ctx.wait(t, ("S", ctx.cnt["S"]))
                        ctx.inc(t.matmul(ctx.P2[0:64, :], ctx.onesf[0:1, 0:64],
                                         rnrow[0:1, c * 512:(c + 1) * 512].rearrange("o n -> o n") if False else rnrow[0:1, c * 512:(c + 1) * 512],
                                         start=True, stop=True), "T")
                    bt = ctx.tok("T")

                    @ctx.section("S")
                    def _(s, bt=bt):
                        ctx.wait(s, bt)
                        ctx.inc(s.copy(bcsb[0:64, :], ctx.P2[0:64, :]), "S")
                    bs = ctx.tok("S")

                    @ctx.section("V")
                    def _(v, c=c, bs=bs, dtil=dtil, dbase=dbase):
                        ctx.wait(v, bs)
                        ctx.inc(v.tensor_tensor(
                            ctx.ON(dtil)[dbase:dbase + 64, c * 512:(c + 1) * 512],
                            ctx.PO[0:64, c * 512:(c + 1) * 512],
                            bcsb[0:64, :], ALU.mult), "V")
                on_tok = ctx.tok("V")
                ctx.psum_guard["V"] = on_tok[1]
                ctx.psum_guard["S"] = ctx.cnt["S"]

        ctx.dump("qb3", qb[:], ("S", ctx.cnt["S"]))
        ctx.dump("kb3", kb[:], ("S", ctx.cnt["S"]))
        ctx.dump("vaug3", vaug[:], ("S", ctx.cnt["S"]))
        ctx.dump("on", ctx.XLNB[:], ("V", ctx.cnt["V"]))
        @ctx.section("G")
        def _(g):
            ctx.wait(g, ("T", ctx.cnt["T"]))
            load_w(ctx, g, inp["L0_attn_a_wo"], wbuf[:, 0:8192])
        wo_ld = ctx.tok("LG")

        wwork = []
        for tt in range(NT):
            for oc in range(2):
                def mk(tt=tt, oc=oc):
                    def mm(t, i):
                        pb = ctx.P0 if i % 2 == 0 else ctx.P1
                        for d in range(ND):
                            r = t.matmul(pb[:], ctx.ON(d)[:, tt * 128:(tt + 1) * 128],
                                         wbuf[:, d * 1024 + oc * 512:d * 1024 + oc * 512 + 512],
                                         start=(d == 0), stop=(d == ND - 1))
                        return r

                    def ev(e, i):
                        pb = ctx.P0 if i % 2 == 0 else ctx.P1
                        sl = ctx.X(tt)[:, oc * 512:(oc + 1) * 512]
                        return e.scalar_tensor_tensor(sl, pb[:], 1.0, sl,
                                                      ALU.mult, ALU.add)
                    return mm, ev
                wwork.append(mk())
        res_tok = mm_pipeline(ctx, wwork, evac="V", t_prewait=[wo_ld])
        ctx.x_tok = res_tok


def build_program(n_sub=8, debug=False, dump_only=None):
    nc = bass.Bass("TRN2", target_bir_lowering=False)
    inp = {k: nc.dram_tensor(k, list(s), d, kind="ExternalInput").ap()
           for k, (s, d) in INPUT_SPECS.items()}
    y = nc.dram_tensor("y", [T, D], F32, kind="ExternalOutput").ap()

    es = contextlib.ExitStack()
    with es:
        blk = es.enter_context(nc.Block())
        ctx = Ctx(nc, es, blk)
        ctx.debug = debug
        ctx.dump_only = dump_only
        ctx.make_sems()
        ctx.XRES = ctx.sb("XRES", [128, NT * D], F32)
        ctx.XLNB = ctx.sb("XLNB", [128, NT * D], BF16)
        ctx.XFM = ctx.sb("XFM", [128, ND * T], BF16)
        ctx.LNJ = ctx.sb("LNJ", [128, 2048], F32)
        ctx.II = ctx.sb("II", [128, 512], I32)
        ctx.CCIN = nc.dram_tensor("ccin", [1, 1], F32)
        ctx.CCOUT = nc.dram_tensor("ccout", [1, 1], F32)
        ctx.LNsum = ctx.sb("LNsum", [128, NT], F32)
        ctx.LNsq = ctx.sb("LNsq", [128, NT], F32)
        ctx.LNrstd = ctx.sb("LNrstd", [128, NT], F32)
        ctx.LNnegb = ctx.sb("LNnegb", [128, NT], F32)
        ctx.identb = ctx.sb("identb", [128, 128], BF16)
        ctx.identf = ctx.sb("identf", [128, 128], F32)
        ctx.tri01 = ctx.sb("tri01", [128, 128], BF16)
        ctx.onesf = ctx.sb("onesf", [1, 128], F32)
        ctx.onesb = ctx.sb("onesb", [1, 128], BF16)
        gi = ctx.sb("gi", [128, 128], I32)
        CONST_VALUES = [1e-5, 1e-6, 1e-30, PI / 2, 1.0 + 1e-6, 1e-8, 0.5, 1.0,
                        2.0 ** 23, -2.0 ** 23, 127.0, -128.0, 448.0, 1e-4, 1e-12, 0.0]
        ctx.CPOOL = ctx.sb("CPOOL", [128, len(CONST_VALUES)], F32)
        ctx.cvals = {v: i for i, v in enumerate(CONST_VALUES)}
        ctx.P0 = es.enter_context(nc.psum_tensor("P0", [128, 512], F32))
        ctx.P1 = es.enter_context(nc.psum_tensor("P1", [128, 512], F32))
        ctx.P2 = es.enter_context(nc.psum_tensor("P2", [128, 512], F32))
        ctx.PB = es.enter_context(nc.psum_tensor("PB", [128, 512], BF16))
        ctx.PO = es.enter_context(nc.psum_tensor("PO", [128, 2048], F32))

        @ctx.section("G")
        def _(g):
            for v, i in ctx.cvals.items():
                g.memset(ctx.CPOOL[:, i:i + 1], float(v))
            g.iota(gi[:], [[1, 128]], base=0, channel_multiplier=-1)
            g.tensor_single_scalar(ctx.identb[:], gi[:], 0, ALU.is_equal)
            g.tensor_single_scalar(ctx.identf[:], gi[:], 0, ALU.is_equal)
            g.tensor_single_scalar(ctx.tri01[:], gi[:], 0, ALU.is_ge)
            g.memset(ctx.onesf[:], 1.0)
            ctx.inc(g.memset(ctx.onesb[:], 1.0), "G")
        ctx.consts_tok = ctx.tok("G")

        @ctx.section("Y")
        def _(sy):
            ctx.dma(sy, ctx.XRES[:].rearrange("p (a d) -> p a d", a=NT),
                    inp["L0_attn_a_x"].rearrange("(a p) d -> p a d", p=128))
        ctx.x_tok = ctx.tok("LD")

        es_f0 = contextlib.ExitStack()
        ffn0_state = {"scope": es_f0}
        emitters = [emit_l0_attn_a,
                    lambda c, i: emit_l0_ffn(c, i, ffn0_state),
                    emit_l0_attn_b,
                    lambda c, i: emit_l0_ffn(c, i, ffn0_state),
                    emit_l1_attn_a,
                    lambda c, i: emit_l1_ffn(c, i, {}),
                    emit_l1_attn_b,
                    lambda c, i: emit_l1_ffn(c, i, {})]
        for i_em, em in enumerate(emitters[:n_sub]):
            em(ctx, inp)
            if i_em == 3:
                es_f0.close()
        if n_sub <= 3:
            es_f0.close()

        @ctx.section("Y")
        def _(sy):
            ctx.wait(sy, ctx.x_tok)
            ctx.wait(sy, ctx.tok("V"))
            ctx.wait(sy, ctx.tok("S"))
            sy.dma_start(y.rearrange("(a p) d -> p a d", p=128),
                         ctx.XRES[:].rearrange("p (a d) -> p a d", a=NT))\
                .then_inc(ctx.sem["ST"], 16)
            ctx.cnt["ST"] += 16
            sy.wait_ge(ctx.sem["ST"], ctx.cnt["ST"])

    return nc


def kernel(**inputs):
    nc = build_program(n_sub=8)
    in_maps = []
    for b in range(8):
        m = {}
        for k, (shape, dt) in INPUT_SPECS.items():
            if k == "L0_attn_a_x":
                a = np.asarray(inputs[k][b])
            else:
                a = np.asarray(inputs[k])
                if k == "L0_attn_a_active_mask":
                    a = a.view(np.uint8)
            m[k] = np.ascontiguousarray(a.reshape(shape))
        in_maps.append(m)
    res = run_bass_kernel_spmd(nc, in_maps, core_ids=list(range(8)))
    return np.stack([np.asarray(r["y"], dtype=np.float32) for r in res.results])


def _e4m3_round(ctx, v, u_ap, out_ap, ii0, ii1, ii2, fscr):
    ub = u_ap.bitcast(I32)
    ctx.f(v, "V", v.tensor_scalar(ii0, ub, 23, 0xFF, ALU.logical_shift_right, ALU.bitwise_and))
    v.tensor_scalar(ii1, ii0, 3, None, ALU.subtract)
    ctx.f(v, "V", v.tensor_scalar(ii2, ii0, -1, 257, ALU.mult, ALU.add))
    v.tensor_scalar(ii1, ii1, 23, None, ALU.logical_shift_left)
    ctx.f(v, "V", v.tensor_scalar(ii2, ii2, 136, None, ALU.min))
    v.tensor_scalar(ii1, ii1, 118 << 23, None, ALU.max)
    ctx.f(v, "V", v.tensor_scalar(ii2, ii2, 23, None, ALU.logical_shift_left))
    ctx.f(v, "V", v.tensor_tensor(fscr, u_ap, ii2.bitcast(F32), ALU.mult))
    ctx.f(v, "V", v.tensor_copy(ii2, fscr))
    ctx.f(v, "V", v.tensor_copy(fscr, ii2))
    ctx.inc(v.tensor_tensor(out_ap, fscr, ii1.bitcast(F32), ALU.mult), "V")


def emit_l0_ffn(ctx, inp, state):
    ctx.barrier()
    es = contextlib.ExitStack()
    with es:
        if "wqT" not in state:
            sc = state.get("scope")
            state["wqT"] = ctx.sb("wqT", [128, 8192], BF16, sc)
            state["winv"] = ctx.sb("winv", [1, 2], F32, sc)
            state["bias_bc"] = ctx.sb("biasbc", [128, D], F32, sc)
        ctx.II2 = ctx.sb("II2", [128, 1024], I32, es)
        ctx.II3 = ctx.sb("II3", [128, 1024], I32, es)
        ctx.II4 = ctx.sb("II4", [128, 1024], I32, es)
        ctx.LNJ2 = ctx.sb("LNJ2", [128, 1024], F32, es)
        if "wqT" in state and "w_done" not in state:
            pass
        if "w_done" not in state:
            wf = ctx.sb("wf", [128, 4096], F32, es)
            wq8 = ctx.XLNB[:, 0:8192]
            wcol = ctx.sb("wcol", [128, 9], F32, es)
            brow = ctx.sb("brow", [1, D], F32, es)
            swcol = ctx.sb("swcol", [128, 1], F32, es)

            wsrc = inp["L0_ffn_weight"].rearrange("(a p) c -> p a c", p=128)

            sg0 = ctx.start_guard()

            @ctx.section("Y")
            def _(sy):
                for tokk in sg0:
                    ctx.wait(sy, tokk)
                ctx.dma(sy, brow[:], inp["L0_ffn_bias"])
                ctx.dma(sy, wf[:].rearrange("p (a c) -> p a c", a=4), wsrc[:, 0:4, :])
            wld0 = ctx.tok("LD")

            @ctx.section("V")
            def _(v):
                ctx.wait(v, wld0)
                for d in range(4):
                    ctx.inc(v.tensor_reduce(wcol[:, d:d + 1], wf[:, d * 1024:(d + 1) * 1024],
                                            AX.X, ALU.max, apply_absolute_value=True), "V")
                v.wait_ge(ctx.sem["V"], ctx.cnt["V"])
                ctx.inc(v.tensor_copy(ctx.LNJ[:, 1100:1101], wcol[:, 3:4]), "V")
            wv0 = ctx.tok("V")

            @ctx.section("Y")
            def _(sy):
                ctx.wait(sy, wv0)
                ctx.dma(sy, wf[:].rearrange("p (a c) -> p a c", a=4), wsrc[:, 4:8, :])
            wld1 = ctx.tok("LD")

            @ctx.section("V")
            def _(v):
                ctx.wait(v, wld1)
                for d in range(4):
                    ctx.inc(v.tensor_reduce(wcol[:, 4 + d:5 + d], wf[:, d * 1024:(d + 1) * 1024],
                                            AX.X, ALU.max, apply_absolute_value=True), "V")
                v.wait_ge(ctx.sem["V"], ctx.cnt["V"])
                ctx.inc(v.tensor_reduce(wcol[:, 8:9], wcol[:, 0:8], AX.X, ALU.max), "V")
                v.wait_ge(ctx.sem["V"], ctx.cnt["V"])
            wv_tok = ctx.tok("V")

            @ctx.section("G")
            def _(g):
                ctx.wait(g, wv_tok)
                ctx.inc(g.tensor_reduce(wcol[0:1, 0:1], wcol[:, 8:9], AX.C, ALU.max), "G")
                g.wait_ge(ctx.sem["G"], ctx.cnt["G"])
            wg_tok = ctx.tok("G")

            @ctx.section("V")
            def _(v):
                ctx.wait(v, wg_tok)
                ctx.f(v, "V", v.tensor_scalar(wcol[0:1, 1:2], wcol[0:1, 0:1], 1e-12, None, ALU.max))
                ctx.f(v, "V", v.reciprocal(wcol[0:1, 2:3], wcol[0:1, 1:2]))
                ctx.f(v, "V", v.tensor_scalar(wcol[0:1, 3:4], wcol[0:1, 2:3], 448.0, 448.0, ALU.mult, ALU.min))
                ctx.inc(v.reciprocal(state["winv"][0:1, 0:1], wcol[0:1, 3:4]), "V")
            winv_tok = ctx.tok("V")

            def swb_mm(t, i):
                return t.matmul(ctx.P2[0:128, 0:1], ctx.onesf[0:1, :],
                                wcol[0:1, 3:4], start=True, stop=True)

            def swb_ev(e, i):
                return e.copy(swcol[:], ctx.P2[0:128, 0:1])
            swb_tok = mm_pipeline(ctx, [(swb_mm, swb_ev)], evac="S",
                                  t_prewait=[winv_tok])

            for half in range(2):
                @ctx.section("Y")
                def _(sy, half=half, vg=ctx.tok("V")):
                    ctx.wait(sy, vg)
                    ctx.dma(sy, wf[:].rearrange("p (a c) -> p a c", a=4),
                            wsrc[:, 4 * half:4 * half + 4, :])
                wldq = ctx.tok("LD")

                @ctx.section("V")
                def _(v, half=half, wldq=wldq, tguard=("T", ctx.cnt["T"])):
                    ctx.wait(v, wldq)
                    ctx.wait(v, swb_tok)
                    ctx.wait(v, tguard)
                    for hh in range(4):
                        sl = slice(half * 4096 + hh * 1024, half * 4096 + (hh + 1) * 1024)
                        lsl = slice(hh * 1024, (hh + 1) * 1024)
                        ctx.f(v, "V", v.tensor_scalar(ctx.LNJ[:, 0:1024], wf[:, lsl], swcol[:], None, ALU.mult))
                        _e4m3_round(ctx, v, ctx.LNJ[:, 0:1024], wq8[:, sl],
                                    ctx.II2[:, 0:1024], ctx.II3[:, 0:1024],
                                    ctx.II4[:, 0:1024], ctx.LNJ2[:, 0:1024])
            wq8_tok = ctx.tok("V")
            ctx.dump("wcol", wcol[:], winv_tok)

            def wt_src(tt):
                return wq8[:, tt * 1024:(tt + 1) * 1024]

            def wt_dst(d):
                return state["wqT"][:, d * 1024:(d + 1) * 1024]
            wtw = []
            for po in range(8):
                for d in range(8):
                    def mk(po=po, d=d):
                        def mm(t, i):
                            pb = (ctx.P0 if i % 2 == 0 else ctx.P1)[:].bitcast(BF16)
                            return t.transpose(pb[:, 0:128],
                                               wq8[:, po * 1024 + d * 128:po * 1024 + d * 128 + 128],
                                               ctx.identb[:])

                        def ev(e, i):
                            pb = (ctx.P0 if i % 2 == 0 else ctx.P1)[:].bitcast(BF16)
                            return e.copy(state["wqT"][:, d * 1024 + po * 128:d * 1024 + po * 128 + 128],
                                          pb[:, 0:128])
                        return mm, ev
                    wtw.append(mk())
            mm_pipeline(ctx, wtw, evac="S", t_prewait=[wq8_tok])

            def bb_mm(t, i):
                return t.matmul(ctx.PO[0:128, 0:512], ctx.onesf[0:1, :],
                                brow[0:1, i * 512:(i + 1) * 512], start=True, stop=True)

            def bb_ev(e, i):
                return e.copy(state["bias_bc"][:, i * 512:(i + 1) * 512], ctx.PO[0:128, 0:512])
            mm_pipeline(ctx, [(bb_mm, bb_ev)] * 2, evac="S", nbuf=1, t_prewait=[ctx.tok("LD")])
            state["w_done"] = (ctx.tok("S"), ctx.tok("V"))

        rstd, negb, mean, _ = emit_ln(ctx, wait_tok=ctx.x_tok, apply=False)
        amscr = ctx.sb("amscr", [128, 48], F32, es)
        mx = amscr[:, 0:NT]
        mn = amscr[:, NT:2 * NT]
        am = amscr[:, 2 * NT:3 * NT]
        sx = ctx.sb("sx", [1, 8], F32, es)
        sxcol = ctx.sb("sxcol", [128, 2], F32, es)
        vt_ln = ctx.tok("V")

        @ctx.section("V")
        def _(v):
            ctx.wait(v, ctx.x_tok)
            ctx.wait(v, vt_ln)
            for tt in range(NT):
                ctx.inc(v.tensor_reduce(mx[:, tt:tt + 1], ctx.X(tt), AX.X, ALU.max), "V")
            for tt in range(NT):
                ctx.inc(v.tensor_reduce(mn[:, tt:tt + 1], ctx.X(tt), AX.X, ALU.min), "V")
            v.wait_ge(ctx.sem["V"], ctx.cnt["V"])
            ctx.f(v, "V", v.tensor_tensor(mx[:], mx[:], mean[:], ALU.subtract))
            ctx.f(v, "V", v.tensor_tensor(mn[:], mean[:], mn[:], ALU.subtract))
            ctx.f(v, "V", v.tensor_tensor(mx[:], mx[:], mn[:], ALU.max))
            ctx.f(v, "V", v.tensor_tensor(am[:], mx[:], rstd[:], ALU.mult))
            ctx.inc(v.tensor_reduce(am[:, 0:1], am[:], AX.X, ALU.max), "V")
            v.wait_ge(ctx.sem["V"], ctx.cnt["V"])
        amax_tok = ctx.tok("V")

        @ctx.section("G")
        def _(g):
            ctx.wait(g, amax_tok)
            ctx.inc(g.tensor_reduce(sx[0:1, 0:1], am[:, 0:1], AX.C, ALU.max), "G")
            g.wait_ge(ctx.sem["G"], ctx.cnt["G"])
            ctx.dma(g, ctx.CCIN.ap(), sx[0:1, 0:1])
            g.wait_ge(ctx.sem["LG"], ctx.cnt["LG"])
            g.collective_compute("AllReduce", ALU.max, replica_groups=[list(range(8))],
                                 ins=[ctx.CCIN.ap()], outs=[ctx.CCOUT.ap()])\
                .then_inc(ctx.sem["CC"], 1)
            ctx.cnt["CC"] += 1
            g.wait_ge(ctx.sem["CC"], ctx.cnt["CC"])
            ctx.dma(g, sx[0:1, 1:2], ctx.CCOUT.ap())
        cc_tok = ctx.tok("LG")

        @ctx.section("V")
        def _(v):
            ctx.wait(v, cc_tok)
            ctx.f(v, "V", v.tensor_scalar(sx[0:1, 2:3], sx[0:1, 1:2], 1e-12, None, ALU.max))
            ctx.f(v, "V", v.reciprocal(sx[0:1, 3:4], sx[0:1, 2:3]))
            ctx.f(v, "V", v.tensor_scalar(sx[0:1, 4:5], sx[0:1, 3:4], 448.0, 448.0, ALU.mult, ALU.min))
            ctx.f(v, "V", v.reciprocal(sx[0:1, 5:6], sx[0:1, 4:5]))
            ctx.inc(v.tensor_tensor(sx[0:1, 6:7], sx[0:1, 5:6], state["winv"][0:1, 0:1], ALU.mult), "V")
        sx_tok = ctx.tok("V")

        def sxb_mm(t, i):
            return t.matmul(ctx.P2[0:128, 0:2], ctx.onesf[0:1, :],
                            sx[0:1, 4:6].rearrange("a b -> a b") if False else sx[0:1, 4:6],
                            start=True, stop=True)

        def sxb_ev(e, i):
            return e.copy(sxcol[:], ctx.P2[0:128, 0:2])
        sxb_tok = mm_pipeline(ctx, [(sxb_mm, sxb_ev)], evac="S", t_prewait=[sx_tok])
        ctx.dump("sx", sx[:], sx_tok)
        ctx.dump("amx", amscr[:], amax_tok)
        comb = ctx.sb("comb", [128, 1], F32, es)

        def cb2_mm(t, i):
            return t.matmul(ctx.P2[0:128, 2:3], ctx.onesf[0:1, :],
                            sx[0:1, 6:7], start=True, stop=True)

        def cb2_ev(e, i):
            return e.copy(comb[:], ctx.P2[0:128, 2:3])
        comb_tok = mm_pipeline(ctx, [(cb2_mm, cb2_ev)], evac="S", t_prewait=[sx_tok])

        xqfm = ctx.XFM
        sA = ctx.sb("sA", [128, NT], F32, es)
        sB = ctx.sb("sB", [128, NT], F32, es)
        xqt = ctx.sb("xqt", [128, D], BF16, es)

        @ctx.section("V")
        def _(v):
            ctx.wait(v, sxb_tok)
            ctx.f(v, "V", v.tensor_scalar(sA[:], rstd[:], sxcol[:, 0:1], None, ALU.mult))
            ctx.inc(v.tensor_scalar(sB[:], negb[:], sxcol[:, 0:1], None, ALU.mult), "V")
        sab_tok = ctx.tok("V")

        xq_toks = []
        for tt in range(NT):
            @ctx.section("S")
            def _(s, tt=tt):
                if tt == 0:
                    ctx.wait(s, sab_tok)
                else:
                    ctx.wait(s, xq_toks[tt - 1])
                ctx.inc(s.activation(ctx.LNJ2[:, 0:D], ctx.X(tt), AF.Identity,
                                     bias=sB[:, tt:tt + 1], scale=sA[:, tt:tt + 1]), "S")
            u_tok = ctx.tok("S")

            @ctx.section("V")
            def _(v, u_tok=u_tok, tt=tt, tguard=("T", ctx.cnt["T"])):
                ctx.wait(v, u_tok)
                ctx.wait(v, tguard)
                _e4m3_round(ctx, v, ctx.LNJ2[:, 0:D], xqt[:], ctx.II2[:, 0:D],
                            ctx.II3[:, 0:D], ctx.II4[:, 0:D], ctx.LNJ[:, 0:D])
            xq_toks.append(ctx.tok("V"))

            tw = []
            for d in range(ND):
                def mk(d=d, tt=tt):
                    def mm(t, i):
                        pb = (ctx.P0 if i % 2 == 0 else ctx.P1)[:].bitcast(BF16)
                        return t.transpose(pb[:, 0:128], xqt[:, d * 128:(d + 1) * 128],
                                           ctx.identb[:])

                    def ev(e, i):
                        pb = (ctx.P0 if i % 2 == 0 else ctx.P1)[:].bitcast(BF16)
                        return ecopy(e, xqfm[:, d * T + tt * 128:d * T + tt * 128 + 128],
                                     pb[:, 0:128])
                    return mm, ev
                tw.append(mk())
            mm_pipeline(ctx, tw, evac="V", t_prewait=[xq_toks[-1]])
        xq_done = ctx.tok("V")


        ctx.dump("xqfm2", xqfm[:], xq_done)
        ctx.dump("wqT2", state["wqT"][:], xq_done)
        nrounds = NT * 2
        t0 = ctx.cnt["T"]; s0 = ctx.cnt["S"]; v0 = ctx.cnt["V"]
        guards = ctx.guard_tokens()
        ctx.cnt["T"] = t0 + nrounds
        ctx.cnt["S"] = s0 + nrounds
        ctx.cnt["V"] = v0 + 2 * nrounds
        ctx.psum_guard["S"] = ctx.cnt["S"]

        @ctx.section("T")
        def _(t, t0=t0, s0=s0):
            ctx.wait(t, xq_done)
            ctx.wait(t, comb_tok)
            ctx.wait(t, state["w_done"][0])
            for tok in guards:
                ctx.wait(t, tok)
            for i in range(nrounds):
                tt, oc = i // 2, i % 2
                pb = ctx.P0 if i % 2 == 0 else ctx.P1
                if i >= 2:
                    t.wait_ge(ctx.sem["S"], s0 + i - 1)
                for d in range(ND):
                    r = t.matmul(pb[:], xqfm[:, d * T + tt * 128:d * T + tt * 128 + 128],
                                 state["wqT"][:, d * 1024 + oc * 512:d * 1024 + oc * 512 + 512],
                                 start=(d == 0), stop=(d == ND - 1))
                r.then_inc(ctx.sem["T"], 1)

        @ctx.section("S")
        def _(s, t0=t0, s0=s0, v0=v0):
            for i in range(nrounds):
                pb = ctx.P0 if i % 2 == 0 else ctx.P1
                s.wait_ge(ctx.sem["T"], t0 + i + 1)
                if i >= 2:
                    s.wait_ge(ctx.sem["V"], v0 + 2 * (i - 1))
                s.activation(ctx.LNJ2[:, (i % 2) * 512:(i % 2) * 512 + 512],
                             pb[:], AF.Identity, scale=comb[:, 0:1])                    .then_inc(ctx.sem["S"], 1)

        @ctx.section("V")
        def _(v, s0=s0, v0=v0):
            for i in range(nrounds):
                tt, oc = i // 2, i % 2
                v.wait_ge(ctx.sem["S"], s0 + i + 1)
                sl = ctx.LNJ2[:, (i % 2) * 512:(i % 2) * 512 + 512]
                xsl = ctx.X(tt)[:, oc * 512:(oc + 1) * 512]
                v.tensor_tensor(ctx.XLNB[:, 0:512], sl,
                                state["bias_bc"][:, oc * 512:(oc + 1) * 512], ALU.add)\
                    .then_inc(ctx.sem["V"], 1)
                v.wait_ge(ctx.sem["V"], v0 + 2 * i + 1)
                v.tensor_tensor(xsl, xsl, ctx.XLNB[:, 0:512], ALU.add)\
                    .then_inc(ctx.sem["V"], 1)
        ctx.x_tok = ctx.tok("V")


F16 = mybir.dt.float16


def emit_l0_attn_b(ctx, inp):
    nc = ctx.nc
    es = contextlib.ExitStack()
    with es:
        GNWC = ctx.sb("GNWC", [128, 8], F32, es)
        HK = ctx.sb("HK", [128, 2048], F32, es)
        HV = ctx.sb("HV", [128, 2048], F32, es)
        HKB = ctx.sb("HKB", [128, 512], BF16, es)
        HVB = ctx.sb("HVB", [128, 512], BF16, es)
        TRIF = ctx.sb("TRIF", [128, 128], F16, es)
        onescol = ctx.sb("onescol", [128, 1], F16, es)
        QF = ctx.sb("QF", [128, 1024], BF16, es)
        KF = ctx.sb("KF", [128, 1024], BF16, es)
        KT = ctx.sb("KT", [128, 1024], BF16, es)
        VT = ctx.sb("VT", [128, 1024], BF16, es)
        GH = ctx.sb("GH", [128, 1024], F16, es)
        EGP = ctx.sb("EGP", [128, 1024], BF16, es)
        EGM = ctx.sb("EGM", [128, 1024], BF16, es)
        SP = ctx.sb("SP", [128, 1024], BF16, es)
        SPF = ctx.sb("SPF", [128, 1024], BF16, es)
        PPF = ctx.sb("PPF", [128, 256], BF16, es)
        ATS = ctx.sb("ATS", [128, 128], BF16, es)
        PTS = ctx.sb("PTS", [128, 128], BF16, es)
        ZF = ctx.sb("ZF", [128, 256], F32, es)
        USCR = None
        PB_ = ctx.sb("PBu", [128, 256], BF16, es)
        zcols = ctx.sb("zcols", [128, 4], F32, es)
        LAMB = ctx.sb("LAMB", [128, 1024], F32, es)
        LAMR = LAMB[0:1, :]
        gnr = ctx.LNJ[0:1, 0:2048]
        LAMT = ctx.sb("LAMT", [128, 8], F32, es)
        OSW = ctx.sb("OSW", [128, 1024], BF16, es)
        RSQR = ctx.sb("RSQR", [1, 128], F32, es)
        RSQB = ctx.sb("RSQB", [128, 128], F32, es)
        FS = ctx.LNJ[:, 0:1024]
        GB = ctx.LNJ[:, 1024:2048]
        SQB = ctx.LNJ[:, 0:1024]
        STMP = ctx.LNJ[:, 1024:2048]
        ONC = OSW
        GNWB = None
        XSP = nc.dram_tensor("xsp_gsa", [T, D], F32)

        XRB = ctx.XRES[:].bitcast(BF16)
        wqT = XRB[:, 0:8192]
        wkT = XRB[:, 8192:16384]
        wvT = XRB[:, 16384:24576]
        wfT = XRB[:, 24576:32768]
        woT = ctx.XLNB[:, 0:8192]
        wstage = ctx.XLNB[:, 8192:16384]

        sg0 = ctx.start_guard()

        @ctx.section("Y")
        def _(sy):
            for tokk in sg0:
                ctx.wait(sy, tokk)
            ctx.dma(sy, XSP.ap().rearrange("(a p) d -> p a d", p=128),
                    ctx.XRES[:].rearrange("p (a d) -> p a d", a=NT))
            ctx.dma(sy, gnr[:, 0:D], inp["L0_attn_b_norm_weight"])
            ctx.dma(sy, gnr[:, D:2 * D], inp["L0_attn_b_gnorm_weight"])
        spill_tok = ctx.tok("LD")

        @ctx.section("G")
        def _(g):
            for tokk in sg0:
                ctx.wait(g, tokk)
            g.memset(HK[:], 0.0)
            g.memset(HV[:], 0.0)
            g.memset(onescol[:], 1.0)
            gi2 = ctx.II[:, 0:128]
            g.iota(gi2, [[1, 128]], base=0, channel_multiplier=-1)
            ctx.inc(g.tensor_single_scalar(TRIF[:], gi2, 0, ALU.is_ge), "G")
        ginit_tok = ctx.tok("G")

        def nb_mm(t, i):
            return t.matmul(ctx.PO[0:128, 0:512], ctx.onesf[0:1, :],
                            gnr[0:1, i * 512:(i + 1) * 512], start=True, stop=True)

        def nb_ev(e, i):
            return e.copy(LAMB[:, i * 512:(i + 1) * 512], ctx.PO[0:128, 0:512])
        nw_tok = mm_pipeline(ctx, [(nb_mm, nb_ev)] * 2, evac="S", nbuf=1,
                             t_prewait=[spill_tok])
        gc_work = []
        for d in range(8):
            def mk(d=d):
                def mm(t, i):
                    return t.transpose(ctx.P2[0:128, 0:1],
                                       gnr[0:1, D + d * 128:D + (d + 1) * 128],
                                       ctx.identf[0:1, 0:1])

                def ev(e, i):
                    return e.copy(GNWC[:, d:d + 1], ctx.P2[0:128, 0:1])
                return mm, ev
            gc_work.append(mk())
        gnc_tok = mm_pipeline(ctx, gc_work, evac="S", nbuf=1, t_prewait=[spill_tok])

        _, _, _, lntok = emit_ln(ctx, wait_tok=nw_tok, rms=True, gamma_bcast=LAMB[:])
        xfm_tok = emit_transpose_to_xfm(ctx, lntok)

        def prep_weight(wname, dst, stage, pre):
            @ctx.section("G")
            def _(g):
                for tokk in pre:
                    ctx.wait(g, tokk)
                ctx.dma(g, stage[:, 0:8192].rearrange("p (a c) -> p a c", a=8),
                        inp[wname].rearrange("(a p) c -> p a c", p=128))
            ldt = ctx.tok("LG")
            work = []
            for po in range(8):
                for dd in range(8):
                    def mk(po=po, dd=dd):
                        def mm(t, i):
                            pb = (ctx.P0 if i % 2 == 0 else ctx.P1)[:].bitcast(BF16)
                            return t.transpose(pb[:, 0:128],
                                               stage[:, po * 1024 + dd * 128:po * 1024 + dd * 128 + 128],
                                               ctx.identb[:])

                        def ev(e, i):
                            pb = (ctx.P0 if i % 2 == 0 else ctx.P1)[:].bitcast(BF16)
                            return e.copy(dst[:, dd * 1024 + po * 128:dd * 1024 + po * 128 + 128],
                                          pb[:, 0:128])
                        return mm, ev
                    work.append(mk())
            return mm_pipeline(ctx, work, evac="S", t_prewait=[ldt])

        wq_tok = prep_weight("L0_attn_b_w_q", wqT, wstage, [spill_tok, lntok, xfm_tok])
        wk_tok = prep_weight("L0_attn_b_w_k", wkT, wstage, [wq_tok])
        wv_tok = prep_weight("L0_attn_b_w_v", wvT, wstage, [wk_tok])
        wf_tok = prep_weight("L0_attn_b_w_f", wfT, wstage, [wv_tok])
        wo_tok = prep_weight("L0_attn_b_w_o", woT, wstage, [wf_tok])

        NCH = 16
        for c in range(NCH):
            tsl = slice(c * 128, (c + 1) * 128)

            xt = ctx.LNJ[:, 1024:2048]

            fw = []
            for half in range(2):
                def mk(half=half):
                    def mm(t, i):
                        pb = ctx.P0 if i % 2 == 0 else ctx.P1
                        for dd in range(ND):
                            r = t.matmul(pb[:], ctx.XF(dd)[:, tsl],
                                         wfT[:, dd * 1024 + half * 512:dd * 1024 + half * 512 + 512],
                                         start=(dd == 0), stop=(dd == ND - 1))
                        return r

                    def ev(e, i):
                        pb = ctx.P0 if i % 2 == 0 else ctx.P1
                        return e.activation(FS[:, half * 512:half * 512 + 512], pb[:],
                                            AF.Sigmoid)
                    return mm, ev
                fw.append(mk())
            pre = [wf_tok] if c == 0 else []
            fs_tok = mm_pipeline(ctx, fw, evac="S", t_prewait=pre)

            @ctx.section("S")
            def _(s, fs_tok=fs_tok, stg=("ST", ctx.cnt["ST"])):
                ctx.wait(s, fs_tok)
                ctx.wait(s, stg)
                s.activation(GB[:, 0:512], FS[:, 0:512], AF.Ln)
                ctx.inc(s.activation(GB[:, 512:1024], FS[:, 512:1024], AF.Ln), "S")
            lng_tok = ctx.tok("S")

            @ctx.section("V")
            def _(v, lng_tok=lng_tok, tg=("T", ctx.cnt["T"])):
                ctx.wait(v, lng_tok)
                ctx.wait(v, tg)
                ctx.inc(v.tensor_scalar(GH[:], GB[:], 1.0 / 16.0, None, ALU.mult), "V")
            gh_tok = ctx.tok("V")

            @ctx.section("T")
            def _(t, gh_tok=gh_tok, g=ctx.guard_tokens()):
                ctx.wait(t, gh_tok)
                for tokk in g:
                    ctx.wait(t, tokk)
                t.matmul(ctx.PO[:, 0:512], TRIF[:], GH[:, 0:512], start=True, stop=True)\
                    .then_inc(ctx.sem["T"], 1)
                t.matmul(ctx.PO[:, 512:1024], TRIF[:], GH[:, 512:1024], start=True, stop=True)\
                    .then_inc(ctx.sem["T"], 1)
                t.matmul(ctx.P2[0:1, 0:512], onescol[:], GH[:, 0:512], start=True, stop=True)\
                    .then_inc(ctx.sem["T"], 1)
                t.matmul(ctx.P2[0:1, 0:512].bitcast(F32) if False else ctx.PO[0:1, 1024:1536],
                         onescol[:], GH[:, 512:1024], start=True, stop=True)\
                    .then_inc(ctx.sem["T"], 1)
            ctx.cnt["T"] += 4
            gl_T = ctx.tok("T")

            @ctx.section("S")
            def _(s, gl_T=gl_T):
                ctx.wait(s, gl_T)
                s.activation(EGP[:, 0:512], ctx.PO[:, 0:512], AF.Exp)
                s.activation(EGP[:, 512:1024], ctx.PO[:, 512:1024], AF.Exp)
                s.activation(EGM[:, 0:512], ctx.PO[:, 0:512], AF.Exp, scale=-1.0)
                s.activation(EGM[:, 512:1024], ctx.PO[:, 512:1024], AF.Exp, scale=-1.0)
                s.activation(LAMR[0:1, 0:512], ctx.P2[0:1, 0:512], AF.Exp)
                ctx.inc(s.activation(LAMR[0:1, 512:1024], ctx.PO[0:1, 1024:1536], AF.Exp), "S")
            eg_tok = ctx.tok("S")
            ctx.psum_guard["S"] = ctx.cnt["S"]

            lb = []
            for half in range(2):
                def mk(half=half):
                    def mm(t, i):
                        return t.matmul(ctx.PO[0:128, half * 512:half * 512 + 512],
                                        ctx.onesf[0:1, :],
                                        LAMR[0:1, half * 512:half * 512 + 512],
                                        start=True, stop=True)

                    def ev(e, i):
                        return ecopy(e, LAMB[:, half * 512:half * 512 + 512],
                                      ctx.PO[0:128, half * 512:half * 512 + 512])
                    return mm, ev
                lb.append(mk())
            for dd in range(8):
                def mk(dd=dd):
                    def mm(t, i):
                        return t.transpose(ctx.P2[0:128, 0:1],
                                           LAMR[0:1, dd * 128:(dd + 1) * 128],
                                           ctx.identf[0:1, 0:1])

                    def ev(e, i):
                        return ecopy(e, LAMT[:, dd:dd + 1], ctx.P2[0:128, 0:1])
                    return mm, ev
                lb.append(mk())
            lam_tok = mm_pipeline(ctx, lb, evac="V", nbuf=1, t_prewait=[eg_tok])

            for wi, (wT, dstb) in enumerate([(wqT, QF), (wkT, KF)]):
                qw = []
                for dqt in range(8):
                    def mk(dqt=dqt, wT=wT, dstb=dstb):
                        def mm(t, i):
                            pb = ctx.P0 if i % 2 == 0 else ctx.P1
                            for dd in range(ND):
                                r = t.matmul(pb[:, 0:128], wT[:, dd * 1024 + dqt * 128:dd * 1024 + dqt * 128 + 128],
                                             ctx.XF(dd)[:, tsl],
                                             start=(dd == 0), stop=(dd == ND - 1))
                            return r

                        def ev(e, i):
                            pb = ctx.P0 if i % 2 == 0 else ctx.P1
                            return e.activation(dstb[:, dqt * 128:(dqt + 1) * 128],
                                                pb[:, 0:128], AF.Silu)
                        return mm, ev
                    qw.append(mk())
                pre = []
                if c == 0:
                    pre = [wq_tok if wi == 0 else wk_tok]
                mm_pipeline(ctx, qw, evac="S", t_prewait=pre)
            vw = []
            for half in range(2):
                def mk(half=half):
                    def mm(t, i):
                        pb = ctx.P0 if i % 2 == 0 else ctx.P1
                        for dd in range(ND):
                            r = t.matmul(pb[:], ctx.XF(dd)[:, tsl],
                                         wvT[:, dd * 1024 + half * 512:dd * 1024 + half * 512 + 512],
                                         start=(dd == 0), stop=(dd == ND - 1))
                        return r

                    def ev(e, i):
                        pb = ctx.P0 if i % 2 == 0 else ctx.P1
                        return e.activation(VT[:, half * 512:half * 512 + 512], pb[:], AF.Silu)
                    return mm, ev
                vw.append(mk())
            vt_tok = mm_pipeline(ctx, vw, evac="S",
                                 t_prewait=[wv_tok] if c == 0 else [])
            qkv_S = ctx.tok("S")

            @ctx.section("S")
            def _(s, gh_tok=gh_tok):
                ctx.wait(s, gh_tok)
                ctx.inc(s.activation(FS[:], GH[:], AF.Exp), "S")
            eg2 = ctx.tok("S")

            @ctx.section("V")
            def _(v, eg2=eg2, eg_tok=eg_tok):
                ctx.wait(v, eg2)
                ctx.wait(v, eg_tok)
                ctx.f(v, "V", v.tensor_scalar(STMP[:], FS[:], -1.0, 1.0, ALU.mult, ALU.add))
                ctx.inc(v.tensor_tensor(SP[:], STMP[:], EGM[:], ALU.mult), "V")
            sp_tok = ctx.tok("V")

            tw = []
            for dd in range(8):
                def mk(dd=dd):
                    def mm(t, i):
                        pb = (ctx.P0 if i % 2 == 0 else ctx.P1)[:].bitcast(BF16)
                        return t.transpose(pb[:, 0:128], SP[:, dd * 128:(dd + 1) * 128],
                                           ctx.identb[:])

                    def ev(e, i):
                        pb = (ctx.P0 if i % 2 == 0 else ctx.P1)[:].bitcast(BF16)
                        return ecopy(e, SPF[:, dd * 128:(dd + 1) * 128], pb[:, 0:128])
                    return mm, ev
                tw.append(mk())
            for dd in range(8):
                def mk(dd=dd):
                    def mm(t, i):
                        pb = (ctx.P0 if i % 2 == 0 else ctx.P1)[:].bitcast(BF16)
                        return t.transpose(pb[:, 0:128], KF[:, dd * 128:(dd + 1) * 128],
                                           ctx.identb[:])

                    def ev(e, i):
                        pb = (ctx.P0 if i % 2 == 0 else ctx.P1)[:].bitcast(BF16)
                        return ecopy(e, KT[:, dd * 128:(dd + 1) * 128], pb[:, 0:128])
                    return mm, ev
                tw.append(mk())
            tr_tok = mm_pipeline(ctx, tw, evac="V", t_prewait=[sp_tok, qkv_S])

            @ctx.section("Y")
            def _(sy, c=c, xt=xt, sp_tok=sp_tok):
                ctx.wait(sy, spill_tok)
                ctx.wait(sy, sp_tok)
                ctx.dma(sy, xt, XSP.ap()[c * 128:(c + 1) * 128, :])
            xld = ctx.tok("LD")

            pass

            for h in range(4):
                hs = slice(h * 256, (h + 1) * 256)
                @ctx.section("G")
                def _(g, h=h, pv=ctx.tok("V")):
                    ctx.wait(g, pv)
                    g.tensor_copy(HKB[:, 0:256], HK[:, h * 512:h * 512 + 256])
                    g.tensor_copy(HKB[:, 256:512], HK[:, h * 512 + 256:h * 512 + 512])
                    g.tensor_copy(HVB[:, 0:256], HV[:, h * 512:h * 512 + 256])
                    ctx.inc(g.tensor_copy(HVB[:, 256:512], HV[:, h * 512 + 256:h * 512 + 512]), "G")
                hb_tok = ctx.tok("G")

                @ctx.section("T")
                def _(t, h=h, tr_tok=tr_tok, g=ctx.guard_tokens()):
                    ctx.wait(t, tr_tok)
                    for tokk in g:
                        ctx.wait(t, tokk)
                    t.matmul(ctx.P2[:, 0:128], KF[:, h * 256:h * 256 + 128],
                             QF[:, h * 256:h * 256 + 128], start=True, stop=False)
                    ctx.inc(t.matmul(ctx.P2[:, 0:128], KF[:, h * 256 + 128:h * 256 + 256],
                                     QF[:, h * 256 + 128:h * 256 + 256],
                                     start=False, stop=True), "T")
                at_T = ctx.tok("T")

                @ctx.section("V")
                def _(v, at_T=at_T):
                    ctx.wait(v, at_T)
                    ctx.inc(v.tensor_tensor(ATS[:], ctx.P2[:, 0:128], ctx.tri01[:],
                                            ALU.mult), "V")
                at_V = ctx.tok("V")
                if c == 0 and h == 0:
                    ctx.dump("h0at", ATS[:], at_V, hold="T")

                @ctx.section("T")
                def _(t, h=h, at_V=at_V, hb_tok=hb_tok):
                    ctx.wait(t, at_V)
                    ctx.wait(t, hb_tok)
                    t.matmul(ctx.PO[:, 1024:1280], QF[:, h * 256:h * 256 + 128],
                             HKB[:, 0:256], start=True, stop=False)
                    t.matmul(ctx.PO[:, 1024:1280], QF[:, h * 256 + 128:h * 256 + 256],
                             HKB[:, 256:512], start=False, stop=False)
                    ctx.inc(t.matmul(ctx.PO[:, 1024:1280], ATS[:], SP[:, hs],
                                     start=False, stop=True), "T")
                z_T = ctx.tok("T")

                @ctx.section("V")
                def _(v, h=h, z_T=z_T):
                    ctx.wait(v, z_T)
                    ctx.f(v, "V", v.tensor_tensor(ZF[:], ctx.PO[:, 1024:1280], EGP[:, hs], ALU.mult))
                    ctx.f(v, "V", v.tensor_reduce(zcols[:, 0:1], ZF[:], AX.X, ALU.max))
                    ctx.inc(v.tensor_scalar(zcols[:, 1:2], zcols[:, 0:1], -1.0, None, ALU.mult), "V")
                zm_tok = ctx.tok("V")
                if c == 0 and h == 0:
                    ctx.dump("h0z", ZF[:], (zm_tok[0], zm_tok[1] - 2), hold="S")

                @ctx.section("S")
                def _(s, zm_tok=zm_tok):
                    ctx.wait(s, zm_tok)
                    ctx.inc(s.activation(ZF[:], ZF[:], AF.Exp, bias=zcols[:, 1:2]), "S")
                ze_tok = ctx.tok("S")

                @ctx.section("V")
                def _(v, h=h, ze_tok=ze_tok):
                    ctx.wait(v, ze_tok)
                    ctx.f(v, "V", v.tensor_reduce(zcols[:, 2:3], ZF[:], AX.X, ALU.add))
                    ctx.f(v, "V", v.reciprocal(zcols[:, 3:4], zcols[:, 2:3]))
                    ctx.f(v, "V", v.tensor_scalar(PB_[:], ZF[:], zcols[:, 3:4], None, ALU.mult))
                    ctx.inc(v.tensor_tensor(PB_[:], PB_[:], EGP[:, hs], ALU.mult), "V")
                pp_tok = ctx.tok("V")
                if c == 0 and h == 0:
                    ctx.dump("h0p", PB_[:], pp_tok, hold="T")

                ppw = []
                for m2 in range(2):
                    def mk(m2=m2):
                        def mm(t, i):
                            pb = (ctx.P0 if i % 2 == 0 else ctx.P1)[:].bitcast(BF16)
                            return t.transpose(pb[:, 0:128], PB_[:, m2 * 128:(m2 + 1) * 128],
                                               ctx.identb[:])

                        def ev(e, i):
                            pb = (ctx.P0 if i % 2 == 0 else ctx.P1)[:].bitcast(BF16)
                            return ecopy(e, PPF[:, m2 * 128:(m2 + 1) * 128], pb[:, 0:128])
                        return mm, ev
                    ppw.append(mk())
                ppf_tok = mm_pipeline(ctx, ppw, evac="V", t_prewait=[pp_tok])

                @ctx.section("T")
                def _(t, h=h, ppf_tok=ppf_tok, g=ctx.guard_tokens()):
                    ctx.wait(t, ppf_tok)
                    for tokk in g:
                        ctx.wait(t, tokk)
                    t.matmul(ctx.P2[:, 0:128], SPF[:, h * 256:h * 256 + 128],
                             PPF[:, 0:128], start=True, stop=False)
                    ctx.inc(t.matmul(ctx.P2[:, 0:128], SPF[:, h * 256 + 128:h * 256 + 256],
                                     PPF[:, 128:256], start=False, stop=True), "T")
                pt_T = ctx.tok("T")

                @ctx.section("V")
                def _(v, pt_T=pt_T):
                    ctx.wait(v, pt_T)
                    ctx.inc(v.tensor_tensor(PTS[:], ctx.P2[:, 0:128], ctx.tri01[:],
                                            ALU.mult), "V")
                pt_V = ctx.tok("V")
                if c == 0 and h == 0:
                    ctx.dump("h0pt", PTS[:], pt_V, hold="T")

                @ctx.section("T")
                def _(t, h=h, pt_V=pt_V, vt_tok=vt_tok):
                    ctx.wait(t, pt_V)
                    for vt2 in range(2):
                        po = ctx.PO[:, 1536 + vt2 * 128:1536 + (vt2 + 1) * 128]
                        t.matmul(po, HVB[:, vt2 * 128:vt2 * 128 + 128], PPF[:, 0:128],
                                 start=True, stop=False)
                        t.matmul(po, HVB[:, 256 + vt2 * 128:256 + vt2 * 128 + 128],
                                 PPF[:, 128:256], start=False, stop=False)
                        r = t.matmul(po, VT[:, h * 256 + vt2 * 128:h * 256 + vt2 * 128 + 128],
                                     PTS[:], start=False, stop=True)
                    ctx.inc(r, "T")
                o_T = ctx.tok("T")

                @ctx.section("S")
                def _(s, h=h, o_T=o_T):
                    ctx.wait(s, o_T)
                    s.activation(OSW[:, h * 256:h * 256 + 128],
                                 ctx.PO[:, 1536:1664], AF.Silu)
                    ctx.inc(s.activation(OSW[:, h * 256 + 128:h * 256 + 256],
                                         ctx.PO[:, 1664:1792], AF.Silu), "S")
                osw_tok = ctx.tok("S")
                ctx.psum_guard["S"] = ctx.cnt["S"]

                @ctx.section("T")
                def _(t, h=h, osw_tok=osw_tok, tr_tok=tr_tok, hb_tok=hb_tok,
                      g=ctx.guard_tokens()):
                    for tokk in g:
                        ctx.wait(t, tokk)
                    for k2 in range(2):
                        pb = ctx.P0 if k2 == 0 else ctx.P1
                        ctx.inc(t.matmul(pb[:, 0:256],
                                         KT[:, h * 256 + k2 * 128:h * 256 + k2 * 128 + 128],
                                         SP[:, hs], start=True, stop=True), "T")
                    for m2 in range(2):
                        pb = ctx.P2 if m2 == 0 else ctx.PO[:, 1024:1280]
                        ap = pb[:, 0:256] if m2 == 0 else pb
                        ctx.inc(t.matmul(ap, SP[:, h * 256 + m2 * 128:h * 256 + m2 * 128 + 128],
                                         VT[:, hs], start=True, stop=True), "T")
                st_T = ctx.tok("T")

                @ctx.section("V")
                def _(v, h=h, st_T=st_T, lam_tok=lam_tok):
                    ctx.wait(v, st_T)
                    ctx.wait(v, lam_tok)
                    for k2, pb in ((0, ctx.P0), (1, ctx.P1)):
                        sl = HK[:, h * 512 + k2 * 256:h * 512 + k2 * 256 + 256]
                        ctx.f(v, "V", v.tensor_tensor(ZF[:], sl, pb[:, 0:256], ALU.add))
                        ctx.inc(v.tensor_tensor(sl, ZF[:], LAMB[:, hs], ALU.mult), "V")
                        v.wait_ge(ctx.sem["V"], ctx.cnt["V"])
                    for m2 in range(2):
                        src = ctx.P2[:, 0:256] if m2 == 0 else ctx.PO[:, 1024:1280]
                        sl = HV[:, h * 512 + m2 * 256:h * 512 + m2 * 256 + 256]
                        ctx.f(v, "V", v.tensor_tensor(ZF[:], sl, src, ALU.add))
                        ctx.inc(v.tensor_scalar(sl, ZF[:], LAMT[:, 2 * h + m2:2 * h + m2 + 1],
                                                None, ALU.mult), "V")
                        v.wait_ge(ctx.sem["V"], ctx.cnt["V"])
                ctx.psum_guard["V"] = ctx.cnt["V"]

            sq_base = ctx.cnt["V"]

            @ctx.section("V")
            def _(v, osw_all=ctx.tok("S")):
                ctx.wait(v, osw_all)
                for dd in range(8):
                    v.tensor_tensor(SQB[:, dd * 128:(dd + 1) * 128],
                                    OSW[:, dd * 128:(dd + 1) * 128],
                                    OSW[:, dd * 128:(dd + 1) * 128], ALU.mult)\
                        .then_inc(ctx.sem["V"], 1)
            ctx.cnt["V"] += 8
            sq_tok = ctx.tok("V")

            @ctx.section("T")
            def _(t, sq_tok=sq_tok, g=ctx.guard_tokens()):
                ctx.wait(t, sq_tok)
                for tokk in g:
                    ctx.wait(t, tokk)
                for dd in range(8):
                    r = t.matmul(ctx.P2[0:1, 0:128], ctx.ccol(1.0),
                                 SQB[:, dd * 128:(dd + 1) * 128],
                                 start=(dd == 0), stop=(dd == 7))
                ctx.inc(r, "T")
            msq_T = ctx.tok("T")

            @ctx.section("S")
            def _(s, msq_T=msq_T):
                ctx.wait(s, msq_T)
                ctx.f(s, "S", s.activation(RSQR[:], ctx.P2[0:1, 0:128], AF.Ln,
                                           scale=1.0 / D, bias=ctx.ccol(1e-5)[0:1, :]))
                ctx.inc(s.activation(RSQR[:], RSQR[:], AF.Exp, scale=-0.5), "S")
            rsq_tok = ctx.tok("S")

            def rb_mm(t, i):
                return t.matmul(ctx.P2[0:128, 0:128], ctx.onesf[0:1, :], RSQR[0:1, :],
                                start=True, stop=True)

            def rb_ev(e, i):
                return ecopy(e, RSQB[:], ctx.P2[0:128, 0:128])
            rsb_tok = mm_pipeline(ctx, [(rb_mm, rb_ev)], evac="V", t_prewait=[rsq_tok])

            @ctx.section("V")
            def _(v, rsb_tok=rsb_tok):
                ctx.wait(v, rsb_tok)
                for dd in range(8):
                    v.scalar_tensor_tensor(ONC[:, dd * 128:(dd + 1) * 128],
                                           OSW[:, dd * 128:(dd + 1) * 128],
                                           GNWC[:, dd:dd + 1], RSQB[:],
                                           ALU.mult, ALU.mult).then_inc(ctx.sem["V"], 1)
            ctx.cnt["V"] += 8
            onc_tok = ctx.tok("V")

            yw = []
            for oc in range(2):
                def mk(oc=oc):
                    def mm(t, i):
                        po = ctx.PO[:, oc * 512:(oc + 1) * 512]
                        for dd in range(ND):
                            r = t.matmul(po, ONC[:, dd * 128:(dd + 1) * 128],
                                         woT[:, dd * 1024 + oc * 512:dd * 1024 + oc * 512 + 512],
                                         start=(dd == 0), stop=(dd == ND - 1))
                        return r

                    def ev(e, i):
                        po = ctx.PO[:, oc * 512:(oc + 1) * 512]
                        xsl = xt[:, oc * 512:(oc + 1) * 512]
                        return e.tensor_tensor(xsl, xsl, po, ALU.add)
                    return mm, ev
                yw.append(mk())
            pre_y = [onc_tok, gnc_tok, xld] + ([wo_tok] if c == 0 else [])
            y_tok = mm_pipeline(ctx, yw, evac="V", nbuf=2, t_prewait=pre_y)

            @ctx.section("Y")
            def _(sy, c=c, xt=xt, y_tok=y_tok):
                ctx.wait(sy, y_tok)
                sy.dma_start(XSP.ap()[c * 128:(c + 1) * 128, :], xt)\
                    .then_inc(ctx.sem["ST"], 16)
                ctx.cnt["ST"] += 16
            if c == 0:
                vtk = ctx.tok("V")
                for nm, ap in [("gh", GH[:]), ("egp", EGP[:]), ("egm", EGM[:]),
                               ("sp", SP[:]), ("qf", QF[:]), ("kf", KF[:]),
                               ("vt", VT[:]), ("osw", OSW[:]), ("hk", HK[:]),
                               ("hv", HV[:]), ("lamb", LAMB[:])]:
                    ctx.dump("g0" + nm, ap, vtk, hold="T")

        @ctx.section("Y")
        def _(sy):
            sy.wait_ge(ctx.sem["ST"], ctx.cnt["ST"])
            ctx.dma(sy, ctx.XRES[:].rearrange("p (a d) -> p a d", a=NT),
                    XSP.ap().rearrange("(a p) d -> p a d", p=128))
        ctx.x_tok = ctx.tok("LD")


def emit_l1_attn_a(ctx, inp):
    nc = ctx.nc
    ctx.barrier()
    es = contextlib.ExitStack()
    with es:
        WINt = ctx.sb("WINt", [128, 2 * 1024], BF16, es)
        TPACK = ctx.sb("TPACK", [64, 2048], F32, es)
        PPACK = ctx.sb("PPACK", [64, 2048], F32, es)
        DENP = ctx.sb("DENP", [64, 2048], F32, es)
        PWT = ctx.sb("PWT", [128, 512], F32, es)
        MQR = ctx.sb("MQR", [128, 512], BF16, es)
        MQI = ctx.sb("MQI", [128, 512], BF16, es)
        CARRE = ctx.sb("CARRE", [128, 66], F32, es)
        CARIM = ctx.sb("CARIM", [128, 66], F32, es)
        CARDN = ctx.sb("CARDN", [64, 1], F32, es)
        XTS = ctx.sb("XTS", [128, 512], F32, es)
        XTSB = XTS[:].bitcast(BF16)
        REN4 = XTSB[:, 0:512]
        IMN4 = XTSB[:, 512:1024]
        OHS = ctx.sb("OHS", [128, 1024], BF16, es)
        WREb = ctx.sb("WREb", [128, 32], BF16, es)
        WIMb = ctx.sb("WIMb", [128, 32], BF16, es)
        NSC = ctx.sb("NSC", [128, 2], F32, es)
        WCC = ctx.sb("WCC", [128, 68], F32, es)
        BCC = ctx.sb("BCC", [128, 17], F32, es)
        WLC = ctx.sb("WLC", [128, 3], F32, es)
        rowsA = ctx.sb("rowsA", [1, 416], F32, es)
        SSC = ctx.sb("SSC", [64, 1], F32, es)
        XSP2 = nc.dram_tensor("xsp_l1a", [T, D], F32)
        XVSP = nc.dram_tensor("xvsp_l1a", [T, D], F32)
        XRB2 = ctx.XRES[:].bitcast(BF16)
        THETB_F = ctx.XRES[:, 6144:10240]
        NUMR = XRB2[:, 4096:8192]
        NUMI = XRB2[:, 8192:12288]
        WOUT = XRB2[:, 12288:20480]
        Q1 = ctx.XRES[:, 10240:11264]
        POLY = ctx.XRES[:, 11264:12288]
        XV2 = ctx.XRES[:, 12288:13312]
        ZRAW = ctx.XRES[:, 13312:15360]
        CONV = ctx.LNJ[:, 0:2048]
        CONVB = XRB2[:, 30720:32768]
        xt = ctx.XRES[:, 15360:16384]

        sg0 = ctx.start_guard()

        @ctx.section("Y")
        def _(sy):
            for tokk in sg0:
                ctx.wait(sy, tokk)
            ctx.dma(sy, XSP2.ap().rearrange("(a p) d -> p a d", p=128),
                    ctx.XRES[:].rearrange("p (a d) -> p a d", a=NT))
            ctx.dma(sy, rowsA[:, 0:28], inp["L1_attn_a_decay_slopes"])
            ctx.dma(sy, rowsA[:, 28:32], inp["L1_attn_a_anchor_slopes"])
            ctx.dma(sy, rowsA[:, 32:64], inp["L1_attn_a_score_scale"])
            ctx.dma(sy, rowsA[:, 64:67], inp["L1_attn_a_deriv_logits"])
        rows_tok = ctx.tok("LD")
        spill_tok = rows_tok

        @ctx.section("Y")
        def _(sy):
            ctx.dma(sy, rowsA[:, 160:416], inp["L1_attn_a_norm_scale"])
        ns_tok = ctx.tok("LD")


        @ctx.section("G")
        def _(g):
            for tokk in sg0:
                ctx.wait(g, tokk)
            pass

        _, _, _, lntok = emit_ln(ctx, wait_tok=ctx.x_tok)
        xfm_tok = emit_transpose_to_xfm(ctx, lntok)

        WCR = ctx.XRES[0:4, 2048:4128]
        BCR = ctx.XRES[0:1, 4128:6208]

        @ctx.section("Y")
        def _(sy):
            ctx.wait(sy, spill_tok)
            ctx.wait(sy, lntok)
            ctx.dma(sy, WCR, inp["L1_attn_a_w_conv"])
            ctx.dma(sy, BCR, inp["L1_attn_a_b_conv"])
        wc_tok = ctx.tok("LD")

        ccw = []
        for ct in range(17):
            def mk(ct=ct):
                w = 128 if ct < 16 else 32

                def mm(t, i):
                    return t.transpose(ctx.P2[0:w, 0:4], WCR[:, ct * 128:ct * 128 + w],
                                       ctx.identf[0:4, 0:4])

                def ev(e, i):
                    return e.copy(WCC[0:w, ct * 4:ct * 4 + 4], ctx.P2[0:w, 0:4])
                return mm, ev
            ccw.append(mk())
        for ct in range(17):
            def mk(ct=ct):
                w = 128 if ct < 16 else 32

                def mm(t, i):
                    return t.transpose(ctx.P2[0:w, 0:1], BCR[:, ct * 128:ct * 128 + w],
                                       ctx.identf[0:1, 0:1])

                def ev(e, i):
                    return e.copy(BCC[0:w, ct:ct + 1], ctx.P2[0:w, 0:1])
                return mm, ev
            ccw.append(mk())
        def ss_mm1(t, i):
            return t.transpose(ctx.P2[0:28, 0:1], rowsA[0:1, 32:60], ctx.identf[0:1, 0:1])

        def ss_ev1(e, i):
            return e.copy(SSC[0:28, 0:1], ctx.P2[0:28, 0:1])

        def ss_mm2(t, i):
            return t.transpose(ctx.P2[0:4, 0:1], rowsA[0:1, 60:64], ctx.identf[0:1, 0:1])

        def ss_ev2(e, i):
            return e.copy(SSC[32:36, 0:1], ctx.P2[0:4, 0:1])

        def ns_mm1(t, i):
            return t.transpose(ctx.P2[0:128, 0:1], rowsA[0:1, 160:288], ctx.identf[0:1, 0:1])

        def ns_ev1(e, i):
            return e.copy(NSC[:, 0:1], ctx.P2[0:128, 0:1])

        def ns_mm2(t, i):
            return t.transpose(ctx.P2[0:128, 0:1], rowsA[0:1, 288:416], ctx.identf[0:1, 0:1])

        def ns_ev2(e, i):
            return e.copy(NSC[:, 1:2], ctx.P2[0:128, 0:1])
        ccw += [(ss_mm1, ss_ev1), (ss_mm2, ss_ev2), (ns_mm1, ns_ev1), (ns_mm2, ns_ev2)]
        cols_tok = mm_pipeline(ctx, ccw, evac="S", nbuf=1,
                               t_prewait=[wc_tok, rows_tok, ns_tok])
        ctx.dump("nsc0", NSC[:], cols_tok, hold=["S", "T", "V", "G"])
        ctx.dump("lnjns", ctx.LNJ[0:1, 0:256], cols_tok, hold=["S", "T", "V", "G"])

        @ctx.section("V")
        def _(v):
            ctx.wait(v, rows_tok)
            r = rowsA[0:1, 64:67]
            ctx.f(v, "V", v.tensor_reduce(rowsA[0:1, 70:71], r, AX.X, ALU.max))
            ctx.f(v, "V", v.tensor_scalar(rowsA[0:1, 71:72], rowsA[0:1, 70:71], -1.0, None, ALU.mult))
            ctx.inc(v.tensor_scalar(rowsA[0:1, 72:75], r, rowsA[0:1, 71:72], None, ALU.add), "V")
        wl1 = ctx.tok("V")

        @ctx.section("S")
        def _(s_):
            ctx.wait(s_, wl1)
            ctx.inc(s_.activation(rowsA[0:1, 72:75], rowsA[0:1, 72:75], AF.Exp), "S")
        wl2 = ctx.tok("S")

        @ctx.section("V")
        def _(v):
            ctx.wait(v, wl2)
            ctx.f(v, "V", v.tensor_reduce(rowsA[0:1, 75:76], rowsA[0:1, 72:75], AX.X, ALU.add))
            ctx.f(v, "V", v.reciprocal(rowsA[0:1, 76:77], rowsA[0:1, 75:76]))
            ctx.inc(v.tensor_scalar(rowsA[0:1, 72:75], rowsA[0:1, 72:75],
                                    rowsA[0:1, 76:77], None, ALU.mult), "V")
        wl3 = ctx.tok("V")

        def wl_mm(t, i):
            return t.matmul(ctx.P2[0:128, 0:3], ctx.onesf[0:1, :], rowsA[0:1, 72:75],
                            start=True, stop=True)

        def wl_ev(e, i):
            return e.copy(WLC[:], ctx.P2[0:128, 0:3])
        wl_tok = mm_pipeline(ctx, [(wl_mm, wl_ev)], evac="S", t_prewait=[wl3])

        @ctx.section("S")
        def _(s_):
            ctx.wait(s_, rows_tok)
            ctx.f(s_, "S", s_.activation(rowsA[0:1, 80:112], rowsA[0:1, 0:32], AF.Abs))
            ctx.f(s_, "S", s_.activation(rowsA[0:1, 80:112], rowsA[0:1, 80:112], AF.Exp, scale=-1.0))
            ctx.inc(s_.activation(rowsA[0:1, 80:112], rowsA[0:1, 80:112], AF.Ln,
                                  bias=ctx.ccol(1.0)[0:1, :]), "S")
        sp2 = ctx.tok("S")

        @ctx.section("V")
        def _(v):
            ctx.wait(v, sp2)
            ctx.f(v, "V", v.tensor_scalar(rowsA[0:1, 112:144], rowsA[0:1, 0:32], 0.0, None, ALU.max))
            ctx.inc(v.tensor_tensor(rowsA[0:1, 0:32], rowsA[0:1, 112:144],
                                    rowsA[0:1, 80:112], ALU.add), "V")
        sp_tok2 = ctx.tok("V")

        DROW = ctx.LNJ[0:1, 0:2048]
        IROWv = ctx.XRES[0:1, 10240:12288].bitcast(I32)

        for phase in range(2):
            @ctx.section("G")
            def _(g, phase=phase, pv=(ctx.tok("S"), ctx.tok("V"))):
                ctx.wait(g, pv[0])
                ctx.wait(g, pv[1])
                if phase == 0:
                    g.iota(IROWv, [[-1, 2048]], base=2047, channel_multiplier=0)
                else:
                    g.iota(IROWv, [[1, 2048]], base=0, channel_multiplier=0)
                g.wait_ge(ctx.sem["G"], ctx.cnt["G"]) if False else None
                ctx.inc(g.tensor_copy(DROW, IROWv), "G")
            io_tok = ctx.tok("G")

            tb0 = ctx.cnt["T"]
            sb0 = ctx.cnt["S"]
            twg = ctx.guard_tokens()
            ctx.cnt["T"] += 4
            ctx.cnt["S"] += 4
            ctx.psum_guard["S"] = ctx.cnt["S"]

            @ctx.section("T")
            def _(t, phase=phase, io_tok=io_tok, tb0=tb0, twg=twg):
                ctx.wait(t, io_tok)
                ctx.wait(t, sp_tok2)
                for tokk in twg:
                    ctx.wait(t, tokk)
                rows = rowsA[0:1, 0:28] if phase == 0 else rowsA[0:1, 28:32]
                for half in range(4):
                    nr = 28 if phase == 0 else 4
                    t.matmul(ctx.PO[0:nr, half * 512:(half + 1) * 512], rows,
                             DROW[0:1, half * 512:(half + 1) * 512], start=True, stop=True)\
                        .then_inc(ctx.sem["T"], 1)

            @ctx.section("S")
            def _(s_, phase=phase, tb0=tb0, sb0=sb0):
                nr = 28 if phase == 0 else 4
                r0 = 0 if phase == 0 else 32
                for half in range(4):
                    s_.wait_ge(ctx.sem["T"], tb0 + half + 1)
                    s_.activation(TPACK[r0:r0 + nr, half * 512:(half + 1) * 512],
                                  ctx.PO[0:nr, half * 512:(half + 1) * 512], AF.Exp,
                                  scale=-1.0).then_inc(ctx.sem["S"], 1)
        tw_tok = ctx.tok("S")

        def emit_ctile(ct, consumer):
            w = 128 if ct < 16 else 32

            @ctx.section("G")
            def _(g, ct=ct, tg=("T", ctx.cnt["T"])):
                ctx.wait(g, tg)
                ctx.dma(g, WINt[:, (ct % 2) * 1024:(ct % 2) * 1024 + 8 * w]
                        .rearrange("p (a c) -> p a c", a=8),
                        inp["L1_attn_a_W_in"][:, ct * 128:ct * 128 + w]
                        .rearrange("(a p) c -> p a c", p=128))
            wld = ctx.tok("LG")

            zw = []
            for tch in range(4):
                def mk(tch=tch, ct=ct, w=w):
                    def mm(t, i):
                        pb = ctx.P0 if i % 2 == 0 else ctx.P1
                        for dd in range(ND):
                            r = t.matmul(pb[:, 0:w] if False else pb[0:w, 0:512],
                                         WINt[:, (ct % 2) * 1024 + dd * w:(ct % 2) * 1024 + (dd + 1) * w],
                                         ctx.XF(dd)[:, tch * 512:(tch + 1) * 512],
                                         start=(dd == 0), stop=(dd == ND - 1))
                        return r

                    def ev(e, i):
                        pb = ctx.P0 if i % 2 == 0 else ctx.P1
                        return e.copy(ctx.LNJ[0:w, tch * 512:(tch + 1) * 512] if False
                                      else ZRAW[0:w, tch * 512:(tch + 1) * 512],
                                      pb[0:w, 0:512])
                    return mm, ev
                zw.append(mk())
            z_tok = mm_pipeline(ctx, zw, evac="S", t_prewait=[wld, xfm_tok, cols_tok, spill_tok])

            @ctx.section("V")
            def _(v, ct=ct, w=w, z_tok=z_tok):
                ctx.wait(v, z_tok)
                wc = lambda k: WCC[0:w, ct * 4 + k:ct * 4 + k + 1]
                bc = BCC[0:w, ct:ct + 1]
                ctx.f(v, "V", v.tensor_scalar(CONV[0:w, :], ZRAW[0:w, :], wc(3), bc, ALU.mult, ALU.add))
                ctx.f(v, "V", v.scalar_tensor_tensor(CONV[0:w, 1:2048], ZRAW[0:w, 0:2047],
                                                     wc(2), CONV[0:w, 1:2048], ALU.mult, ALU.add))
                ctx.f(v, "V", v.scalar_tensor_tensor(CONV[0:w, 2:2048], ZRAW[0:w, 0:2046],
                                                     wc(1), CONV[0:w, 2:2048], ALU.mult, ALU.add))
                ctx.inc(v.scalar_tensor_tensor(CONV[0:w, 3:2048], ZRAW[0:w, 0:2045],
                                               wc(0), CONV[0:w, 3:2048], ALU.mult, ALU.add), "V")
            conv_tok = ctx.tok("V")
            consumer(ct, conv_tok)


        xv_st = []

        def xval_consumer(ct, conv_tok):
            ld0 = ctx.cnt["LD"]
            s0 = ctx.cnt["S"]
            tw2 = []
            for tt in range(NT):
                def mk(tt=tt, ld0=ld0):
                    def mm(t, i):
                        pb = ctx.P0 if i % 2 == 0 else ctx.P1
                        return t.transpose(pb[0:128, 0:128], CONV[:, tt * 128:(tt + 1) * 128],
                                           ctx.identf[:])

                    def ev(e, i):
                        g = i // 2
                        if i % 2 == 0 and g >= 2:
                            e.wait_ge(ctx.sem["LD"], ld0 + 16 * (g - 1))
                        pb = ctx.P0 if i % 2 == 0 else ctx.P1
                        o = (g % 2) * 256 + (i % 2) * 128
                        return e.copy(XTS[:, o:o + 128], pb[0:128, 0:128])
                    return mm, ev
                tw2.append(mk())
            prew = [conv_tok] + ([xv_st[-1]] if xv_st else [])
            mm_pipeline(ctx, tw2, evac="S", t_prewait=prew)

            for g in range(8):
                @ctx.section("Y")
                def _(sy, ct=ct, g=g, s0=s0):
                    sy.wait_ge(ctx.sem["S"], s0 + 2 * (g + 1))
                    ctx.dma(sy, XVSP.ap()[g * 256:(g + 1) * 256, ct * 128:(ct + 1) * 128]
                            .rearrange("(a p) f -> p a f", p=128),
                            XTS[:, (g % 2) * 256:(g % 2) * 256 + 256]
                            .rearrange("p (a f) -> p a f", f=128))
            xv_st.append(ctx.tok("LD"))

        def sraw_consumer(ct, conv_tok):
            @ctx.section("G")
            def _(g, conv_tok=conv_tok):
                ctx.wait(g, conv_tok)
                ctx.dma(g, PPACK[32:36, :], CONV[28:32, :])
            sra_tok = ctx.tok("LG")

            @ctx.section("V")
            def _(v, conv_tok=conv_tok, sra_tok=sra_tok):
                ctx.wait(v, conv_tok)
                ctx.wait(v, sra_tok)
                ctx.f(v, "V", v.tensor_copy(PPACK[0:28, :], CONV[0:28, :]))
                ctx.f(v, "V", v.tensor_scalar(PPACK[0:28, :], PPACK[0:28, :],
                                              SSC[0:28, 0:1], None, ALU.mult))
                ctx.f(v, "V", v.tensor_scalar(PPACK[32:36, :], PPACK[32:36, :],
                                              SSC[32:36, 0:1], None, ALU.mult))
                ctx.f(v, "V", v.tensor_scalar(PPACK[0:28, :], PPACK[0:28, :], 20.0, -20.0,
                                              ALU.min, ALU.max))
                ctx.inc(v.tensor_scalar(PPACK[32:36, :], PPACK[32:36, :], 20.0, -20.0,
                                        ALU.min, ALU.max), "V")
            pcl = ctx.tok("V")

            @ctx.section("S")
            def _(s_, pcl=pcl):
                ctx.wait(s_, pcl)
                s_.activation(PPACK[0:28, :], PPACK[0:28, :], AF.Exp)
                ctx.inc(s_.activation(PPACK[32:36, :], PPACK[32:36, :], AF.Exp), "S")
            pe = ctx.tok("S")

            @ctx.section("V")
            def _(v, pe=pe):
                ctx.wait(v, pe)
                ctx.wait(v, tw_tok)
                ctx.f(v, "V", v.tensor_tensor(PPACK[0:28, :], PPACK[0:28, :],
                                              TPACK[0:28, :], ALU.mult))
                ctx.inc(v.tensor_tensor(PPACK[32:36, :], PPACK[32:36, :],
                                        TPACK[32:36, :], ALU.mult), "V")
            pw_tok = ctx.tok("V")

            pww = []
            for tt in range(NT):
                def mk(tt=tt):
                    def mm(t, i):
                        return t.transpose(ctx.P2[0:128, 0:28],
                                           PPACK[0:28, tt * 128:(tt + 1) * 128],
                                           ctx.identf[0:28, 0:28])

                    def ev(e, i):
                        return e.copy(PWT[:, tt * 32:tt * 32 + 28], ctx.P2[0:128, 0:28])
                    return mm, ev
                pww.append(mk())
            for tt in range(NT):
                def mk(tt=tt):
                    def mm(t, i):
                        return t.transpose(ctx.P2[0:128, 0:4],
                                           PPACK[32:36, tt * 128:(tt + 1) * 128],
                                           ctx.identf[32:36, 32:36])

                    def ev(e, i):
                        return e.copy(PWT[:, tt * 32 + 28:tt * 32 + 32], ctx.P2[0:128, 0:4])
                    return mm, ev
                pww.append(mk())
            mm_pipeline(ctx, pww, evac="S", nbuf=1, t_prewait=[pw_tok])

        for ct in list(range(8)) + [16]:
            emit_ctile(ct, xval_consumer if ct < 8 else sraw_consumer)
        xvt_tok = (ctx.tok("S"), ctx.tok("V"))

        ctx.dump("pwt", PWT[:], xvt_tok[1], hold="G")
        ctx.dump("tpk", TPACK[:], xvt_tok[1], hold="G")
        ctx.dump("ppk", PPACK[:], xvt_tok[1], hold="G")
        SELS = TPACK[:].bitcast(BF16)[0:32, 0:4096]
        ISELv = ctx.XRES[0:32, 2048:6144].bitcast(I32)
        IOHv = ctx.XRES[0:128, 11264:12288].bitcast(I32)

        @ctx.section("G")
        def _(g, pv=(ctx.tok("S"), ctx.tok("V"))):
            ctx.wait(g, pv[0])
            ctx.wait(g, pv[1])
            g.iota(ISELv, [[1, 32], [0, 128]], base=0, channel_multiplier=-1)
            g.wait_ge(ctx.sem["G"], ctx.cnt["G"])
            g.tensor_single_scalar(SELS, ISELv, 0, ALU.is_equal)
            g.iota(IOHv, [[-1, 32], [1, 32]], base=0, channel_multiplier=0)
            ctx.inc(g.tensor_single_scalar(OHS[:], IOHv, 0, ALU.is_equal), "G")
        sel_tok = ctx.tok("G")
        @ctx.section("Y")
        def _(sy):
            ctx.wait(sy, cols_tok)
            ctx.wait(sy, spill_tok)
            ctx.dma(sy, THETB_F[:],
                    inp["L1_attn_a_theta"].rearrange("(o k) f -> o (k f)", o=1)
                    .broadcast_to((128, 4096)))
        th_tok = ctx.tok("LD")


        TRIFF = ctx.sb("TRIFF", [128, 128], F32, es)
        SINB = ctx.sb("SINB", [128, 512], BF16, es)
        COSB = ctx.sb("COSB", [128, 512], BF16, es)
        FPB = ctx.sb("FPB", [64, 2048], BF16, es)
        MSQT = ctx.II[:].bitcast(F32)[0:32, 0:128]
        WREf = MQR[:].bitcast(F32)[:, 0:64]

        @ctx.section("Y")
        def _(sy):
            ctx.dma(sy, WREf[:, 0:32], inp["L1_attn_a_W_re"])
            ctx.dma(sy, WREf[:, 32:64], inp["L1_attn_a_W_im"])
        wre_ld = ctx.tok("LD")

        @ctx.section("S")
        def _(s_):
            ctx.wait(s_, wre_ld)
            s_.copy(WREb[:], WREf[:, 0:32])
            ctx.inc(s_.copy(WIMb[:], WREf[:, 32:64]), "S")
        wre_tok = ctx.tok("S")

        @ctx.section("G")
        def _(g):
            gi3 = ctx.II[:, 128:256]
            g.iota(gi3, [[1, 128]], base=0, channel_multiplier=-1)
            g.tensor_single_scalar(TRIFF[:], gi3, 0, ALU.is_ge)
            g.memset(CARRE[:], 0.0)
            g.memset(CARIM[:], 0.0)
            ctx.inc(g.memset(CARDN[:], 0.0), "G")
        carr_init = ctx.tok("G")

        @ctx.section("V")
        def _(v):
            ctx.wait(v, wl_tok)
            ctx.inc(v.tensor_scalar(WLC[:, 2:3], WLC[:, 2:3], -1.0, None, ALU.mult), "V")
        wlneg_tok = ctx.tok("V")

        ld_xv = []
        xvf_done = []
        prev_msq = [None]
        prev_perdd = [None, None]
        for tt in range(NT):
            tcol = slice(tt * 128, (tt + 1) * 128)

            @ctx.section("Y")
            def _(sy, tt=tt):
                if tt == 0:
                    for tk in xv_st:
                        ctx.wait(sy, tk)
                if tt >= 2:
                    ctx.wait(sy, xvf_done[tt - 2])
                ctx.dma(sy, ctx.XRES[:, (tt % 2) * 1024:(tt % 2) * 1024 + 1024],
                        XVSP.ap()[tt * 128:(tt + 1) * 128, :])
            ld_xv.append(ctx.tok("LD"))

            @ctx.section("T")
            def _(t, tt=tt, g=ctx.guard_tokens(), pw=xvt_tok, pm=prev_msq[0]):
                if tt == 0:
                    ctx.wait(t, pw[0])
                    ctx.wait(t, pw[1])
                    ctx.wait(t, carr_init)
                    ctx.wait(t, wre_tok)
                    ctx.wait(t, wlneg_tok)
                    ctx.wait(t, sel_tok)
                    ctx.wait(t, th_tok)
                    for tokk in g:
                        ctx.wait(t, tokk)
                else:
                    ctx.wait(t, pm)
                ctx.inc(t.matmul(ctx.P2[0:32, 0:128], PWT[:, tt * 32:(tt + 1) * 32],
                                 TRIFF[:], start=True, stop=True), "T")
            dc_T = ctx.tok("T")

            @ctx.section("V")
            def _(v, tt=tt, dc_T=dc_T):
                ctx.wait(v, dc_T)
                ctx.f(v, "V", v.tensor_scalar(DENP[0:32, tcol], ctx.P2[0:32, 0:128],
                                              CARDN[0:32, 0:1], None, ALU.add))
                ctx.inc(v.tensor_copy(CARDN[0:32, 0:1], DENP[0:32, tt * 128 + 127:tt * 128 + 128]), "V")
            den_tok = ctx.tok("V")

            ZRB = ZRAW[:].bitcast(BF16)
            SINP = ZRB[:, 0:1024]
            COSP = ZRB[:, 1024:2048]
            MQRP = ZRB[:, 2048:3072]
            MQIP = ZRB[:, 3072:4096]
            XV2B = ctx.XRES[:, 15360:16384]
            Q1B = ctx.LNJ[:, 0:1024]
            POLYB = ctx.LNJ[:, 1024:2048]
            sb0 = ctx.cnt["S"]
            ctx.cnt["S"] += 8
            ab_toks = [None] * 4
            cst_toks = [None] * 4
            mq_toks = [None] * 4
            cs_drain = [None, None]

            def emit_rr(pp, tt=tt):
                par = pp % 2
                xvp = ctx.XRES[:, (tt % 2) * 1024 + pp * 256:(tt % 2) * 1024 + (pp + 1) * 256]
                XVq = XV2 if par == 0 else XV2B
                POq = POLY if par == 0 else POLYB
                Q1q = Q1 if par == 0 else Q1B

                @ctx.section("V")
                def _(v, tt=tt, pp=pp, xvp=xvp, XVq=XVq, POq=POq, Q1q=Q1q):
                    if pp == 0:
                        ctx.wait(v, den_tok)
                        ctx.wait(v, ld_xv[tt])
                    ctx.f(v, "V", v.tensor_tensor(XVq[:, 0:256], xvp, xvp, ALU.mult))
                    ctx.f(v, "V", v.tensor_scalar(POq[:, 0:256], xvp, WLC[:, 1:2],
                                                  WLC[:, 0:1], ALU.mult, ALU.add))
                    ctx.f(v, "V", v.scalar_tensor_tensor(POq[:, 0:256], XVq[:, 0:256],
                                                         WLC[:, 2:3], POq[:, 0:256],
                                                         ALU.mult, ALU.add))
                    pwv = PWT[:, tt * 32 + pp * 8:tt * 32 + (pp + 1) * 8]
                    ctx.f(v, "V", v.tensor_tensor(
                        Q1q.rearrange("p (a b c) -> p a b c", a=8, b=32),
                        POq[:, 0:256].rearrange("p (a b) -> p a b", a=8)
                        .rearrange("p a (b o) -> p a b o", o=1)
                        .broadcast_to((128, 8, 32, 4)),
                        pwv.rearrange("p (a o) -> p a o", o=1)
                        .rearrange("p (a z) o -> p a z o", z=1)
                        .broadcast_to((128, 8, 32, 4)), ALU.mult))
                    ctx.f(v, "V", v.tensor_tensor(
                        XVq.rearrange("p (a b c) -> p a b c", a=8, b=32),
                        xvp.rearrange("p (a b) -> p a b", a=8)
                        .rearrange("p a (b o) -> p a b o", o=1)
                        .broadcast_to((128, 8, 32, 4)),
                        THETB_F[:, pp * 1024:(pp + 1) * 1024]
                        .rearrange("p (a b c) -> p a b c", a=8, b=32), ALU.mult))
                    ctx.f(v, "V", v.tensor_scalar(POq[:], XVq[:], 1.0 / (2 * PI),
                                                  None, ALU.mult))
                    ctx.f(v, "V", v.tensor_copy(ctx.II[:, 0:512], POq[:, 0:512]))
                    ctx.f(v, "V", v.tensor_copy(POq[:, 0:512], ctx.II[:, 0:512]))
                    ctx.f(v, "V", v.tensor_copy(ctx.II[:, 0:512], POq[:, 512:1024]))
                    ctx.f(v, "V", v.tensor_copy(POq[:, 512:1024], ctx.II[:, 0:512]))
                    ctx.f(v, "V", v.scalar_tensor_tensor(XVq[:], POq[:], -2 * PI,
                                                         XVq[:], ALU.mult, ALU.add))
                    ctx.f(v, "V", v.tensor_scalar(POq[:].bitcast(I32), XVq[:].bitcast(I32),
                                                  0x7FFFFFFF, None, ALU.bitwise_and))
                    ctx.inc(v.tensor_scalar(POq[:], POq[:], -1.0, PI / 2,
                                            ALU.mult, ALU.add), "V")
                ab_toks[pp] = ctx.tok("V")

            def emit_mq(pp, tt=tt):
                par = pp % 2
                Q1q = Q1 if par == 0 else Q1B

                @ctx.section("V")
                def _(v, pp=pp, Q1q=Q1q):
                    v.wait_ge(ctx.sem["S"], sb0 + 2 * (pp + 1))
                    if pp >= 1:
                        ctx.wait(v, cst_toks[pp - 1])
                    v.tensor_tensor(MQRP[:], Q1q, COSP[:], ALU.mult)\
                        .then_inc(ctx.sem["V"], 1)
                    ctx.cnt["V"] += 1
                    ctx.inc(v.tensor_tensor(MQIP[:], Q1q, SINP[:], ALU.mult), "V")
                mq_toks[pp] = ctx.tok("V")
                mq_tok = mq_toks[pp]

                t0b = ctx.cnt["T"]
                v0b = ctx.cnt["V"]
                ctx.cnt["T"] += 4
                ctx.cnt["V"] += 8
                cst_toks[pp] = ("T", t0b + 4)
                cs_drain[0] = cs_drain[1]
                cs_drain[1] = ("V", v0b + 8)
                banks = [ctx.P0, ctx.P1, ctx.P2]

                @ctx.section("T")
                def _(t, tt=tt, pp=pp, mq_tok=mq_tok, t0b=t0b, v0b=v0b,
                      pv=cs_drain[0], g=ctx.guard_tokens()):
                    ctx.wait(t, mq_tok)
                    if pv is not None:
                        ctx.wait(t, pv)
                    for tokk in g:
                        ctx.wait(t, tokk)
                    for g4 in range(4):
                        im, half = g4 % 2, g4 // 2
                        src = MQIP if im else MQRP
                        pb = banks[g4 % 3]
                        if g4 >= 3:
                            t.wait_ge(ctx.sem["V"], v0b + 2)
                        for j in range(4):
                            cb2 = half * 4 + j
                            r = t.matmul(pb[:, j * 128:(j + 1) * 128],
                                         src[:, cb2 * 128:(cb2 + 1) * 128],
                                         ctx.tri01[:], start=True, stop=True)
                        r.then_inc(ctx.sem["T"], 1)

                @ctx.section("V")
                def _(v, tt=tt, pp=pp, t0b=t0b, v0b=v0b):
                    for g4 in range(4):
                        im, half = g4 % 2, g4 // 2
                        pb = banks[g4 % 3]
                        dstb = NUMI if im else NUMR
                        blk0 = pp * 8 + half * 4
                        carrs = (CARIM if im else CARRE)[:, blk0:blk0 + 4]
                        pbv = pb[:, 0:512].rearrange("p (c t) -> p c t", t=128)
                        v.wait_ge(ctx.sem["T"], t0b + g4 + 1)
                        v.tensor_tensor(
                            dstb[:, blk0 * 128:(blk0 + 4) * 128]
                            .rearrange("p (c t) -> p c t", t=128),
                            pbv,
                            carrs.rearrange("p (c o) -> p c o", o=1)
                            .broadcast_to((128, 4, 128)), ALU.add)\
                            .then_inc(ctx.sem["V"], 1)
                        v.tensor_tensor(
                            carrs.rearrange("p (c o) -> p c o", o=1),
                            carrs.rearrange("p (c o) -> p c o", o=1),
                            pbv[:, :, 127:128], ALU.add)\
                            .then_inc(ctx.sem["V"], 1)

            for pp in range(4):
                emit_rr(pp)
                if pp >= 1:
                    emit_mq(pp - 1)

            @ctx.section("S")
            def _(s_, ab=list(ab_toks), mq=list(mq_toks), tt=tt, sb0=sb0):
                for pp in range(4):
                    par = pp % 2
                    XVq = XV2 if par == 0 else XV2B
                    POq = POLY if par == 0 else POLYB
                    ctx.wait(s_, ab[pp])
                    if pp >= 1:
                        ctx.wait(s_, mq[pp - 1])
                    s_.activation(SINP[:], XVq[:], AF.Sin).then_inc(ctx.sem["S"], 1)
                    s_.activation(COSP[:], POq[:], AF.Sin).then_inc(ctx.sem["S"], 1)

            emit_mq(3)
            cs_tok = ctx.tok("V")
            xvf_done.append(cs_tok)
            ctx.psum_guard["V"] = ctx.cnt["V"]

            @ctx.section("V")
            def _(v, tt=tt, cs_tok=cs_tok):
                ctx.wait(v, cs_tok)
                ctx.f(v, "V", v.tensor_scalar(DENP[0:32, tcol], DENP[0:32, tcol],
                                              1e-4, None, ALU.max))
                ctx.inc(v.reciprocal(DENP[0:32, tcol], DENP[0:32, tcol]), "V")
            inv_tok = ctx.tok("V")

            t0c = ctx.cnt["T"]
            v0c = ctx.cnt["V"]
            ctx.cnt["T"] += 16
            ctx.cnt["V"] += 16

            @ctx.section("V")
            def _(v, tt=tt, t0c=t0c, v0c=v0c, cs_tok=cs_tok):
                ctx.wait(v, cs_tok)
                for i in range(16):
                    grp, im = i // 2, i % 2
                    srcb = (NUMI if im else NUMR)[:, grp * 512:(grp + 1) * 512]
                    dstq = MQRP[:, 0:512] if i % 2 == 0 else MQIP[:, 0:512]
                    if i >= 2:
                        v.wait_ge(ctx.sem["T"], t0c + i - 1)
                    v.tensor_tensor(dstq[:], srcb, srcb, ALU.mult)\
                        .then_inc(ctx.sem["V"], 1)

            @ctx.section("T")
            def _(t, tt=tt, t0c=t0c, v0c=v0c, den_tok=den_tok):
                ctx.wait(t, den_tok)
                for i in range(16):
                    grp = i // 2
                    srcq = MQRP[:, 0:512] if i % 2 == 0 else MQIP[:, 0:512]
                    t.wait_ge(ctx.sem["V"], v0c + i + 1)
                    for j in range(4):
                        kk = grp * 4 + j
                        r = t.matmul(ctx.P2[0:32, 0:128], OHS[:, kk * 32:(kk + 1) * 32],
                                     srcq[:, j * 128:(j + 1) * 128],
                                     start=(i == 0 and j == 0), stop=(i == 15 and j == 3))
                    r.then_inc(ctx.sem["T"], 1)
            s2_T = ctx.tok("T")

            @ctx.section("V")
            def _(v, tt=tt, s2_T=s2_T, inv_tok=inv_tok):
                ctx.wait(v, s2_T)
                ctx.wait(v, inv_tok)
                ctx.f(v, "V", v.tensor_tensor(MSQT[:], DENP[0:32, tcol], DENP[0:32, tcol],
                                              ALU.mult))
                ctx.inc(v.tensor_tensor(MSQT[:], MSQT[:], ctx.P2[0:32, 0:128], ALU.mult), "V")
            msq_V = ctx.tok("V")
            prev_msq[0] = msq_V

            @ctx.section("S")
            def _(s_, tt=tt, msq_V=msq_V):
                ctx.wait(s_, msq_V)
                ctx.f(s_, "S", s_.activation(MSQT[:], MSQT[:], AF.Ln, scale=1.0 / 256.0,
                                             bias=ctx.ccol(1e-5)[0:32, :]))
                ctx.inc(s_.activation(MSQT[:], MSQT[:], AF.Exp, scale=-0.5), "S")
            rsq_S = ctx.tok("S")

            @ctx.section("V")
            def _(v, tt=tt, rsq_S=rsq_S):
                ctx.wait(v, rsq_S)
                ctx.inc(v.tensor_tensor(FPB[0:32, tcol], DENP[0:32, tcol], MSQT[:],
                                        ALU.mult), "V")
            f_tok = ctx.tok("V")

            sb2 = ctx.cnt["S"]
            ctx.cnt["S"] += 32
            yk_toks = []
            for dd in range(8):
                @ctx.section("T")
                def _(t, dd=dd, f_tok=f_tok, pv=prev_perdd[0], ps=prev_perdd[1],
                      tt=tt, sb2=sb2):
                    ctx.wait(t, f_tok)
                    if dd == 0 and pv is not None:
                        ctx.wait(t, pv)
                    for j in range(4):
                        kk = dd * 4 + j
                        r = t.matmul(ctx.PO[0:128, j * 128:(j + 1) * 128],
                                     SELS[:, kk * 128:(kk + 1) * 128],
                                     FPB[0:32, tcol], start=True, stop=True)
                    ctx.inc(r, "T")
                fb_T = ctx.tok("T")

                @ctx.section("V")
                def _(v, dd=dd, fb_T=fb_T):
                    ctx.wait(v, fb_T)
                    v.scalar_tensor_tensor(REN4[:], NUMR[:, dd * 512:(dd + 1) * 512],
                                           NSC[:, 0:1], ctx.PO[0:128, 0:512],
                                           ALU.mult, ALU.mult).then_inc(ctx.sem["V"], 1)
                    ctx.cnt["V"] += 1
                    ctx.inc(v.scalar_tensor_tensor(IMN4[:], NUMI[:, dd * 512:(dd + 1) * 512],
                                                   NSC[:, 1:2], ctx.PO[0:128, 0:512],
                                                   ALU.mult, ALU.mult), "V")
                ri_V = ctx.tok("V")

                @ctx.section("T")
                def _(t, dd=dd, ri_V=ri_V, ps=prev_perdd[1], sb2=sb2):
                    ctx.wait(t, ri_V)
                    if dd == 0:
                        if ps is not None:
                            ctx.wait(t, ps)
                    else:
                        t.wait_ge(ctx.sem["S"], sb2 + 4 * dd)
                    for j in range(4):
                        t.matmul(ctx.PO[0:32, 512 + j * 128:512 + (j + 1) * 128],
                                 WREb[:], REN4[:, j * 128:(j + 1) * 128],
                                 start=True, stop=False)
                        r = t.matmul(ctx.PO[0:32, 512 + j * 128:512 + (j + 1) * 128],
                                     WIMb[:], IMN4[:, j * 128:(j + 1) * 128],
                                     start=False, stop=True)
                    ctx.inc(r, "T")
                yk_toks.append(ctx.tok("T"))

            @ctx.section("S")
            def _(s_, tt=tt, yk=list(yk_toks), sb2=sb2):
                for dd in range(8):
                    ctx.wait(s_, yk[dd])
                    for j in range(4):
                        s_.copy(ctx.XLNB[j * 32:(j + 1) * 32,
                                         tt * 1024 + dd * 128:tt * 1024 + (dd + 1) * 128],
                                ctx.PO[0:32, 512 + j * 128:512 + (j + 1) * 128])\
                            .then_inc(ctx.sem["S"], 1)
            prev_perdd[1] = ("S", sb2 + 32)
            prev_perdd[0] = ctx.tok("V")
            ctx.psum_guard["S"] = ctx.cnt["S"]
            ctx.psum_guard["V"] = ctx.cnt["V"]

        ctx.dump("sels", TPACK[:], ctx.tok("G"), hold="G")
        ctx.dump("ych", ctx.XLNB[:], ctx.tok("S"), hold=["S", "V", "T", "G"])
        ctx.dump("denp", DENP[:], ctx.tok("V"), hold="V")
        ctx.dump("fpb", FPB[:], ctx.tok("V"), hold="V")
        ctx.dump("numr", NUMR, ctx.tok("V"), hold="V")
        ctx.dump("numi", NUMI, ctx.tok("V"), hold="V")
        def gate_consumer(ct, conv_tok):
            dd = ct - 8

            @ctx.section("S")
            def _(s_, conv_tok=conv_tok):
                ctx.wait(s_, conv_tok)
                ctx.inc(s_.activation(CONVB[:], CONV[:], AF.Silu), "S")
            sg_tok = ctx.tok("S")

            @ctx.section("V")
            def _(v, dd=dd, sg_tok=sg_tok):
                ctx.wait(v, sg_tok)
                yv = ctx.XLNB[:, dd * 128:dd * 128 + 15 * 1024 + 128]\
                    .rearrange("p (a c) -> p a c", c=128) if False else \
                    ctx.XLNB[:].rearrange("p (a c) -> p a c", c=1024)[:, :, dd * 128:(dd + 1) * 128]
                cv = CONVB[:].rearrange("p (a c) -> p a c", c=128)
                ctx.inc(v.tensor_tensor(yv, yv, cv, ALU.mult), "V")
        for ct in range(8, 16):
            emit_ctile(ct, gate_consumer)
        yg_tok = ctx.tok("V")

        @ctx.section("G")
        def _(g):
            ctx.wait(g, ctx.tok("T"))
            ctx.wait(g, spill_tok)
            ctx.dma(g, WOUT.rearrange("p (a c) -> p a c", a=8),
                    inp["L1_attn_a_W_out"].rearrange("(a p) c -> p a c", p=128))
        wo_ld = ctx.tok("LG")

        for tt in range(NT):
            @ctx.section("Y")
            def _(sy, tt=tt, yg_tok=yg_tok, pv=ctx.tok("V")):
                ctx.wait(sy, spill_tok)
                ctx.wait(sy, yg_tok if tt == 0 else pv)
                ctx.dma(sy, xt, XSP2.ap()[tt * 128:(tt + 1) * 128, :])
            xld = ctx.tok("LD")

            ow = []
            for oc in range(2):
                def mk(tt=tt, oc=oc):
                    def mm(t, i):
                        pb = ctx.P0 if i % 2 == 0 else ctx.P1
                        for dd in range(ND):
                            r = t.matmul(pb[:], ctx.XLNB[:, tt * 1024 + dd * 128:tt * 1024 + (dd + 1) * 128],
                                         WOUT[:, dd * 1024 + oc * 512:dd * 1024 + oc * 512 + 512],
                                         start=(dd == 0), stop=(dd == ND - 1))
                        return r

                    def ev(e, i):
                        pb = ctx.P0 if i % 2 == 0 else ctx.P1
                        xsl = xt[:, oc * 512:(oc + 1) * 512]
                        return e.tensor_tensor(xsl, xsl, pb[:], ALU.add)
                    return mm, ev
                ow.append(mk())
            pre_o = [xld, yg_tok] + ([wo_ld] if tt == 0 else [])
            yo_tok = mm_pipeline(ctx, ow, evac="V", t_prewait=pre_o)

            @ctx.section("Y")
            def _(sy, tt=tt, yo_tok=yo_tok):
                ctx.wait(sy, yo_tok)
                sy.dma_start(XSP2.ap()[tt * 128:(tt + 1) * 128, :], xt)\
                    .then_inc(ctx.sem["ST"], 16)
                ctx.cnt["ST"] += 16

        @ctx.section("Y")
        def _(sy):
            sy.wait_ge(ctx.sem["ST"], ctx.cnt["ST"])
            ctx.dma(sy, ctx.XRES[:].rearrange("p (a d) -> p a d", a=NT),
                    XSP2.ap().rearrange("(a p) d -> p a d", p=128))
        ctx.x_tok = ctx.tok("LD")


def emit_l1_ffn(ctx, inp, state):
    ctx.barrier()
    es = contextlib.ExitStack()
    with es:
        if "wqT" not in state:
            state["wqT"] = ctx.sb("wqT1", [128, 8192], BF16, es)
            state["wsb"] = ctx.sb("wsb1", [128, D], F32, es)
            state["bias_bc"] = ctx.sb("biasbc1", [128, D], F32, es)
            state["wsdr"] = ctx.nc.dram_tensor(f"wsdr{ctx.uid}", [8, 128], F32)
        II3 = ctx.XLNB[:].bitcast(I32)[:, 4096:5120]
        LNJ2 = ctx.XLNB[:].bitcast(F32)[:, 5120:6144]

        if "w_done" not in state:
            wf = ctx.sb("wf", [128, 4096], F32, es)
            wq8 = ctx.XLNB[:, 0:8192]
            wcol = ctx.sb("wcol", [128, 32], F32, es)
            WS8 = ctx.sb("WS8", [8, 128], F32, es)
            wsrc = inp["L1_ffn_weight"].rearrange("(a p) c -> p a c", p=128)
            sg0 = ctx.start_guard()

            @ctx.section("Y")
            def _(sy):
                for tokk in sg0:
                    ctx.wait(sy, tokk)
                ctx.dma(sy, state["bias_bc"][:],
                        inp["L1_ffn_bias"].broadcast_to((128, D)))
                ctx.dma(sy, wf[:].rearrange("p (a c) -> p a c", a=4), wsrc[:, 0:4, :])
            wld0 = ctx.tok("LD")

            @ctx.section("V")
            def _(v):
                ctx.wait(v, wld0)
                for d in range(4):
                    ctx.inc(v.tensor_reduce(wcol[:, d:d + 1], wf[:, d * 1024:(d + 1) * 1024],
                                            AX.X, ALU.max, apply_absolute_value=True), "V")
                v.wait_ge(ctx.sem["V"], ctx.cnt["V"])
            wv0 = ctx.tok("V")

            @ctx.section("Y")
            def _(sy):
                ctx.wait(sy, wv0)
                ctx.dma(sy, wf[:].rearrange("p (a c) -> p a c", a=4), wsrc[:, 4:8, :])
            wld1 = ctx.tok("LD")

            @ctx.section("V")
            def _(v):
                ctx.wait(v, wld1)
                for d in range(4):
                    ctx.inc(v.tensor_reduce(wcol[:, 4 + d:5 + d], wf[:, d * 1024:(d + 1) * 1024],
                                            AX.X, ALU.max, apply_absolute_value=True), "V")
                v.wait_ge(ctx.sem["V"], ctx.cnt["V"])
                ctx.f(v, "V", v.tensor_scalar(wcol[:, 8:16], wcol[:, 0:8], 1e-8, None, ALU.max))
                ctx.f(v, "V", v.reciprocal(wcol[:, 16:24], wcol[:, 8:16]))
                ctx.f(v, "V", v.tensor_scalar(wcol[:, 16:24], wcol[:, 16:24], 127.0, None, ALU.mult))
                ctx.inc(v.tensor_scalar(wcol[:, 24:32], wcol[:, 8:16], 1.0 / 16129.0,
                                        None, ALU.mult), "V")
            wv_tok = ctx.tok("V")

            def ws_mm(t, i):
                return t.transpose(ctx.P2[0:8, 0:128], wcol[:, 24:32], ctx.identf[:])

            def ws_ev(e, i):
                return e.copy(WS8[:], ctx.P2[0:8, 0:128])
            ws_tok = mm_pipeline(ctx, [(ws_mm, ws_ev)], evac="S", t_prewait=[wv_tok])

            @ctx.section("Y")
            def _(sy):
                ctx.wait(sy, ws_tok)
                ctx.dma(sy, state["wsdr"].ap(), WS8[:])
            wsd_tok = ctx.tok("LD")

            @ctx.section("Y")
            def _(sy):
                ctx.wait(sy, wsd_tok)
                ctx.dma(sy, state["wsb"][:],
                        state["wsdr"].ap().rearrange("(o a) j -> o (a j)", o=1)
                        .broadcast_to((128, D)))
            wsb_tok = ctx.tok("LD")

            for half in range(2):
                @ctx.section("Y")
                def _(sy, half=half, vg=ctx.tok("V")):
                    ctx.wait(sy, vg)
                    ctx.dma(sy, wf[:].rearrange("p (a c) -> p a c", a=4),
                            wsrc[:, 4 * half:4 * half + 4, :])
                wldq = ctx.tok("LD")

                @ctx.section("V")
                def _(v, half=half, wldq=wldq, tguard=("T", ctx.cnt["T"])):
                    ctx.wait(v, wldq)
                    ctx.wait(v, tguard)
                    for hh in range(4):
                        a = half * 4 + hh
                        sl = slice(a * 1024, (a + 1) * 1024)
                        lsl = slice(hh * 1024, (hh + 1) * 1024)
                        ctx.f(v, "V", v.tensor_scalar(LNJ2[:, 0:1024], wf[:, lsl],
                                                      wcol[:, 16 + a:17 + a], None, ALU.mult))
                        ctx.f(v, "V", v.tensor_copy(II3[:, 0:1024], LNJ2[:, 0:1024]))
                        ctx.f(v, "V", v.tensor_copy(LNJ2[:, 0:1024], II3[:, 0:1024]))
                        ctx.f(v, "V", v.tensor_scalar(wq8[:, sl], LNJ2[:, 0:1024],
                                                      127.0, -128.0, ALU.min, ALU.max))
            wq8_tok = ctx.tok("V")

            wtw = []
            for po in range(8):
                for d in range(8):
                    def mk(po=po, d=d):
                        def mm(t, i):
                            pb = (ctx.P0 if i % 2 == 0 else ctx.P1)[:].bitcast(BF16)
                            return t.transpose(pb[:, 0:128],
                                               wq8[:, po * 1024 + d * 128:po * 1024 + d * 128 + 128],
                                               ctx.identb[:])

                        def ev(e, i):
                            pb = (ctx.P0 if i % 2 == 0 else ctx.P1)[:].bitcast(BF16)
                            return e.copy(state["wqT"][:, d * 1024 + po * 128:d * 1024 + po * 128 + 128],
                                          pb[:, 0:128])
                        return mm, ev
                    wtw.append(mk())
            mm_pipeline(ctx, wtw, evac="S", t_prewait=[wq8_tok])
            state["w_done"] = (ctx.tok("S"), wsb_tok)

        rstd, negb, mean, _ = emit_ln(ctx, wait_tok=ctx.x_tok, apply=False)
        amscr = ctx.sb("amscr", [128, 48], F32, es)
        mx = amscr[:, 0:NT]
        mn = amscr[:, NT:2 * NT]
        am = amscr[:, 2 * NT:3 * NT]
        sA = ctx.sb("sAq", [128, NT], F32, es)
        sB = ctx.sb("sBq", [128, NT], F32, es)
        xqt = ctx.sb("xqtq", [128, D], BF16, es)
        vt_ln = ctx.tok("V")

        @ctx.section("V")
        def _(v):
            ctx.wait(v, ctx.x_tok)
            ctx.wait(v, vt_ln)
            for tt in range(NT):
                ctx.inc(v.tensor_reduce(mx[:, tt:tt + 1], ctx.X(tt), AX.X, ALU.max), "V")
            for tt in range(NT):
                ctx.inc(v.tensor_reduce(mn[:, tt:tt + 1], ctx.X(tt), AX.X, ALU.min), "V")
            v.wait_ge(ctx.sem["V"], ctx.cnt["V"])
            ctx.f(v, "V", v.tensor_tensor(mx[:], mx[:], mean[:], ALU.subtract))
            ctx.f(v, "V", v.tensor_tensor(mn[:], mean[:], mn[:], ALU.subtract))
            ctx.f(v, "V", v.tensor_tensor(mx[:], mx[:], mn[:], ALU.max))
            ctx.f(v, "V", v.tensor_tensor(am[:], mx[:], rstd[:], ALU.mult))
            ctx.f(v, "V", v.tensor_scalar(am[:], am[:], 1e-8, None, ALU.max))
            ctx.f(v, "V", v.reciprocal(sA[:], am[:]))
            ctx.f(v, "V", v.tensor_scalar(sA[:], sA[:], 127.0, None, ALU.mult))
            ctx.f(v, "V", v.tensor_tensor(sB[:], negb[:], sA[:], ALU.mult))
            ctx.inc(v.tensor_tensor(sA[:], sA[:], rstd[:], ALU.mult), "V")
        sab_tok = ctx.tok("V")
        ctx.dump("amq", amscr[:], sab_tok)

        xqfm = ctx.XFM
        xq_toks = []
        for tt in range(NT):
            @ctx.section("S")
            def _(s, tt=tt):
                if tt == 0:
                    ctx.wait(s, sab_tok)
                else:
                    ctx.wait(s, xq_toks[tt - 1])
                ctx.inc(s.activation(LNJ2[:, 0:D], ctx.X(tt), AF.Identity,
                                     bias=sB[:, tt:tt + 1], scale=sA[:, tt:tt + 1]), "S")
            u_tok = ctx.tok("S")

            @ctx.section("V")
            def _(v, u_tok=u_tok, tt=tt, tguard=("T", ctx.cnt["T"])):
                ctx.wait(v, u_tok)
                ctx.wait(v, tguard)
                ctx.f(v, "V", v.tensor_copy(II3[:, 0:D], LNJ2[:, 0:D]))
                ctx.f(v, "V", v.tensor_copy(LNJ2[:, 0:D], II3[:, 0:D]))
                ctx.inc(v.tensor_scalar(xqt[:], LNJ2[:, 0:D], 127.0, -128.0,
                                        ALU.min, ALU.max), "V")
            xq_toks.append(ctx.tok("V"))

            tw = []
            for d in range(ND):
                def mk(d=d, tt=tt):
                    def mm(t, i):
                        pb = (ctx.P0 if i % 2 == 0 else ctx.P1)[:].bitcast(BF16)
                        return t.transpose(pb[:, 0:128], xqt[:, d * 128:(d + 1) * 128],
                                           ctx.identb[:])

                    def ev(e, i):
                        pb = (ctx.P0 if i % 2 == 0 else ctx.P1)[:].bitcast(BF16)
                        return ecopy(e, xqfm[:, d * T + tt * 128:d * T + tt * 128 + 128],
                                     pb[:, 0:128])
                    return mm, ev
                tw.append(mk())
            mm_pipeline(ctx, tw, evac="V", t_prewait=[xq_toks[-1]])
        xq_done = ctx.tok("V")
        ctx.dump("xqfmq", xqfm[:], xq_done)

        nrounds = NT * 2
        t0 = ctx.cnt["T"]; s0 = ctx.cnt["S"]; v0 = ctx.cnt["V"]
        guards = ctx.guard_tokens()
        ctx.cnt["T"] = t0 + nrounds
        ctx.cnt["S"] = s0 + nrounds
        ctx.cnt["V"] = v0 + 3 * nrounds
        ctx.psum_guard["S"] = ctx.cnt["S"]
        ctx.psum_guard["V"] = ctx.cnt["V"]

        @ctx.section("T")
        def _(t, t0=t0, s0=s0):
            ctx.wait(t, xq_done)
            ctx.wait(t, state["w_done"][0])
            for tok in guards:
                ctx.wait(t, tok)
            for i in range(nrounds):
                tt, oc = i // 2, i % 2
                pb = ctx.P0 if i % 2 == 0 else ctx.P1
                if i >= 2:
                    t.wait_ge(ctx.sem["S"], s0 + i - 1)
                for d in range(ND):
                    r = t.matmul(pb[:], xqfm[:, d * T + tt * 128:d * T + tt * 128 + 128],
                                 state["wqT"][:, d * 1024 + oc * 512:d * 1024 + oc * 512 + 512],
                                 start=(d == 0), stop=(d == ND - 1))
                r.then_inc(ctx.sem["T"], 1)

        @ctx.section("S")
        def _(s, t0=t0, s0=s0, v0=v0):
            for i in range(nrounds):
                tt = i // 2
                pb = ctx.P0 if i % 2 == 0 else ctx.P1
                s.wait_ge(ctx.sem["T"], t0 + i + 1)
                if i >= 2:
                    s.wait_ge(ctx.sem["V"], v0 + 3 * (i - 1))
                s.activation(LNJ2[:, (i % 2) * 512:(i % 2) * 512 + 512],
                             pb[:], AF.Identity, scale=am[:, tt:tt + 1])\
                    .then_inc(ctx.sem["S"], 1)

        @ctx.section("V")
        def _(v, s0=s0, v0=v0):
            ctx.wait(v, state["w_done"][1])
            for i in range(nrounds):
                tt, oc = i // 2, i % 2
                v.wait_ge(ctx.sem["S"], s0 + i + 1)
                sl = LNJ2[:, (i % 2) * 512:(i % 2) * 512 + 512]
                xsl = ctx.X(tt)[:, oc * 512:(oc + 1) * 512]
                v.tensor_tensor(sl, sl, state["wsb"][:, oc * 512:(oc + 1) * 512],
                                ALU.mult).then_inc(ctx.sem["V"], 1)
                v.wait_ge(ctx.sem["V"], v0 + 3 * i + 1)
                v.tensor_tensor(sl, sl, state["bias_bc"][:, oc * 512:(oc + 1) * 512],
                                ALU.add).then_inc(ctx.sem["V"], 1)
                v.wait_ge(ctx.sem["V"], v0 + 3 * i + 2)
                v.tensor_tensor(xsl, xsl, sl, ALU.add).then_inc(ctx.sem["V"], 1)
        ctx.x_tok = ctx.tok("V")


def _out_pieces(p, shift, Tp, Tn=2048):
    pieces = []
    run = None
    for j in range(120):
        tpr = p * 120 + j
        ok = tpr < Tp
        t = (tpr + shift) % Tp if ok else 0
        ok = ok and t < Tn
        if ok and run is not None and t == run[2] + (j - run[0]):
            run = (run[0], j + 1, run[2])
        elif ok:
            if run:
                pieces.append((run[0], run[1], run[2]))
            run = (j, j + 1, t)
        else:
            if run:
                pieces.append((run[0], run[1], run[2]))
            run = None
    if run:
        pieces.append((run[0], run[1], run[2]))
    return pieces


def emit_l1_attn_b(ctx, inp):
    nc = ctx.nc
    ctx.barrier()
    es = contextlib.ExitStack()
    with es:
        DWNT = ctx.sb("DWNT", [128, 8 * 512], BF16, es)
        UPT = ctx.sb("UPT", [128, 4 * 1024], BF16, es)
        NW1B = ctx.sb("NW1B", [128, 512], F32, es)
        NW2B = ctx.sb("NW2B", [128, 1024], F32, es)
        DBB = ctx.sb("DBB", [128, 512], F32, es)
        UBB = ctx.sb("UBB", [128, 1024], F32, es)
        XIT = ctx.sb("XITb", [128, 2048], BF16, es)

        sg0 = ctx.start_guard()

        @ctx.section("Y")
        def _(sy):
            for tokk in sg0:
                ctx.wait(sy, tokk)
            ctx.dma(sy, NW1B[:], inp["L1_attn_b_norm1_w"].broadcast_to((128, 512)))
            ctx.dma(sy, NW2B[:], inp["L1_attn_b_norm2_w"].broadcast_to((128, D)))
            ctx.dma(sy, DBB[:], inp["L1_attn_b_down_b"].broadcast_to((128, 512)))
            ctx.dma(sy, UBB[:], inp["L1_attn_b_up_b"].broadcast_to((128, D)))

        @ctx.section("G")
        def _(g):
            for tokk in sg0:
                ctx.wait(g, tokk)
            load_w(ctx, g, inp["L1_attn_b_down_w"], DWNT)
            load_w(ctx, g, inp["L1_attn_b_up_w"], UPT)
        dwn_ld = ctx.tok("LD")
        dwn_lg = ctx.tok("LG")

        _, _, _, lntok = emit_ln(ctx, wait_tok=ctx.x_tok)
        xfm_tok = emit_transpose_to_xfm(ctx, lntok)

        RS1 = ctx.LNrstd
        xi_apply = []
        for tt in range(NT):
            @ctx.section("T")
            def _(t, tt=tt, pg=ctx.guard_tokens()):
                if tt == 0:
                    ctx.wait(t, xfm_tok)
                    ctx.wait(t, dwn_ld)
                    ctx.wait(t, dwn_lg)
                    for tokk in pg:
                        ctx.wait(t, tokk)
                if tt >= 2:
                    ctx.wait(t, xi_apply[tt - 2])
                pb = ctx.P0 if tt % 2 == 0 else ctx.P1
                for d in range(ND):
                    r = t.matmul(pb[:], ctx.XF(d)[:, tt * 128:(tt + 1) * 128],
                                 DWNT[:, d * 512:(d + 1) * 512],
                                 start=(d == 0), stop=(d == ND - 1))
                ctx.inc(r, "T")
            dn_T = ctx.tok("T")

            @ctx.section("V")
            def _(v, tt=tt, dn_T=dn_T):
                ctx.wait(v, dn_T)
                pb = ctx.P0 if tt % 2 == 0 else ctx.P1
                ctx.inc(v.tensor_tensor(ctx.LNJ[:, (tt % 2) * 512:(tt % 2) * 512 + 512],
                                        pb[:, 0:512], DBB[:], ALU.add), "V")
            ev_V = ctx.tok("V")

            @ctx.section("V")
            def _(v, tt=tt, ev_V=ev_V):
                ctx.wait(v, ev_V)
                sq = ctx.LNJ[:, 1024 + (tt % 2) * 512:1024 + (tt % 2) * 512 + 512]
                xd = ctx.LNJ[:, (tt % 2) * 512:(tt % 2) * 512 + 512]
                ctx.f(v, "V", v.tensor_tensor(sq, xd, xd, ALU.mult))
                ctx.f(v, "V", v.tensor_reduce(ctx.LNsq[:, tt:tt + 1], sq, AX.X, ALU.add))
                ctx.f(v, "V", v.tensor_scalar(RS1[:, tt:tt + 1], ctx.LNsq[:, tt:tt + 1],
                                              1.0 / 512.0, 1e-6, ALU.mult, ALU.add))
                ctx.inc(v.reciprocal(RS1[:, tt:tt + 1], RS1[:, tt:tt + 1]), "V")
            ms_V = ctx.tok("V")

            @ctx.section("S")
            def _(s_, tt=tt, ms_V=ms_V):
                ctx.wait(s_, ms_V)
                ctx.inc(s_.activation(RS1[:, tt:tt + 1], RS1[:, tt:tt + 1], AF.Sqrt), "S")
            rs_S = ctx.tok("S")

            @ctx.section("V")
            def _(v, tt=tt, rs_S=rs_S, tg=ctx.tok("T")):
                ctx.wait(v, rs_S)
                ctx.wait(v, tg)
                ctx.inc(v.scalar_tensor_tensor(XIT[:, (tt % 2) * 512:(tt % 2) * 512 + 512],
                                               ctx.LNJ[:, (tt % 2) * 512:(tt % 2) * 512 + 512],
                                               RS1[:, tt:tt + 1], NW1B[:],
                                               ALU.mult, ALU.mult), "V")
            xi_apply.append(ctx.tok("V"))

            tw = []
            for j in range(4):
                def mk(j=j, tt=tt):
                    def mm(t, i):
                        pb = ctx.P2[:].bitcast(BF16)
                        return t.transpose(pb[:, 0:128],
                                           XIT[:, (tt % 2) * 512 + j * 128:(tt % 2) * 512 + (j + 1) * 128],
                                           ctx.identb[:])

                    def ev(e, i):
                        pb = ctx.P2[:].bitcast(BF16)
                        return ecopy(e, ctx.XF(4 + j)[:, tt * 128:(tt + 1) * 128], pb[:, 0:128])
                    return mm, ev
                tw.append(mk())
            mm_pipeline(ctx, tw, evac="V", nbuf=1, t_prewait=[xi_apply[-1]])
        xif_tok = ctx.tok("V")
        ctx.dump("xif", ctx.XFM[:, 8192:16384], xif_tok)

        es2 = contextlib.ExitStack()
        with es2:
            XROLL = ctx.sb("XROLL", [128, 2 * 2176], BF16, es2)
            QKVT = ctx.sb("QKVT", [128, 2 * 768], BF16, es2)
            PROJT = ctx.sb("PROJT", [128, 2 * 256], BF16, es2)
            BQB = ctx.sb("BQB", [128, 768], F32, es2)
            PBROW = ctx.sb("PBROW", [1, 256], F32, es2)
            PBC = ctx.sb("PBC", [128, 2], F32, es2)
            QKVS = ctx.sb("QKVS", [120, 768], BF16, es2)
            KT = ctx.sb("KTb", [32, 120], BF16, es2)
            QT = ctx.sb("QTb", [32, 120], BF16, es2)
            VA = ctx.sb("VAb", [120, 33], BF16, es2)
            SSb = ctx.sb("SSb", [120, 120], F32, es2)
            PPb = ctx.sb("PPb", [120, 120], BF16, es2)
            BT8 = ctx.sb("BT8", [120, 8 * 120], F32, es2)
            AMb = ctx.sb("AMb", [120, 120], F32, es2)
            MSb = ctx.sb("MSb", [120, 120], F32, es2)
            QMK = ctx.sb("QMK", [120, 120], I32, es2)
            T1b = ctx.sb("T1b", [120, 120], F32, es2)
            T2b = ctx.sb("T2b", [120, 120], F32, es2)
            SLI = ctx.sb("SLI", [1, 120], I32, es2)
            SLF = ctx.sb("SLF", [1, 120], F32, es2)
            SLP = ctx.sb("SLP", [120, 1], F32, es2)
            RPBB = ctx.sb("RPBB", [128, 312], F32, es2)
            OPACK = ctx.sb("OPACK", [120, 256], BF16, es2)
            ONT = ctx.sb("ONT", [128, 2 * 120], BF16, es2)
            RC = ctx.sb("RCb", [120, 1], F32, es2)

            @ctx.section("G")
            def _(g):
                ctx.inc(g.memset(VA[:, 32:33], 1.0), "G")
            va1_tok = ctx.tok("G")

            prev_end = (xif_tok, ctx.tok("V"), ctx.tok("T"))
            for br in range(2):
                w = 10 if br == 0 else 20
                shift = 5 if br == 0 else 10
                Tp = 2050 if br == 0 else 2060
                spp = 12 if br == 0 else 6
                R = 2 * w - 1
                nm = "1" if br == 0 else "2"
                hscale = 32.0 ** -0.5

                @ctx.section("Y")
                def _(sy, nm=nm, R=R, prev_end=prev_end):
                    ctx.wait(sy, prev_end[2])
                    ctx.dma(sy, BQB[:], inp[f"L1_attn_b_qkv{nm}_b"].broadcast_to((128, 768)))
                    ctx.dma(sy, PBROW[:], inp[f"L1_attn_b_proj{nm}_b"])
                    ctx.dma(sy, RPBB[:, 0:R * 8],
                            inp[f"L1_attn_b_rpb{nm}"].rearrange("(o r) h -> o (r h)", o=1)
                            .broadcast_to((128, R * 8)))

                @ctx.section("G")
                def _(g, nm=nm, prev_end=prev_end):
                    ctx.wait(g, prev_end[2])
                    load_w(ctx, g, inp[f"L1_attn_b_qkv{nm}_w"], QKVT)
                    load_w(ctx, g, inp[f"L1_attn_b_proj{nm}_w"], PROJT)
                br_ld = ctx.tok("LD")
                br_lg = ctx.tok("LG")

                pbw = []
                for oc in range(2):
                    def mk(oc=oc):
                        def mm(t, i):
                            return t.transpose(ctx.P2[0:128, 0:1],
                                               PBROW[0:1, oc * 128:(oc + 1) * 128],
                                               ctx.identf[0:1, 0:1])

                        def ev(e, i):
                            return e.copy(PBC[:, oc:oc + 1], ctx.P2[0:128, 0:1])
                        return mm, ev
                    pbw.append(mk())
                pbc_tok = mm_pipeline(ctx, pbw, evac="S", nbuf=1,
                                      t_prewait=[br_ld, prev_end[0]])

                @ctx.section("G")
                def _(g, prev_end=prev_end):
                    ctx.wait(g, prev_end[1])
                    ctx.inc(g.iota(QMK[:], [[1, 120]], base=0, channel_multiplier=0), "G")
                gf_tok = ctx.tok("G")

                @ctx.section("V")
                def _(v, w=w, shift=shift, gf_tok=gf_tok):
                    ctx.wait(v, gf_tok)
                    ctx.f(v, "V", v.tensor_single_scalar(MSb[:], QMK[:], w - shift, ALU.is_lt))
                    ctx.inc(v.tensor_single_scalar(SSb[:], QMK[:], w, ALU.is_lt), "V")
                fcmp_V = ctx.tok("V")

                @ctx.section("G")
                def _(g, fcmp_V=fcmp_V):
                    ctx.wait(g, fcmp_V)
                    ctx.inc(g.iota(QMK[:], [[0, 120]], base=0, channel_multiplier=1), "G")
                gp_tok = ctx.tok("G")

                @ctx.section("V")
                def _(v, w=w, shift=shift, gp_tok=gp_tok):
                    ctx.wait(v, gp_tok)
                    ctx.f(v, "V", v.tensor_single_scalar(T1b[:], QMK[:], w - shift, ALU.is_lt))
                    ctx.f(v, "V", v.tensor_single_scalar(T2b[:], QMK[:], w, ALU.is_lt))
                    ctx.f(v, "V", v.tensor_tensor(AMb[:], T1b[:], MSb[:], ALU.subtract))
                    ctx.f(v, "V", v.tensor_tensor(AMb[:], AMb[:], AMb[:], ALU.mult))
                    ctx.f(v, "V", v.tensor_tensor(AMb[:], AMb[:], SSb[:], ALU.mult))
                    ctx.f(v, "V", v.tensor_tensor(AMb[:], AMb[:], T2b[:], ALU.mult))
                    ctx.inc(v.tensor_scalar(AMb[:], AMb[:], -100.0, None, ALU.mult), "V")
                am_V = ctx.tok("V")

                @ctx.section("G")
                def _(g, spp=spp, w=w, am_V=am_V):
                    ctx.wait(g, am_V)
                    g.iota(QMK[:], [[1, 120]], base=0, channel_multiplier=-1)
                    g.iota(SLI[:], [[1, spp], [0, w]], base=0, channel_multiplier=0)
                    g.wait_ge(ctx.sem["G"], ctx.cnt["G"])
                    ctx.inc(g.tensor_copy(SLF[:], SLI[:]), "G")
                qmk_tok = ctx.tok("G")

                def slb_mm(t, i):
                    return t.matmul(ctx.P2[0:120, 0:120], ctx.onesf[0:1, 0:120],
                                    SLF[0:1, :], start=True, stop=True)

                def slb_ev(e, i):
                    return e.copy(MSb[:], ctx.P2[0:120, 0:120])
                slb_tok = mm_pipeline(ctx, [(slb_mm, slb_ev)], evac="S",
                                      nbuf=1, t_prewait=[qmk_tok])

                def slp_mm(t, i):
                    return t.transpose(ctx.P2[0:120, 0:1], SLF[0:1, :],
                                       ctx.identf[0:1, 0:1])

                def slp_ev(e, i):
                    return e.copy(SLP[:], ctx.P2[0:120, 0:1])
                slp_tok = mm_pipeline(ctx, [(slp_mm, slp_ev)], evac="S",
                                      nbuf=1, t_prewait=[slb_tok])

                @ctx.section("V")
                def _(v, R=R, w=w, slp_tok=slp_tok, br_ld=br_ld):
                    ctx.wait(v, slp_tok)
                    ctx.wait(v, br_ld)
                    ctx.f(v, "V", v.tensor_scalar(MSb[:], MSb[:], SLP[:, 0:1], None, ALU.is_equal))
                    for h in range(8):
                        ctx.f(v, "V", v.tensor_scalar(BT8[:, h * 120:(h + 1) * 120], MSb[:],
                                                      60000.0, -60000.0, ALU.mult, ALU.add))
                    for r in range(R):
                        ctx.f(v, "V", v.tensor_single_scalar(MSb[:], QMK[:], r - w + 1,
                                                             ALU.is_equal))
                        for h in range(8):
                            ctx.f(v, "V", v.scalar_tensor_tensor(
                                BT8[:, h * 120:(h + 1) * 120], MSb[:],
                                RPBB[0:120, r * 8 + h:r * 8 + h + 1],
                                BT8[:, h * 120:(h + 1) * 120], ALU.mult, ALU.add))
                btd_tok = ctx.tok("V")
                if br == 0:
                    ctx.dump("bt8", BT8[:], btd_tok)
                    ctx.dump("am", AMb[:], btd_tok)

                @ctx.section("V")
                def _(v, br=br, shift=shift, Tp=Tp, prev_end=prev_end):
                    ctx.wait(v, xif_tok)
                    ctx.wait(v, prev_end[2])
                    for kt in range(2):
                        src = ctx.XF(4 + br * 2 + kt)
                        dst = XROLL[:, kt * 2176:(kt + 1) * 2176]
                        n0 = 2048 - shift
                        ctx.f(v, "V", v.tensor_copy(dst[:, 0:n0], src[:, shift:2048]))
                        ctx.f(v, "V", v.memset(dst[:, n0:Tp - shift], 0.0))
                        ctx.f(v, "V", v.tensor_copy(dst[:, Tp - shift:Tp], src[:, 0:shift]))
                        ctx.inc(v.memset(dst[:, Tp:2176], 0.0), "V")
                xr_tok = ctx.tok("V")
                if br == 0:
                    ctx.dump("xroll", XROLL[:], xr_tok)

                for p in range(18):
                    pieces = _out_pieces(p, shift, Tp)
                    is_last = (p == 17)

                    @ctx.section("T")
                    def _(t, p=p, xr_tok=xr_tok, br_ld=br_ld, prev_end=prev_end,
                          pg=ctx.guard_tokens()):
                        if p == 0:
                            ctx.wait(t, xr_tok)
                            ctx.wait(t, br_ld)
                            ctx.wait(t, br_lg)
                            ctx.wait(t, prev_end[0])
                            for tokk in pg:
                                ctx.wait(t, tokk)
                        else:
                            ctx.wait(t, pg[-1] if False else ("V", ctx.cnt["V"]))
                        for cc in range(2):
                            csl = slice(cc * 512, 512 + cc * 256)
                            for kt in range(2):
                                r = t.matmul(ctx.PO[0:120, cc * 512:512 + cc * 256],
                                             XROLL[:, kt * 2176 + p * 120:kt * 2176 + (p + 1) * 120],
                                             QKVT[:, kt * 768 + cc * 512:kt * 768 + 512 + cc * 256],
                                             start=(kt == 0), stop=(kt == 1))
                            ctx.inc(r, "T")
                    qkv_T = ctx.tok("T")

                    @ctx.section("V")
                    def _(v, qkv_T=qkv_T):
                        ctx.wait(v, qkv_T)
                        ctx.inc(v.tensor_tensor(QKVS[:], ctx.PO[0:120, 0:768],
                                                BQB[0:120, :], ALU.add), "V")
                    qkvs_V = ctx.tok("V")

                    tb = ctx.cnt["T"]
                    vb = ctx.cnt["V"]
                    sbp = ctx.cnt["S"]
                    ssn = 2 if is_last else 1
                    vper = 2 + ssn + 1 + 2
                    ctx.cnt["T"] += 32
                    ctx.cnt["V"] += 8 * vper
                    ctx.cnt["S"] += 8

                    @ctx.section("T")
                    def _(t, tb=tb, vb=vb, sbp=sbp, vper=vper, qkvs_V=qkvs_V,
                          pg=ctx.guard_tokens()):
                        ctx.wait(t, qkvs_V)
                        for tokk in pg:
                            ctx.wait(t, tokk)
                        for h in range(8):
                            if h > 0:
                                t.wait_ge(ctx.sem["V"], vb + vper * (h - 1) + 2)
                            t.transpose(ctx.PB[0:32, 0:120],
                                        QKVS[0:120, h * 32:(h + 1) * 32],
                                        ctx.identb[0:120, 0:120])\
                                .then_inc(ctx.sem["T"], 1)
                            t.transpose(ctx.PB[0:32, 128:248],
                                        QKVS[0:120, 256 + h * 32:256 + (h + 1) * 32],
                                        ctx.identb[0:120, 0:120])\
                                .then_inc(ctx.sem["T"], 1)
                            t.wait_ge(ctx.sem["V"], vb + vper * h + 2)
                            t.matmul(ctx.P0[0:120, 0:120], KT[:], QT[:],
                                     start=True, stop=True).then_inc(ctx.sem["T"], 1)
                            t.wait_ge(ctx.sem["S"], sbp + h + 1)
                            if h > 0:
                                t.wait_ge(ctx.sem["V"], vb + vper * h)
                            t.matmul(ctx.P1[0:120, 0:33], PPb[:], VA[:],
                                     start=True, stop=True).then_inc(ctx.sem["T"], 1)

                    @ctx.section("V")
                    def _(v, tb=tb, vb=vb, sbp=sbp, vper=vper, is_last=is_last,
                          hscale=hscale):
                        for h in range(8):
                            v.wait_ge(ctx.sem["T"], tb + 4 * h + 2)
                            v.tensor_scalar(QT[:], ctx.PB[0:32, 0:120], hscale,
                                            None, ALU.mult).then_inc(ctx.sem["V"], 1)
                            v.tensor_copy(KT[:], ctx.PB[0:32, 128:248])\
                                .then_inc(ctx.sem["V"], 1)
                            v.wait_ge(ctx.sem["T"], tb + 4 * h + 3)
                            if h == 0:
                                ctx.wait(v, btd_tok)
                                v.wait_ge(ctx.sem["T"], tb)
                                v.wait_ge(ctx.sem["S"], sbp)
                            else:
                                v.wait_ge(ctx.sem["S"], sbp + h)
                                v.wait_ge(ctx.sem["T"], tb + 4 * (h - 1) + 4)
                            if is_last:
                                v.tensor_tensor(SSb[:], ctx.P0[0:120, 0:120],
                                                BT8[:, h * 120:(h + 1) * 120], ALU.add)\
                                    .then_inc(ctx.sem["V"], 1)
                                v.wait_ge(ctx.sem["V"], vb + vper * h + 3)
                                v.tensor_tensor(SSb[:], SSb[:], AMb[:], ALU.add)\
                                    .then_inc(ctx.sem["V"], 1)
                            else:
                                v.tensor_tensor(SSb[:], ctx.P0[0:120, 0:120],
                                                BT8[:, h * 120:(h + 1) * 120], ALU.add)\
                                    .then_inc(ctx.sem["V"], 1)
                            v.tensor_copy(VA[0:120, 0:32],
                                          QKVS[0:120, 512 + h * 32:512 + (h + 1) * 32])\
                                .then_inc(ctx.sem["V"], 1)
                            v.wait_ge(ctx.sem["T"], tb + 4 * h + 4)
                            v.reciprocal(RC[:], ctx.P1[0:120, 32:33])\
                                .then_inc(ctx.sem["V"], 1)
                            v.wait_ge(ctx.sem["V"], vb + vper * h + 2 + ssn + 2)
                            v.tensor_scalar(OPACK[:, h * 32:(h + 1) * 32],
                                            ctx.P1[0:120, 0:32], RC[:, 0:1],
                                            None, ALU.mult).then_inc(ctx.sem["V"], 1)

                    @ctx.section("S")
                    def _(s_, tb=tb, vb=vb, sbp=sbp, vper=vper, ssn=ssn):
                        for h in range(8):
                            s_.wait_ge(ctx.sem["V"], vb + vper * h + 2 + ssn + 1)
                            if h > 0:
                                s_.wait_ge(ctx.sem["T"], tb + 4 * (h - 1) + 4)
                            else:
                                s_.wait_ge(ctx.sem["T"], tb)
                            s_.activation(PPb[:], SSb[:], AF.Exp)\
                                .then_inc(ctx.sem["S"], 1)

                    @ctx.section("T")
                    def _(t, rc_V=("V", vb + 8 * vper), kte=("V", vb + 7 * vper + 2)):
                        ctx.wait(t, rc_V)
                        ctx.wait(t, kte)
                        pbt = ctx.PB
                        t.transpose(pbt[0:128, 256:376], OPACK[:, 0:128],
                                    ctx.identb[0:120, 0:120]).then_inc(ctx.sem["T"], 1)
                        ctx.cnt["T"] += 1
                        ctx.inc(t.transpose(pbt[0:128, 376:496], OPACK[:, 128:256],
                                            ctx.identb[0:120, 0:120]), "T")
                    opt_T = ctx.tok("T")

                    @ctx.section("V")
                    def _(v, opt_T=opt_T):
                        ctx.wait(v, opt_T)
                        v.tensor_copy(ONT[:, 0:120], ctx.PB[0:128, 256:376])\
                            .then_inc(ctx.sem["V"], 1)
                        ctx.cnt["V"] += 1
                        ctx.inc(v.tensor_copy(ONT[:, 120:240], ctx.PB[0:128, 376:496]), "V")
                    ont_V = ctx.tok("V")

                    @ctx.section("T")
                    def _(t, ont_V=ont_V, pbc_tok=pbc_tok, vg2=ctx.tok("V")):
                        ctx.wait(t, ont_V)
                        ctx.wait(t, pbc_tok)
                        ctx.wait(t, vg2)
                        for oc in range(2):
                            for kt in range(2):
                                r = t.matmul(ctx.P2[0:128, oc * 120:(oc + 1) * 120],
                                             PROJT[:, kt * 256 + oc * 128:kt * 256 + (oc + 1) * 128],
                                             ONT[:, kt * 120:(kt + 1) * 120],
                                             start=(kt == 0), stop=(kt == 1))
                            ctx.inc(r, "T")
                    pj_T = ctx.tok("T")

                    @ctx.section("V")
                    def _(v, p=p, br=br, pj_T=pj_T, pieces=pieces):
                        ctx.wait(v, pj_T)
                        n = 0
                        for oc in range(2):
                            for (j0, j1, t0) in pieces:
                                r = v.tensor_scalar(
                                    ctx.XF(br * 2 + oc)[:, t0:t0 + (j1 - j0)],
                                    ctx.P2[0:128, oc * 120 + j0:oc * 120 + j1],
                                    PBC[:, oc:oc + 1], None, ALU.add)
                                r.then_inc(ctx.sem["V"], 1)
                                n += 1
                        ctx.cnt["V"] += n
                    of_S = ctx.tok("V")
                    ctx.psum_guard["S"] = ctx.cnt["S"]
                    ctx.psum_guard["V"] = ctx.cnt["V"]

                prev_end = (of_S, ctx.tok("V"), ctx.tok("T"))
                if br == 0:
                    ctx.dump("qkvs17", QKVS[:], prev_end[1])
            branches_done = prev_end
            ctx.dump("of", ctx.XFM[:, 0:8192], branches_done[0])

        up_rounds = []
        for tt in range(NT):
            for oc in range(2):
                def mk(tt=tt, oc=oc):
                    def mm(t, i):
                        pb = ctx.P0 if i % 2 == 0 else ctx.P1
                        for kt in range(4):
                            r = t.matmul(pb[:], ctx.XF(kt)[:, tt * 128:(tt + 1) * 128],
                                         UPT[:, kt * 1024 + oc * 512:kt * 1024 + oc * 512 + 512],
                                         start=(kt == 0), stop=(kt == 3))
                        return r

                    def ev(e, i):
                        pb = ctx.P0 if i % 2 == 0 else ctx.P1
                        t1 = ctx.LNJ[:, 1024 + (i % 2) * 512:1024 + (i % 2) * 512 + 512]
                        xsl = ctx.X(tt)[:, oc * 512:(oc + 1) * 512]
                        lsl = ctx.XL(tt)[:, oc * 512:(oc + 1) * 512]
                        e.tensor_tensor(t1, pb[:, 0:512], UBB[:, oc * 512:(oc + 1) * 512], ALU.add)
                        e.tensor_tensor(xsl, xsl, t1, ALU.add)
                        e.tensor_tensor(xsl, xsl, lsl, ALU.add)
                        return e.tensor_tensor(lsl, lsl, t1, ALU.add)
                    return mm, ev
                up_rounds.append(mk())
        up_tok = mm_pipeline(ctx, up_rounds, evac="V",
                             t_prewait=[branches_done[0], dwn_ld, dwn_lg])
        ctx.dump("xp", ctx.XLNB[:], up_tok)
        ctx.dump("xres_up", ctx.XRES[:], up_tok)

        RS2 = ctx.LNnegb
        h_apply = []
        for tt in range(NT):
            @ctx.section("V")
            def _(v, tt=tt, up_tok=up_tok):
                if tt == 0:
                    ctx.wait(v, up_tok)
                sq = ctx.LNJ[:, (tt % 2) * 1024:(tt % 2) * 1024 + 1024]
                ctx.f(v, "V", v.tensor_tensor(sq, ctx.XL(tt), ctx.XL(tt), ALU.mult))
                ctx.f(v, "V", v.tensor_reduce(ctx.LNsq[:, tt:tt + 1], sq, AX.X, ALU.add))
                ctx.f(v, "V", v.tensor_scalar(RS2[:, tt:tt + 1], ctx.LNsq[:, tt:tt + 1],
                                              1.0 / 1024.0, 1e-6, ALU.mult, ALU.add))
                ctx.inc(v.reciprocal(RS2[:, tt:tt + 1], RS2[:, tt:tt + 1]), "V")
            ms2_V = ctx.tok("V")

            @ctx.section("S")
            def _(s_, tt=tt, ms2_V=ms2_V):
                ctx.wait(s_, ms2_V)
                ctx.inc(s_.activation(RS2[:, tt:tt + 1], RS2[:, tt:tt + 1], AF.Sqrt), "S")
            rs2_S = ctx.tok("S")

            @ctx.section("V")
            def _(v, tt=tt, rs2_S=rs2_S, tg=ctx.tok("T")):
                ctx.wait(v, rs2_S)
                ctx.wait(v, tg)
                ctx.inc(v.scalar_tensor_tensor(XIT[:, (tt % 2) * 1024:(tt % 2) * 1024 + 1024],
                                               ctx.XL(tt), RS2[:, tt:tt + 1], NW2B[:],
                                               ALU.mult, ALU.mult), "V")
            h_apply.append(ctx.tok("V"))

            tw = []
            for j in range(ND):
                def mk(j=j, tt=tt):
                    def mm(t, i):
                        pb = (ctx.P0 if i % 2 == 0 else ctx.P1)[:].bitcast(BF16)
                        return t.transpose(pb[:, 0:128],
                                           XIT[:, (tt % 2) * 1024 + j * 128:(tt % 2) * 1024 + (j + 1) * 128],
                                           ctx.identb[:])

                    def ev(e, i):
                        pb = (ctx.P0 if i % 2 == 0 else ctx.P1)[:].bitcast(BF16)
                        return ecopy(e, ctx.XF(j)[:, tt * 128:(tt + 1) * 128], pb[:, 0:128])
                    return mm, ev
                tw.append(mk())
            mm_pipeline(ctx, tw, evac="V", t_prewait=[h_apply[-1]])
        hf_tok = ctx.tok("V")
        ctx.dump("hf", ctx.XFM[:], hf_tok)
        ctx.dump("rs2", RS2[:], hf_tok)
        ctx.dump("msq2", ctx.LNsq[:], hf_tok)

        W1T = ctx.XLNB[:, 0:16384]
        W2Q = ctx.sb("W2Q", [128, 2 * 4 * 1024], BF16, es)
        B1ROW = ctx.sb("B1ROW", [1, 2048], F32, es)
        B1C = ctx.sb("B1C", [128, 16], F32, es)
        B2BC = ctx.sb("B2BC", [128, 1024], F32, es)
        G1 = ctx.sb("G1", [128, 256], BF16, es)

        @ctx.section("Y")
        def _(sy):
            ctx.wait(sy, branches_done[0])
            ctx.wait(sy, branches_done[1])
            ctx.wait(sy, branches_done[2])
            ctx.dma(sy, B1ROW[:], inp["L1_attn_b_ffn_b1"])
            ctx.dma(sy, B2BC[:], inp["L1_attn_b_ffn_b2"].broadcast_to((128, D)))

        @ctx.section("G")
        def _(g, hf_tok=hf_tok, vg=ctx.tok("V")):
            ctx.wait(g, hf_tok)
            ctx.wait(g, vg)
            load_w(ctx, g, inp["L1_attn_b_ffn_w1"], W1T)
            load_w(ctx, g, inp["L1_attn_b_ffn_w2"][0:512, :], W2Q[:, 0:4096])
        w1_ld = ctx.tok("LD")
        w1_lg = ctx.tok("LG")

        b1w = []
        for gt in range(16):
            def mk(gt=gt):
                def mm(t, i):
                    return t.transpose(ctx.P2[0:128, 0:1],
                                       B1ROW[0:1, gt * 128:(gt + 1) * 128],
                                       ctx.identf[0:1, 0:1])

                def ev(e, i):
                    return e.copy(B1C[:, gt:gt + 1], ctx.P2[0:128, 0:1])
                return mm, ev
            b1w.append(mk())
        b1c_tok = mm_pipeline(ctx, b1w, evac="S", nbuf=1, t_prewait=[w1_ld])

        @ctx.section("V")
        def _(v, w1_ld=w1_ld):
            ctx.wait(v, w1_ld)
            for tt in range(NT):
                for oc in range(2):
                    xsl = ctx.X(tt)[:, oc * 512:(oc + 1) * 512]
                    v.tensor_tensor(xsl, xsl, B2BC[:, oc * 512:(oc + 1) * 512], ALU.add)\
                        .then_inc(ctx.sem["V"], 1)
                    ctx.cnt["V"] += 1
        b2_V = ctx.tok("V")

        for q in range(4):
            if q < 3:
                @ctx.section("G")
                def _(g, q=q, tg=ctx.tok("T")):
                    ctx.wait(g, tg)
                    load_w(ctx, g, inp["L1_attn_b_ffn_w2"][(q + 1) * 512:(q + 2) * 512, :],
                           W2Q[:, ((q + 1) % 2) * 4096:((q + 1) % 2) * 4096 + 4096])
                w2_lds = ctx.tok("LG")
            for c in range(8):
                tb = ctx.cnt["T"]
                sb = ctx.cnt["S"]
                vb = ctx.cnt["V"]
                guards = ctx.guard_tokens()
                ctx.cnt["T"] += 8
                ctx.cnt["S"] += 4
                ctx.cnt["V"] += 4
                ctx.psum_guard["V"] = ctx.cnt["V"]

                @ctx.section("T")
                def _(t, q=q, c=c, tb=tb, sb=sb, vb=vb, guards=guards):
                    if c == 0:
                        ctx.wait(t, w1_lg if q == 0 else ("LG", ctx.cnt["LG"]))
                        ctx.wait(t, b1c_tok)
                        ctx.wait(t, hf_tok)
                        ctx.wait(t, b2_V)
                        for tokk in guards:
                            ctx.wait(t, tokk)
                    for gtq in range(4):
                        gt = q * 4 + gtq
                        pb = ctx.P0 if gtq % 2 == 0 else ctx.P1
                        if gtq >= 2:
                            t.wait_ge(ctx.sem["S"], sb + gtq - 1)
                        for d in range(ND):
                            r = t.matmul(pb[0:128, 0:256],
                                         W1T[:, d * 2048 + gt * 128:d * 2048 + (gt + 1) * 128],
                                         ctx.XF(d)[:, c * 256:(c + 1) * 256],
                                         start=(d == 0), stop=(d == ND - 1))
                        r.then_inc(ctx.sem["T"], 1)
                        t.wait_ge(ctx.sem["S"], sb + gtq + 1)
                        if gtq == 0:
                            t.wait_ge(ctx.sem["V"], vb)
                        for sg in range(2):
                            for oc2 in range(2):
                                rr = t.matmul(ctx.PO[0:128, sg * 1024 + oc2 * 512:sg * 1024 + oc2 * 512 + 512],
                                              G1[:, sg * 128:(sg + 1) * 128],
                                              W2Q[:, (q % 2) * 4096 + gtq * 1024 + oc2 * 512:(q % 2) * 4096 + gtq * 1024 + oc2 * 512 + 512],
                                              start=(gtq == 0), stop=(gtq == 3))
                        rr.then_inc(ctx.sem["T"], 1)

                @ctx.section("S")
                def _(s_, q=q, c=c, tb=tb, sb=sb):
                    for gtq in range(4):
                        gt = q * 4 + gtq
                        pb = ctx.P0 if gtq % 2 == 0 else ctx.P1
                        s_.wait_ge(ctx.sem["T"], tb + 2 * gtq + 1)
                        s_.activation(G1[:], pb[0:128, 0:256], AF.Gelu,
                                      bias=B1C[:, gt:gt + 1]).then_inc(ctx.sem["S"], 1)

                @ctx.section("V")
                def _(v, q=q, c=c, tb=tb, vb=vb):
                    v.wait_ge(ctx.sem["T"], tb + 8)
                    for sg in range(2):
                        for oc2 in range(2):
                            xsl = ctx.X(2 * c + sg)[:, oc2 * 512:(oc2 + 1) * 512]
                            v.tensor_tensor(xsl, xsl,
                                            ctx.PO[0:128, sg * 1024 + oc2 * 512:sg * 1024 + oc2 * 512 + 512],
                                            ALU.add).then_inc(ctx.sem["V"], 1)
        ctx.x_tok = ctx.tok("V")
        ctx.dump("b2bc", B2BC[:], ctx.x_tok)
        ctx.dump("ubb", UBB[:], ctx.x_tok)
        ctx.dump("b1c", B1C[:], ctx.x_tok)
        ctx.dump("dbb", DBB[:], ctx.x_tok)


# version 13
# speedup vs baseline: 1.1925x; 1.0013x over previous
import math
import contextlib
import numpy as np
import ml_dtypes
import concourse.bass as bass
import concourse.mybir as mybir
from concourse.bass_utils import run_bass_kernel_spmd

F32 = mybir.dt.float32
BF16 = mybir.dt.bfloat16
I32 = mybir.dt.int32
U8 = mybir.dt.uint8
AF = mybir.ActivationFunctionType
ALU = mybir.AluOpType
AX = mybir.AxisListType

T = 2048
D = 1024
NT = 16
ND = 8
PI = math.pi

INPUT_SPECS = {
    "L0_attn_a_x": ((T, D), F32),
    "L0_attn_a_wq": ((D, D), F32), "L0_attn_a_wk": ((D, D), F32),
    "L0_attn_a_wv": ((D, D), F32), "L0_attn_a_wo": ((D, D), F32),
    "L0_attn_a_w_phase": ((D, 2), F32),
    "L0_attn_a_carrier": ((1, 16), F32), "L0_attn_a_lock": ((1, 16), F32),
    "L0_attn_a_bandwidth": ((1, 16), F32), "L0_attn_a_active_mask": ((1, 16), U8),
    "L0_attn_b_norm_weight": ((1, D), F32),
    "L0_attn_b_w_q": ((D, D), F32), "L0_attn_b_w_k": ((D, D), F32),
    "L0_attn_b_w_v": ((D, D), F32), "L0_attn_b_w_f": ((D, D), F32),
    "L0_attn_b_gnorm_weight": ((1, D), F32), "L0_attn_b_w_o": ((D, D), F32),
    "L0_ffn_weight": ((D, D), F32), "L0_ffn_bias": ((1, D), F32),
    "L1_attn_a_W_in": ((D, 2080), F32), "L1_attn_a_w_conv": ((4, 2080), F32),
    "L1_attn_a_b_conv": ((1, 2080), F32), "L1_attn_a_theta": ((32, 128), F32),
    "L1_attn_a_decay_slopes": ((1, 28), F32), "L1_attn_a_anchor_slopes": ((1, 4), F32),
    "L1_attn_a_score_scale": ((1, 32), F32), "L1_attn_a_deriv_logits": ((1, 3), F32),
    "L1_attn_a_norm_scale": ((1, 256), F32),
    "L1_attn_a_W_re": ((128, 32), F32), "L1_attn_a_W_im": ((128, 32), F32),
    "L1_attn_a_W_out": ((D, D), F32),
    "L1_attn_b_down_w": ((D, 512), F32), "L1_attn_b_down_b": ((1, 512), F32),
    "L1_attn_b_up_w": ((512, D), F32), "L1_attn_b_up_b": ((1, D), F32),
    "L1_attn_b_norm1_w": ((1, 512), F32), "L1_attn_b_norm2_w": ((1, D), F32),
    "L1_attn_b_qkv1_w": ((256, 768), F32), "L1_attn_b_qkv1_b": ((1, 768), F32),
    "L1_attn_b_proj1_w": ((256, 256), F32), "L1_attn_b_proj1_b": ((1, 256), F32),
    "L1_attn_b_rpb1": ((19, 8), F32),
    "L1_attn_b_qkv2_w": ((256, 768), F32), "L1_attn_b_qkv2_b": ((1, 768), F32),
    "L1_attn_b_proj2_w": ((256, 256), F32), "L1_attn_b_proj2_b": ((1, 256), F32),
    "L1_attn_b_rpb2": ((39, 8), F32),
    "L1_attn_b_ffn_w1": ((D, 2048), F32), "L1_attn_b_ffn_b1": ((1, 2048), F32),
    "L1_attn_b_ffn_w2": ((2048, D), F32), "L1_attn_b_ffn_b2": ((1, D), F32),
    "L1_ffn_weight": ((D, D), F32), "L1_ffn_bias": ((1, D), F32),
}

ENG_SECTION = {"S": "scalar", "V": "vector", "G": "gpsimd", "T": "tensor", "Y": "sync"}


class Ctx:
    def __init__(self, nc, es, blk):
        self.nc = nc
        self.es = es
        self.blk = blk
        self.cnt = {}
        self.sem = {}
        self.uid = 0

    def barrier(self):
        g = self.start_guard()
        for eng in ("V", "S", "T", "G"):
            sec = self.section(eng)

            @sec
            def _(e, g=g, eng=eng):
                for tok in g:
                    if tok[0] != eng:
                        self.wait(e, tok)

    def start_guard(self):
        return [("T", self.cnt["T"]), ("S", self.cnt["S"]), ("V", self.cnt["V"]),
                ("G", self.cnt["G"])]

    def guard_tokens(self):
        return [(e, c) for e, c in self.psum_guard.items()]

    def make_sems(self):
        self.psum_guard = {}
        for name in ("LD", "LG", "ST", "V", "S", "T", "G", "CC"):
            self.sem[name] = self.es.enter_context(self.nc.semaphore(name))
            self.cnt[name] = 0

    def sb(self, name, shape, dtype, scope=None):
        self.uid += 1
        return (scope or self.es).enter_context(
            self.nc.sbuf_tensor(f"{name}{self.uid}", shape, dtype))

    def f(self, eng_obj, eng, instr):
        tok = self.inc(instr, eng)
        eng_obj.wait_ge(self.sem[eng], tok[1])
        return tok

    def inc(self, instr, eng, n=1):
        self.cnt[eng] += n
        instr.then_inc(self.sem[eng], n)
        return (eng, self.cnt[eng])

    def tok(self, eng):
        return (eng, self.cnt[eng])

    def wait(self, engine, token):
        if token is not None:
            engine.wait_ge(self.sem[token[0]], token[1])

    def section(self, eng):
        return getattr(self.blk, ENG_SECTION[eng])

    def X(self, tt):
        return self.XRES[:, tt * D:(tt + 1) * D]

    def XL(self, tt):
        return self.XLNB[:, tt * D:(tt + 1) * D]

    def XF(self, d):
        return self.XFM[:, d * T:(d + 1) * T]

    def ON(self, d):
        return self.XLNB[:, d * T:(d + 1) * T]

    def ccol(self, v):
        assert v in self.cvals, f"const {v} not pre-registered"
        return self.CPOOL[:, self.cvals[v]:self.cvals[v] + 1]

    def dump(self, name, ap, wait_tok, hold=None):
        if not getattr(self, "debug", False):
            return
        if getattr(self, "dump_only", None) is not None and name not in self.dump_only:
            return
        self.uid += 1
        t = self.nc.dram_tensor(f"d_{name}_{self.uid}", list(ap.shape), ap.dtype,
                                kind="ExternalOutput").ap()

        @self.section("Y")
        def _(sy):
            self.wait(sy, wait_tok)
            sy.dma_start(t, ap).then_inc(self.sem["ST"], 16)
            self.cnt["ST"] += 16
        if hold:
            st_tok = ("ST", self.cnt["ST"])
            for hh in ([hold] if isinstance(hold, str) else hold):
                @self.section(hh)
                def _(e, hh=hh):
                    self.wait(e, st_tok)

    def dma(self, eng_obj, dst, src):
        nm = "LG" if str(getattr(eng_obj, "engine", "")) == "EngineType.Pool" else "LD"
        eng_obj.dma_start(dst, src).then_inc(self.sem[nm], 16)
        self.cnt[nm] += 16
        return (nm, self.cnt[nm])


def ecopy(e, dst, src):
    return e.copy(dst, src) if hasattr(e, "copy") else e.tensor_copy(dst, src)


def mm_pipeline(ctx, work, evac="S", nbuf=2, t_prewait=None, e_prewait=None):
    n = len(work)
    tbase = ctx.cnt["T"]
    ebase = ctx.cnt[evac]
    guards = ctx.guard_tokens()
    ctx.cnt["T"] = tbase + n
    ctx.cnt[evac] = ebase + n
    ctx.psum_guard[evac] = ebase + n

    @ctx.section("T")
    def _(t):
        for tok in (t_prewait or []):
            ctx.wait(t, tok)
        for tok in guards:
            ctx.wait(t, tok)
        for i, (emit_mms, _e) in enumerate(work):
            if i >= nbuf:
                t.wait_ge(ctx.sem[evac], ebase + i - nbuf + 1)
            emit_mms(t, i).then_inc(ctx.sem["T"], 1)

    @ctx.section(evac)
    def _(e):
        for tok in (e_prewait or []):
            ctx.wait(e, tok)
        for i, (_m, emit_evac) in enumerate(work):
            e.wait_ge(ctx.sem["T"], tbase + i + 1)
            emit_evac(e, i).then_inc(ctx.sem[evac], 1)

    return (evac, ctx.cnt[evac])


def emit_ln(ctx, wait_tok=None, rms=False, eps=1e-5, out_scale_col=None,
            gamma_bcast=None, apply_out=None, apply=True):
    blk = ctx.blk
    ssum, ssq, rstd, negb = ctx.LNsum, ctx.LNsq, ctx.LNrstd, ctx.LNnegb

    @ctx.section("V")
    def _(v):
        ctx.wait(v, wait_tok)
        if rms:
            ctx.f(v, "V", v.memset(ssum[:], 0.0))
        else:
            for tt in range(NT):
                ctx.inc(v.tensor_reduce(ssum[:, tt:tt + 1], ctx.X(tt), AX.X, ALU.add), "V")
            v.wait_ge(ctx.sem["V"], ctx.cnt["V"])
        ctx.inc(v.tensor_scalar(ssum[:], ssum[:], 1.0 / D, None, ALU.mult), "V")
    vt = ctx.tok("V")

    s0 = ctx.cnt["S"]
    v0 = ctx.cnt["V"]

    @ctx.section("S")
    def _(s):
        ctx.wait(s, wait_tok)
        for tt in range(NT):
            if tt >= 2:
                s.wait_ge(ctx.sem["V"], v0 + tt)
            s.activation(ctx.LNJ[:, (tt % 2) * 1024:(tt % 2) * 1024 + D],
                         ctx.X(tt), AF.Square).then_inc(ctx.sem["S"], 1)
    ctx.cnt["S"] = s0 + NT

    @ctx.section("V")
    def _(v):
        ctx.wait(v, vt)
        for tt in range(NT):
            v.wait_ge(ctx.sem["S"], s0 + tt + 1)
            v.tensor_reduce(ssq[:, tt:tt + 1],
                            ctx.LNJ[:, (tt % 2) * 1024:(tt % 2) * 1024 + D],
                            AX.X, ALU.add).then_inc(ctx.sem["V"], 1)
    ctx.cnt["V"] = v0 + NT + 1 - 1
    st = ctx.tok("V")

    @ctx.section("V")
    def _(v):
        ctx.wait(v, st)
        v.wait_ge(ctx.sem["V"], ctx.cnt["V"])
        ctx.f(v, "V", v.tensor_scalar(ssq[:], ssq[:], 1.0 / D, None, ALU.mult))
        ctx.f(v, "V", v.tensor_tensor(negb[:], ssum[:], ssum[:], ALU.mult))
        ctx.inc(v.tensor_tensor(ssq[:], ssq[:], negb[:], ALU.subtract), "V")
    vt2 = ctx.tok("V")

    @ctx.section("S")
    def _(s):
        ctx.wait(s, vt2)
        ctx.f(s, "S", s.activation(rstd[:], ssq[:], AF.Ln, bias=ctx.ccol(eps)))
        ctx.inc(s.activation(rstd[:], rstd[:], AF.Exp, scale=-0.5), "S")
    st2 = ctx.tok("S")

    @ctx.section("V")
    def _(v):
        ctx.wait(v, st2)
        if out_scale_col is not None:
            ctx.f(v, "V", v.tensor_scalar(rstd[:], rstd[:], out_scale_col, None, ALU.mult))
        ctx.inc(v.scalar_tensor_tensor(negb[:], ssum[:], -1.0, rstd[:],
                                       ALU.mult, ALU.mult), "V")
    vt3 = ctx.tok("V")

    done = vt3
    if apply:
        dst = apply_out or ctx.XL
        if gamma_bcast is None:
            @ctx.section("S")
            def _(s):
                ctx.wait(s, vt3)
                for tt in range(NT):
                    ctx.inc(s.activation(dst(tt), ctx.X(tt), AF.Identity,
                                         bias=negb[:, tt:tt + 1],
                                         scale=rstd[:, tt:tt + 1]), "S")
            done = ctx.tok("S")
        else:
            @ctx.section("V")
            def _(v):
                ctx.wait(v, vt3)
                for tt in range(NT):
                    ctx.inc(v.scalar_tensor_tensor(dst(tt), ctx.X(tt),
                                                   rstd[:, tt:tt + 1], gamma_bcast,
                                                   ALU.mult, ALU.mult), "V")
            done = ctx.tok("V")
    return rstd, negb, ssum, done


def emit_transpose_to_xfm(ctx, wait_tok, src=None, dst=None, nd=ND, nt=NT):
    src = src or ctx.XL
    dst = dst or ctx.XF
    work = []
    for tt in range(nt):
        for d in range(nd):
            def mk(tt=tt, d=d):
                def mm(t, i):
                    pb = (ctx.P0 if i % 2 == 0 else ctx.P1)[:].bitcast(BF16)
                    return t.transpose(pb[:, 0:128],
                                       src(tt)[:, d * 128:(d + 1) * 128], ctx.identb[:])

                def ev(e, i):
                    pb = (ctx.P0 if i % 2 == 0 else ctx.P1)[:].bitcast(BF16)
                    return e.copy(dst(d)[:, tt * 128:(tt + 1) * 128], pb[:, 0:128])
                return mm, ev
            work.append(mk())
    return mm_pipeline(ctx, work, evac="S", nbuf=2,
                       t_prewait=[wait_tok, ctx.consts_tok])


def load_w(ctx, eng_obj, src_ap, dst, cols=None):
    if cols is not None:
        src_ap = src_ap[:, cols[0]:cols[1]]
    R, C = src_ap.shape
    nr = R // 128
    return ctx.dma(eng_obj, dst[:, 0:nr * C].rearrange("p (a c) -> p a c", a=nr),
                   src_ap.rearrange("(a p) c -> p a c", p=128))


def emit_l0_attn_a(ctx, inp):
    H = 16
    es = contextlib.ExitStack()
    with es:
        _, _, _, lntok = emit_ln(ctx, wait_tok=ctx.x_tok)
        xfm_tok = emit_transpose_to_xfm(ctx, lntok)
        ctx.dump("xln", ctx.XLNB[:], lntok)
        ctx.dump("xfm", ctx.XFM[:], xfm_tok)

        wbuf = ctx.sb("waa", [128, 2 * 4096], BF16, es)
        wph = ctx.sb("wph", [128, 16], BF16, es)
        wphf = ctx.sb("wphf", [128, 16], F32, es)
        rows = ctx.sb("aarow", [1, 96], F32, es)
        am8 = ctx.sb("am8", [1, 16], U8, es)
        CARR = ctx.sb("CARR", [128, 80], F32, es)
        AFR = ctx.sb("afr", [1, 16], F32, es)
        LG = ctx.sb("LG", [128, 16 * NT], F32, es)
        PH0 = ctx.sb("PH0", [128, NT], F32, es)
        PH1 = ctx.sb("PH1", [128, NT], F32, es)
        RINV = ctx.sb("RINV", [128, NT], F32, es)
        gt1 = ctx.sb("gt1", [128, NT], F32, es)
        gt2 = ctx.sb("gt2", [128, NT], F32, es)
        qb = ctx.sb("qb", [128, 2 * T], BF16, es)
        kb = ctx.sb("kb", [128, 2 * T], BF16, es)
        vaug = ctx.sb("vaug", [128, NT * 4 * 65], BF16, es)
        expb = ctx.sb("expb", [128, 1024], BF16, es)
        bcsb = ctx.sb("bcsb", [128, 512], F32, es)
        rnrow = ctx.LNJ[0:1, 0:T]

        @ctx.section("Y")
        def _(sy):
            ctx.dma(sy, rows[:, 0:16], inp["L0_attn_a_carrier"])
            ctx.dma(sy, rows[:, 16:32], inp["L0_attn_a_lock"])
            ctx.dma(sy, rows[:, 32:48], inp["L0_attn_a_bandwidth"])
            ctx.dma(sy, am8[:], inp["L0_attn_a_active_mask"])

        @ctx.section("G")
        def _(g):
            g.memset(vaug[:].rearrange("p (a c) -> p a c", c=65)[:, :, 64:65], 1.0)
            ctx.inc(g.memset(ctx.onesb[:], 1.0), "G")
        wph_ld = None

        @ctx.section("Y")
        def _(sy):
            nonlocal wph_ld
            wph_ld = ctx.dma(sy, wphf[:].rearrange("p (a c) -> p a c", a=8),
                             inp["L0_attn_a_w_phase"].rearrange("(a p) c -> p a c", p=128))
        rows_tok = ctx.tok("LD")
        ones_tok = ctx.tok("G")

        @ctx.section("S")
        def _(s_):
            ctx.wait(s_, wph_ld)
            ctx.inc(s_.copy(wph[:], wphf[:]), "S")
        wph_tok = ctx.tok("S")
        ctx.dump("wphf", wphf[:], wph_ld)
        ctx.dump("wph", wph[:], wph_tok)

        rawb = ctx.sb("rawb", [1, 48], F32, es)

        @ctx.section("V")
        def _(v):
            ctx.wait(v, rows_tok)
            v.tensor_copy(rawb[:, 0:16], rows[:, 0:16])
            v.tensor_copy(rawb[:, 16:32], rows[:, 16:32])
            v.tensor_copy(rawb[:, 32:48], rows[:, 32:48])
            ctx.inc(v.tensor_copy(AFR[:], am8[:]), "V")
        vt0 = ctx.tok("V")

        def cb_mm(t, i):
            return t.matmul(ctx.PO[0:128, 0:48], ctx.onesf[0:1, :], rawb[:],
                            start=True, stop=True)

        def cb_ev(e, i):
            e.copy(CARR[:, 0:16], ctx.PO[0:128, 0:16])
            e.copy(CARR[:, 48:64], ctx.PO[0:128, 16:32])
            return e.copy(CARR[:, 32:48], ctx.PO[0:128, 32:48])
        carr0_tok = mm_pipeline(ctx, [(cb_mm, cb_ev)], evac="S",
                                t_prewait=[vt0, ctx.consts_tok])

        dbg4 = ctx.sb("dbg4", [128, 64], F32, es)

        @ctx.section("V")
        def _(v):
            ctx.wait(v, carr0_tok)
            v.memset(ctx.LNJ[:, 0:512], 0.0)
            ctx.inc(v.tensor_scalar(ctx.LNJ[:, 0:16], CARR[:, 0:16], 1.0 / (2 * PI), None, ALU.mult), "V")
            v.wait_ge(ctx.sem["V"], ctx.cnt["V"])
            v.tensor_copy(ctx.II[:, 0:512], ctx.LNJ[:, 0:512])
            v.wait_ge(ctx.sem["V"], ctx.cnt["V"]) if False else None
            ctx.inc(v.tensor_copy(ctx.LNJ[:, 512:1024], ctx.II[:, 0:512]), "V")
            v.wait_ge(ctx.sem["V"], ctx.cnt["V"])
            ctx.inc(v.tensor_copy(gt1[:], ctx.LNJ[:, 512:528]), "V")
            v.wait_ge(ctx.sem["V"], ctx.cnt["V"])
            v.scalar_tensor_tensor(gt2[:], gt1[:], -2 * PI, CARR[:, 0:16],
                                   ALU.mult, ALU.add)
            v.tensor_copy(dbg4[:, 0:16], ctx.LNJ[:, 0:16])
            v.tensor_copy(dbg4[:, 16:32], gt1[:])
            v.tensor_copy(dbg4[:, 32:48], gt2[:])
            v.tensor_copy(dbg4[:, 48:64], CARR[:, 0:16])
            ctx.f(v, "V", v.reciprocal(CARR[:, 32:48], CARR[:, 32:48]))
            ctx.inc(v.tensor_scalar(CARR[:, 64:80], CARR[:, 48:64], -1.0,
                                    1.0 + 1e-6, ALU.mult, ALU.add), "V")
        vtc = ctx.tok("V")

        ctx.dump("dbg4", dbg4[:], vtc)

        @ctx.section("S")
        def _(s):
            ctx.wait(s, vtc)
            ctx.f(s, "S", s.activation(gt1[:], gt2[:], AF.Abs))
            s.activation(CARR[:, 16:32], gt2[:], AF.Sin)
            ctx.inc(s.activation(CARR[:, 0:16], gt1[:], AF.Sin,
                                 scale=-1.0, bias=ctx.ccol(PI / 2)), "S")
        carr_tok = ctx.tok("S")

        ph_work = []
        for tt in range(NT):
            def mk(tt=tt):
                def mm(t, i):
                    pb = ctx.P0 if i % 2 == 0 else ctx.P1
                    for d in range(ND):
                        r = t.matmul(pb[:, 0:2], ctx.XF(d)[:, tt * 128:(tt + 1) * 128],
                                     wph[:, d * 2:(d + 1) * 2],
                                     start=(d == 0), stop=(d == ND - 1))
                    return r

                def ev(e, i):
                    pb = ctx.P0 if i % 2 == 0 else ctx.P1
                    e.copy(PH0[:, tt:tt + 1], pb[:, 0:1])
                    return e.copy(PH1[:, tt:tt + 1], pb[:, 1:2])
                return mm, ev
            ph_work.append(mk())
        ph_tok = mm_pipeline(ctx, ph_work, evac="S",
                             t_prewait=[xfm_tok, wph_tok, ctx.tok("LD")])

        ctx.dump("ph0", PH0[:], ph_tok)
        ctx.dump("ph1", PH1[:], ph_tok)
        ctx.dump("carr", CARR[:], carr_tok)
        @ctx.section("V")
        def _(v):
            ctx.wait(v, ph_tok)
            ctx.wait(v, carr_tok)
            v.tensor_tensor(gt1[:], PH0[:], PH0[:], ALU.mult)
            ctx.f(v, "V", v.tensor_tensor(gt2[:], PH1[:], PH1[:], ALU.mult))
            ctx.inc(v.tensor_tensor(RINV[:], gt1[:], gt2[:], ALU.add), "V")
        vt3 = ctx.tok("V")

        @ctx.section("S")
        def _(s):
            ctx.wait(s, vt3)
            ctx.f(s, "S", s.activation(RINV[:], RINV[:], AF.Ln, bias=ctx.ccol(1e-30)))
            ctx.inc(s.activation(RINV[:], RINV[:], AF.Exp, scale=-0.5), "S")
        st3 = ctx.tok("S")

        for h in range(H):
            cc, sc = CARR[:, h:h + 1], CARR[:, 16 + h:17 + h]
            ib = CARR[:, 32 + h:33 + h]
            lk, ol = CARR[:, 48 + h:49 + h], CARR[:, 64 + h:65 + h]
            lgh = LG[:, NT * h:NT * (h + 1)]

            @ctx.section("V")
            def _(v, cc=cc, sc=sc, ib=ib, lgh=lgh, h=h):
                if h == 0:
                    ctx.wait(v, st3)
                ctx.f(v, "V", v.tensor_scalar(gt1[:], PH1[:], cc, None, ALU.mult))
                ctx.f(v, "V", v.scalar_tensor_tensor(gt2[:], PH0[:], sc, gt1[:], ALU.mult, ALU.add))
                ctx.f(v, "V", v.tensor_tensor(gt2[:], gt2[:], RINV[:], ALU.mult))
                ctx.inc(v.tensor_scalar(lgh[:], gt2[:], ib, ib,
                                        ALU.mult, ALU.subtract), "V")
            vth = ctx.tok("V")

            @ctx.section("S")
            def _(s, lgh=lgh, vth=vth):
                ctx.wait(s, vth)
                ctx.inc(s.activation(lgh[:], lgh[:], AF.Exp), "S")
            sth = ctx.tok("S")

            @ctx.section("V")
            def _(v, lk=lk, ol=ol, lgh=lgh, sth=sth):
                ctx.wait(v, sth)
                ctx.inc(v.tensor_scalar(lgh[:], lgh[:], lk, ol, ALU.mult, ALU.add), "V")
            vth2 = ctx.tok("V")

            @ctx.section("S")
            def _(s, lgh=lgh, vth2=vth2):
                ctx.wait(s, vth2)
                ctx.inc(s.activation(lgh[:], lgh[:], AF.Ln), "S")
        lg_tok = ctx.tok("S")

        ctx.dump("lg", LG[:], lg_tok)
        GH = 4
        grp_T = [None, None]
        for grp in range(4):
            cols = (256 * grp, 256 * grp + 256)

            wb0 = (grp % 2) * 4096

            @ctx.section("G")
            def _(g, cols=cols, grp=grp, wb0=wb0, pv=grp_T[grp % 2]):
                if pv is not None:
                    ctx.wait(g, pv)
                load_w(ctx, g, inp["L0_attn_a_wq"], wbuf[:, wb0:wb0 + 2048], cols)
                load_w(ctx, g, inp["L0_attn_a_wk"], wbuf[:, wb0 + 2048:wb0 + 4096], cols)
            qk_ld = ctx.tok("LG")

            for wi in range(2):
                dstw = wbuf[:, wb0 + wi * 2048:wb0 + (wi + 1) * 2048]
                qdst = qb if wi == 0 else kb
                work = []
                for dqt in range(2):
                    for tch in range(4):
                        def mk(dqt=dqt, tch=tch, dstw=dstw, qdst=qdst):
                            def mm(t, i):
                                pb = ctx.P0 if i % 2 == 0 else ctx.P1
                                for d in range(ND):
                                    r = t.matmul(
                                        pb[:],
                                        dstw[:, d * 256 + dqt * 128:d * 256 + dqt * 128 + 128],
                                        ctx.XF(d)[:, tch * 512:(tch + 1) * 512],
                                        start=(d == 0), stop=(d == ND - 1))
                                return r

                            def ev(e, i):
                                pb = ctx.P0 if i % 2 == 0 else ctx.P1
                                return e.copy(
                                    qdst[:, dqt * T + tch * 512:dqt * T + tch * 512 + 512],
                                    pb[:])
                            return mm, ev
                        work.append(mk())
                pre = [qk_ld] + ([xfm_tok] if grp == 0 and wi == 0 else [])
                mm_pipeline(ctx, work, evac="S", t_prewait=pre)
            qk_done_T = ctx.tok("T")

            @ctx.section("G")
            def _(g, cols=cols, qk_done_T=qk_done_T, wb0=wb0):
                ctx.wait(g, qk_done_T)
                load_w(ctx, g, inp["L0_attn_a_wv"], wbuf[:, wb0:wb0 + 2048], cols)
            wv_ld = ctx.tok("LG")

            vwork = []
            for tt in range(NT):
                def mk(tt=tt):
                    def mm(t, i):
                        pb = ctx.P0 if i % 2 == 0 else ctx.P1
                        for d in range(ND):
                            r = t.matmul(pb[:, 0:256], ctx.XF(d)[:, tt * 128:(tt + 1) * 128],
                                         wbuf[:, wb0 + d * 256:wb0 + (d + 1) * 256],
                                         start=(d == 0), stop=(d == ND - 1))
                        return r

                    def ev(e, i):
                        pb = ctx.P0 if i % 2 == 0 else ctx.P1
                        for hl in range(GH):
                            r = e.copy(vaug[:, (tt * GH + hl) * 65:(tt * GH + hl) * 65 + 64],
                                       pb[:, hl * 64:(hl + 1) * 64])
                        return r
                    return mm, ev
                vwork.append(mk())
            mm_pipeline(ctx, vwork, evac="S", t_prewait=[wv_ld])
            v_done_S = ctx.tok("S")
            grp_T[grp % 2] = ctx.tok("T")

            for hl in range(GH):
                Hh = grp * GH + hl
                til, base = hl // 2, (hl % 2) * 64
                qh = lambda c0, c1, til=til, base=base: qb[base:base + 64, til * T + c0:til * T + c1]
                kh = lambda c0, c1, til=til, base=base: kb[base:base + 64, til * T + c0:til * T + c1]

                segs = []
                for kt in range(NT):
                    for c in range(kt // 4, 4):
                        q0 = max(kt * 128, c * 512)
                        segs.append((kt, c, q0, (c + 1) * 512, q0 == kt * 128))

                t0, s0, v0 = ctx.cnt["T"], ctx.cnt["S"], ctx.cnt["V"]
                ndiag = sum(1 for sgg in segs if sgg[4])
                ctx.cnt["T"] += 2 * len(segs)
                ctx.cnt["S"] += len(segs)
                ctx.cnt["V"] += ndiag

                head_guards = ctx.guard_tokens()

                @ctx.section("T")
                def _(t, segs=segs, t0=t0, s0=s0, v0=v0, hl=hl, grp=grp, Hh=Hh,
                      qh=qh, kh=kh, v_done_S=v_done_S, head_guards=head_guards):
                    ctx.wait(t, v_done_S)
                    t.wait_ge(ctx.sem["S"], s0)
                    for tokg in head_guards:
                        ctx.wait(t, tokg)
                    if hl == 0 and grp == 0:
                        ctx.wait(t, lg_tok)
                    dcount = 0
                    for i, (kt, c, q0, q1, diag) in enumerate(segs):
                        pb = ctx.P0 if i % 2 == 0 else ctx.P1
                        if i >= 2:
                            t.wait_ge(ctx.sem["S"], s0 + i - 1)
                        t.matmul(pb[:, 0:q1 - q0], kh(kt * 128, kt * 128 + 128),
                                 qh(q0, q1), start=True, stop=True)\
                            .then_inc(ctx.sem["T"], 1)
                        if diag:
                            dcount += 1
                            t.wait_ge(ctx.sem["V"], v0 + dcount)
                        else:
                            t.wait_ge(ctx.sem["S"], s0 + i + 1)
                        t.matmul(ctx.PO[0:65, q0:q1],
                                 vaug[:, (kt * GH + hl) * 65:(kt * GH + hl) * 65 + 65],
                                 expb[:, (i % 2) * 512:(i % 2) * 512 + (q1 - q0)],
                                 start=(kt == 0), stop=(kt == min(15, 4 * c + 3)))\
                            .then_inc(ctx.sem["T"], 1)

                @ctx.section("S")
                def _(s, segs=segs, t0=t0, Hh=Hh):
                    for i, (kt, c, q0, q1, diag) in enumerate(segs):
                        pb = ctx.P0 if i % 2 == 0 else ctx.P1
                        s.wait_ge(ctx.sem["T"], t0 + 2 * i + 1)
                        s.activation(expb[:, (i % 2) * 512:(i % 2) * 512 + (q1 - q0)],
                                     pb[:, 0:q1 - q0], AF.Exp, scale=0.125,
                                     bias=LG[:, NT * Hh + kt:NT * Hh + kt + 1])\
                            .then_inc(ctx.sem["S"], 1)

                @ctx.section("V")
                def _(v, segs=segs, s0=s0):
                    for i, (kt, c, q0, q1, diag) in enumerate(segs):
                        if diag:
                            v.wait_ge(ctx.sem["S"], s0 + i + 1)
                            sl = expb[:, (i % 2) * 512:(i % 2) * 512 + 128]
                            v.tensor_tensor(sl, sl, ctx.tri01[:], ALU.mult)\
                                .then_inc(ctx.sem["V"], 1)

                attn_T = ctx.tok("T")

                @ctx.section("V")
                def _(v, attn_T=attn_T, Hh=Hh):
                    ctx.wait(v, attn_T)
                    ctx.f(v, "V", v.reciprocal(rnrow, ctx.PO[64:65, 0:T]))
                    ctx.inc(v.tensor_scalar(rnrow, rnrow,
                                            AFR[0:1, Hh:Hh + 1], None, ALU.mult), "V")
                rn_tok = ctx.tok("V")

                dtil, dbase = Hh // 2, (Hh % 2) * 64
                for c in range(4):
                    @ctx.section("T")
                    def _(t, c=c, rn_tok=rn_tok):
                        ctx.wait(t, rn_tok)
                        if c > 0:
                            ctx.wait(t, ("S", ctx.cnt["S"]))
                        ctx.inc(t.matmul(ctx.P2[0:64, :], ctx.onesf[0:1, 0:64],
                                         rnrow[0:1, c * 512:(c + 1) * 512].rearrange("o n -> o n") if False else rnrow[0:1, c * 512:(c + 1) * 512],
                                         start=True, stop=True), "T")
                    bt = ctx.tok("T")

                    @ctx.section("S")
                    def _(s, bt=bt):
                        ctx.wait(s, bt)
                        ctx.inc(s.copy(bcsb[0:64, :], ctx.P2[0:64, :]), "S")
                    bs = ctx.tok("S")

                    @ctx.section("V")
                    def _(v, c=c, bs=bs, dtil=dtil, dbase=dbase):
                        ctx.wait(v, bs)
                        ctx.inc(v.tensor_tensor(
                            ctx.ON(dtil)[dbase:dbase + 64, c * 512:(c + 1) * 512],
                            ctx.PO[0:64, c * 512:(c + 1) * 512],
                            bcsb[0:64, :], ALU.mult), "V")
                on_tok = ctx.tok("V")
                ctx.psum_guard["V"] = on_tok[1]
                ctx.psum_guard["S"] = ctx.cnt["S"]

        ctx.dump("qb3", qb[:], ("S", ctx.cnt["S"]))
        ctx.dump("kb3", kb[:], ("S", ctx.cnt["S"]))
        ctx.dump("vaug3", vaug[:], ("S", ctx.cnt["S"]))
        ctx.dump("on", ctx.XLNB[:], ("V", ctx.cnt["V"]))
        @ctx.section("G")
        def _(g):
            ctx.wait(g, ("T", ctx.cnt["T"]))
            load_w(ctx, g, inp["L0_attn_a_wo"], wbuf[:, 0:8192])
        wo_ld = ctx.tok("LG")

        wwork = []
        for tt in range(NT):
            for oc in range(2):
                def mk(tt=tt, oc=oc):
                    def mm(t, i):
                        pb = ctx.P0 if i % 2 == 0 else ctx.P1
                        for d in range(ND):
                            r = t.matmul(pb[:], ctx.ON(d)[:, tt * 128:(tt + 1) * 128],
                                         wbuf[:, d * 1024 + oc * 512:d * 1024 + oc * 512 + 512],
                                         start=(d == 0), stop=(d == ND - 1))
                        return r

                    def ev(e, i):
                        pb = ctx.P0 if i % 2 == 0 else ctx.P1
                        sl = ctx.X(tt)[:, oc * 512:(oc + 1) * 512]
                        return e.scalar_tensor_tensor(sl, pb[:], 1.0, sl,
                                                      ALU.mult, ALU.add)
                    return mm, ev
                wwork.append(mk())
        res_tok = mm_pipeline(ctx, wwork, evac="V", t_prewait=[wo_ld])
        ctx.x_tok = res_tok


def build_program(n_sub=8, debug=False, dump_only=None):
    nc = bass.Bass("TRN2", target_bir_lowering=False)
    inp = {k: nc.dram_tensor(k, list(s), d, kind="ExternalInput").ap()
           for k, (s, d) in INPUT_SPECS.items()}
    y = nc.dram_tensor("y", [T, D], F32, kind="ExternalOutput").ap()

    es = contextlib.ExitStack()
    with es:
        blk = es.enter_context(nc.Block())
        ctx = Ctx(nc, es, blk)
        ctx.debug = debug
        ctx.dump_only = dump_only
        ctx.make_sems()
        ctx.XRES = ctx.sb("XRES", [128, NT * D], F32)
        ctx.XLNB = ctx.sb("XLNB", [128, NT * D], BF16)
        ctx.XFM = ctx.sb("XFM", [128, ND * T], BF16)
        ctx.LNJ = ctx.sb("LNJ", [128, 2048], F32)
        ctx.II = ctx.sb("II", [128, 512], I32)
        ctx.CCIN = nc.dram_tensor("ccin", [1, 1], F32)
        ctx.CCOUT = nc.dram_tensor("ccout", [1, 1], F32)
        ctx.LNsum = ctx.sb("LNsum", [128, NT], F32)
        ctx.LNsq = ctx.sb("LNsq", [128, NT], F32)
        ctx.LNrstd = ctx.sb("LNrstd", [128, NT], F32)
        ctx.LNnegb = ctx.sb("LNnegb", [128, NT], F32)
        ctx.identb = ctx.sb("identb", [128, 128], BF16)
        ctx.identf = ctx.sb("identf", [128, 128], F32)
        ctx.tri01 = ctx.sb("tri01", [128, 128], BF16)
        ctx.onesf = ctx.sb("onesf", [1, 128], F32)
        ctx.onesb = ctx.sb("onesb", [1, 128], BF16)
        gi = ctx.sb("gi", [128, 128], I32)
        CONST_VALUES = [1e-5, 1e-6, 1e-30, PI / 2, 1.0 + 1e-6, 1e-8, 0.5, 1.0,
                        2.0 ** 23, -2.0 ** 23, 127.0, -128.0, 448.0, 1e-4, 1e-12, 0.0]
        ctx.CPOOL = ctx.sb("CPOOL", [128, len(CONST_VALUES)], F32)
        ctx.cvals = {v: i for i, v in enumerate(CONST_VALUES)}
        ctx.P0 = es.enter_context(nc.psum_tensor("P0", [128, 512], F32))
        ctx.P1 = es.enter_context(nc.psum_tensor("P1", [128, 512], F32))
        ctx.P2 = es.enter_context(nc.psum_tensor("P2", [128, 512], F32))
        ctx.PB = es.enter_context(nc.psum_tensor("PB", [128, 512], BF16))
        ctx.PO = es.enter_context(nc.psum_tensor("PO", [128, 2048], F32))

        @ctx.section("G")
        def _(g):
            for v, i in ctx.cvals.items():
                g.memset(ctx.CPOOL[:, i:i + 1], float(v))
            g.iota(gi[:], [[1, 128]], base=0, channel_multiplier=-1)
            g.tensor_single_scalar(ctx.identb[:], gi[:], 0, ALU.is_equal)
            g.tensor_single_scalar(ctx.identf[:], gi[:], 0, ALU.is_equal)
            g.tensor_single_scalar(ctx.tri01[:], gi[:], 0, ALU.is_ge)
            g.memset(ctx.onesf[:], 1.0)
            ctx.inc(g.memset(ctx.onesb[:], 1.0), "G")
        ctx.consts_tok = ctx.tok("G")

        @ctx.section("Y")
        def _(sy):
            ctx.dma(sy, ctx.XRES[:].rearrange("p (a d) -> p a d", a=NT),
                    inp["L0_attn_a_x"].rearrange("(a p) d -> p a d", p=128))
        ctx.x_tok = ctx.tok("LD")

        es_f0 = contextlib.ExitStack()
        ffn0_state = {"scope": es_f0}
        emitters = [emit_l0_attn_a,
                    lambda c, i: emit_l0_ffn(c, i, ffn0_state),
                    emit_l0_attn_b,
                    lambda c, i: emit_l0_ffn(c, i, ffn0_state),
                    emit_l1_attn_a,
                    lambda c, i: emit_l1_ffn(c, i, {}),
                    emit_l1_attn_b,
                    lambda c, i: emit_l1_ffn(c, i, {})]
        for i_em, em in enumerate(emitters[:n_sub]):
            em(ctx, inp)
            if i_em == 3:
                es_f0.close()
        if n_sub <= 3:
            es_f0.close()

        @ctx.section("Y")
        def _(sy):
            ctx.wait(sy, ctx.x_tok)
            ctx.wait(sy, ctx.tok("V"))
            ctx.wait(sy, ctx.tok("S"))
            sy.dma_start(y.rearrange("(a p) d -> p a d", p=128),
                         ctx.XRES[:].rearrange("p (a d) -> p a d", a=NT))\
                .then_inc(ctx.sem["ST"], 16)
            ctx.cnt["ST"] += 16
            sy.wait_ge(ctx.sem["ST"], ctx.cnt["ST"])

    return nc


def kernel(**inputs):
    nc = build_program(n_sub=8)
    in_maps = []
    for b in range(8):
        m = {}
        for k, (shape, dt) in INPUT_SPECS.items():
            if k == "L0_attn_a_x":
                a = np.asarray(inputs[k][b])
            else:
                a = np.asarray(inputs[k])
                if k == "L0_attn_a_active_mask":
                    a = a.view(np.uint8)
            m[k] = np.ascontiguousarray(a.reshape(shape))
        in_maps.append(m)
    res = run_bass_kernel_spmd(nc, in_maps, core_ids=list(range(8)))
    return np.stack([np.asarray(r["y"], dtype=np.float32) for r in res.results])


def _e4m3_round(ctx, v, u_ap, out_ap, ii0, ii1, ii2, fscr):
    ub = u_ap.bitcast(I32)
    ctx.f(v, "V", v.tensor_scalar(ii0, ub, 23, 0xFF, ALU.logical_shift_right, ALU.bitwise_and))
    v.tensor_scalar(ii1, ii0, 3, None, ALU.subtract)
    ctx.f(v, "V", v.tensor_scalar(ii2, ii0, -1, 257, ALU.mult, ALU.add))
    v.tensor_scalar(ii1, ii1, 23, None, ALU.logical_shift_left)
    ctx.f(v, "V", v.tensor_scalar(ii2, ii2, 136, None, ALU.min))
    v.tensor_scalar(ii1, ii1, 118 << 23, None, ALU.max)
    ctx.f(v, "V", v.tensor_scalar(ii2, ii2, 23, None, ALU.logical_shift_left))
    ctx.f(v, "V", v.tensor_tensor(fscr, u_ap, ii2.bitcast(F32), ALU.mult))
    ctx.f(v, "V", v.tensor_copy(ii2, fscr))
    ctx.f(v, "V", v.tensor_copy(fscr, ii2))
    ctx.inc(v.tensor_tensor(out_ap, fscr, ii1.bitcast(F32), ALU.mult), "V")


def emit_l0_ffn(ctx, inp, state):
    ctx.barrier()
    es = contextlib.ExitStack()
    with es:
        if "wqT" not in state:
            sc = state.get("scope")
            state["wqT"] = ctx.sb("wqT", [128, 8192], BF16, sc)
            state["winv"] = ctx.sb("winv", [1, 2], F32, sc)
            state["bias_bc"] = ctx.sb("biasbc", [128, D], F32, sc)
        ctx.II2 = ctx.sb("II2", [128, 1024], I32, es)
        ctx.II3 = ctx.sb("II3", [128, 1024], I32, es)
        ctx.II4 = ctx.sb("II4", [128, 1024], I32, es)
        ctx.LNJ2 = ctx.sb("LNJ2", [128, 1024], F32, es)
        if "wqT" in state and "w_done" not in state:
            pass
        if "w_done" not in state:
            wf = ctx.sb("wf", [128, 4096], F32, es)
            wq8 = ctx.XLNB[:, 0:8192]
            wcol = ctx.sb("wcol", [128, 9], F32, es)
            brow = ctx.sb("brow", [1, D], F32, es)
            swcol = ctx.sb("swcol", [128, 1], F32, es)

            wsrc = inp["L0_ffn_weight"].rearrange("(a p) c -> p a c", p=128)

            sg0 = ctx.start_guard()

            @ctx.section("Y")
            def _(sy):
                for tokk in sg0:
                    ctx.wait(sy, tokk)
                ctx.dma(sy, brow[:], inp["L0_ffn_bias"])
                ctx.dma(sy, wf[:].rearrange("p (a c) -> p a c", a=4), wsrc[:, 0:4, :])
            wld0 = ctx.tok("LD")

            @ctx.section("V")
            def _(v):
                ctx.wait(v, wld0)
                for d in range(4):
                    ctx.inc(v.tensor_reduce(wcol[:, d:d + 1], wf[:, d * 1024:(d + 1) * 1024],
                                            AX.X, ALU.max, apply_absolute_value=True), "V")
                v.wait_ge(ctx.sem["V"], ctx.cnt["V"])
                ctx.inc(v.tensor_copy(ctx.LNJ[:, 1100:1101], wcol[:, 3:4]), "V")
            wv0 = ctx.tok("V")

            @ctx.section("Y")
            def _(sy):
                ctx.wait(sy, wv0)
                ctx.dma(sy, wf[:].rearrange("p (a c) -> p a c", a=4), wsrc[:, 4:8, :])
            wld1 = ctx.tok("LD")

            @ctx.section("V")
            def _(v):
                ctx.wait(v, wld1)
                for d in range(4):
                    ctx.inc(v.tensor_reduce(wcol[:, 4 + d:5 + d], wf[:, d * 1024:(d + 1) * 1024],
                                            AX.X, ALU.max, apply_absolute_value=True), "V")
                v.wait_ge(ctx.sem["V"], ctx.cnt["V"])
                ctx.inc(v.tensor_reduce(wcol[:, 8:9], wcol[:, 0:8], AX.X, ALU.max), "V")
                v.wait_ge(ctx.sem["V"], ctx.cnt["V"])
            wv_tok = ctx.tok("V")

            @ctx.section("G")
            def _(g):
                ctx.wait(g, wv_tok)
                ctx.inc(g.tensor_reduce(wcol[0:1, 0:1], wcol[:, 8:9], AX.C, ALU.max), "G")
                g.wait_ge(ctx.sem["G"], ctx.cnt["G"])
            wg_tok = ctx.tok("G")

            @ctx.section("V")
            def _(v):
                ctx.wait(v, wg_tok)
                ctx.f(v, "V", v.tensor_scalar(wcol[0:1, 1:2], wcol[0:1, 0:1], 1e-12, None, ALU.max))
                ctx.f(v, "V", v.reciprocal(wcol[0:1, 2:3], wcol[0:1, 1:2]))
                ctx.f(v, "V", v.tensor_scalar(wcol[0:1, 3:4], wcol[0:1, 2:3], 448.0, 448.0, ALU.mult, ALU.min))
                ctx.inc(v.reciprocal(state["winv"][0:1, 0:1], wcol[0:1, 3:4]), "V")
            winv_tok = ctx.tok("V")

            def swb_mm(t, i):
                return t.matmul(ctx.P2[0:128, 0:1], ctx.onesf[0:1, :],
                                wcol[0:1, 3:4], start=True, stop=True)

            def swb_ev(e, i):
                return e.copy(swcol[:], ctx.P2[0:128, 0:1])
            swb_tok = mm_pipeline(ctx, [(swb_mm, swb_ev)], evac="S",
                                  t_prewait=[winv_tok])

            for half in range(2):
                @ctx.section("Y")
                def _(sy, half=half, vg=ctx.tok("V")):
                    ctx.wait(sy, vg)
                    ctx.dma(sy, wf[:].rearrange("p (a c) -> p a c", a=4),
                            wsrc[:, 4 * half:4 * half + 4, :])
                wldq = ctx.tok("LD")

                @ctx.section("V")
                def _(v, half=half, wldq=wldq, tguard=("T", ctx.cnt["T"])):
                    ctx.wait(v, wldq)
                    ctx.wait(v, swb_tok)
                    ctx.wait(v, tguard)
                    for hh in range(4):
                        sl = slice(half * 4096 + hh * 1024, half * 4096 + (hh + 1) * 1024)
                        lsl = slice(hh * 1024, (hh + 1) * 1024)
                        ctx.f(v, "V", v.tensor_scalar(ctx.LNJ[:, 0:1024], wf[:, lsl], swcol[:], None, ALU.mult))
                        _e4m3_round(ctx, v, ctx.LNJ[:, 0:1024], wq8[:, sl],
                                    ctx.II2[:, 0:1024], ctx.II3[:, 0:1024],
                                    ctx.II4[:, 0:1024], ctx.LNJ2[:, 0:1024])
            wq8_tok = ctx.tok("V")
            ctx.dump("wcol", wcol[:], winv_tok)

            def wt_src(tt):
                return wq8[:, tt * 1024:(tt + 1) * 1024]

            def wt_dst(d):
                return state["wqT"][:, d * 1024:(d + 1) * 1024]
            wtw = []
            for po in range(8):
                for d in range(8):
                    def mk(po=po, d=d):
                        def mm(t, i):
                            pb = (ctx.P0 if i % 2 == 0 else ctx.P1)[:].bitcast(BF16)
                            return t.transpose(pb[:, 0:128],
                                               wq8[:, po * 1024 + d * 128:po * 1024 + d * 128 + 128],
                                               ctx.identb[:])

                        def ev(e, i):
                            pb = (ctx.P0 if i % 2 == 0 else ctx.P1)[:].bitcast(BF16)
                            return e.copy(state["wqT"][:, d * 1024 + po * 128:d * 1024 + po * 128 + 128],
                                          pb[:, 0:128])
                        return mm, ev
                    wtw.append(mk())
            mm_pipeline(ctx, wtw, evac="S", t_prewait=[wq8_tok])

            def bb_mm(t, i):
                return t.matmul(ctx.PO[0:128, 0:512], ctx.onesf[0:1, :],
                                brow[0:1, i * 512:(i + 1) * 512], start=True, stop=True)

            def bb_ev(e, i):
                return e.copy(state["bias_bc"][:, i * 512:(i + 1) * 512], ctx.PO[0:128, 0:512])
            mm_pipeline(ctx, [(bb_mm, bb_ev)] * 2, evac="S", nbuf=1, t_prewait=[ctx.tok("LD")])
            state["w_done"] = (ctx.tok("S"), ctx.tok("V"))

        rstd, negb, mean, _ = emit_ln(ctx, wait_tok=ctx.x_tok, apply=False)
        amscr = ctx.sb("amscr", [128, 48], F32, es)
        mx = amscr[:, 0:NT]
        mn = amscr[:, NT:2 * NT]
        am = amscr[:, 2 * NT:3 * NT]
        sx = ctx.sb("sx", [1, 8], F32, es)
        sxcol = ctx.sb("sxcol", [128, 2], F32, es)
        vt_ln = ctx.tok("V")

        @ctx.section("V")
        def _(v):
            ctx.wait(v, ctx.x_tok)
            ctx.wait(v, vt_ln)
            for tt in range(NT):
                ctx.inc(v.tensor_reduce(mx[:, tt:tt + 1], ctx.X(tt), AX.X, ALU.max), "V")
            for tt in range(NT):
                ctx.inc(v.tensor_reduce(mn[:, tt:tt + 1], ctx.X(tt), AX.X, ALU.min), "V")
            v.wait_ge(ctx.sem["V"], ctx.cnt["V"])
            ctx.f(v, "V", v.tensor_tensor(mx[:], mx[:], mean[:], ALU.subtract))
            ctx.f(v, "V", v.tensor_tensor(mn[:], mean[:], mn[:], ALU.subtract))
            ctx.f(v, "V", v.tensor_tensor(mx[:], mx[:], mn[:], ALU.max))
            ctx.f(v, "V", v.tensor_tensor(am[:], mx[:], rstd[:], ALU.mult))
            ctx.inc(v.tensor_reduce(am[:, 0:1], am[:], AX.X, ALU.max), "V")
            v.wait_ge(ctx.sem["V"], ctx.cnt["V"])
        amax_tok = ctx.tok("V")

        @ctx.section("G")
        def _(g):
            ctx.wait(g, amax_tok)
            ctx.inc(g.tensor_reduce(sx[0:1, 0:1], am[:, 0:1], AX.C, ALU.max), "G")
            g.wait_ge(ctx.sem["G"], ctx.cnt["G"])
            ctx.dma(g, ctx.CCIN.ap(), sx[0:1, 0:1])
            g.wait_ge(ctx.sem["LG"], ctx.cnt["LG"])
            g.collective_compute("AllReduce", ALU.max, replica_groups=[list(range(8))],
                                 ins=[ctx.CCIN.ap()], outs=[ctx.CCOUT.ap()])\
                .then_inc(ctx.sem["CC"], 1)
            ctx.cnt["CC"] += 1
            g.wait_ge(ctx.sem["CC"], ctx.cnt["CC"])
            ctx.dma(g, sx[0:1, 1:2], ctx.CCOUT.ap())
        cc_tok = ctx.tok("LG")

        @ctx.section("V")
        def _(v):
            ctx.wait(v, cc_tok)
            ctx.f(v, "V", v.tensor_scalar(sx[0:1, 2:3], sx[0:1, 1:2], 1e-12, None, ALU.max))
            ctx.f(v, "V", v.reciprocal(sx[0:1, 3:4], sx[0:1, 2:3]))
            ctx.f(v, "V", v.tensor_scalar(sx[0:1, 4:5], sx[0:1, 3:4], 448.0, 448.0, ALU.mult, ALU.min))
            ctx.f(v, "V", v.reciprocal(sx[0:1, 5:6], sx[0:1, 4:5]))
            ctx.inc(v.tensor_tensor(sx[0:1, 6:7], sx[0:1, 5:6], state["winv"][0:1, 0:1], ALU.mult), "V")
        sx_tok = ctx.tok("V")

        def sxb_mm(t, i):
            return t.matmul(ctx.P2[0:128, 0:2], ctx.onesf[0:1, :],
                            sx[0:1, 4:6].rearrange("a b -> a b") if False else sx[0:1, 4:6],
                            start=True, stop=True)

        def sxb_ev(e, i):
            return e.copy(sxcol[:], ctx.P2[0:128, 0:2])
        sxb_tok = mm_pipeline(ctx, [(sxb_mm, sxb_ev)], evac="S", t_prewait=[sx_tok])
        ctx.dump("sx", sx[:], sx_tok)
        ctx.dump("amx", amscr[:], amax_tok)
        comb = ctx.sb("comb", [128, 1], F32, es)

        def cb2_mm(t, i):
            return t.matmul(ctx.P2[0:128, 2:3], ctx.onesf[0:1, :],
                            sx[0:1, 6:7], start=True, stop=True)

        def cb2_ev(e, i):
            return e.copy(comb[:], ctx.P2[0:128, 2:3])
        comb_tok = mm_pipeline(ctx, [(cb2_mm, cb2_ev)], evac="S", t_prewait=[sx_tok])

        xqfm = ctx.XFM
        sA = ctx.sb("sA", [128, NT], F32, es)
        sB = ctx.sb("sB", [128, NT], F32, es)
        xqt = ctx.sb("xqt", [128, D], BF16, es)

        @ctx.section("V")
        def _(v):
            ctx.wait(v, sxb_tok)
            ctx.f(v, "V", v.tensor_scalar(sA[:], rstd[:], sxcol[:, 0:1], None, ALU.mult))
            ctx.inc(v.tensor_scalar(sB[:], negb[:], sxcol[:, 0:1], None, ALU.mult), "V")
        sab_tok = ctx.tok("V")

        xq_toks = []
        for tt in range(NT):
            @ctx.section("S")
            def _(s, tt=tt):
                if tt == 0:
                    ctx.wait(s, sab_tok)
                else:
                    ctx.wait(s, xq_toks[tt - 1])
                ctx.inc(s.activation(ctx.LNJ2[:, 0:D], ctx.X(tt), AF.Identity,
                                     bias=sB[:, tt:tt + 1], scale=sA[:, tt:tt + 1]), "S")
            u_tok = ctx.tok("S")

            @ctx.section("V")
            def _(v, u_tok=u_tok, tt=tt, tguard=("T", ctx.cnt["T"])):
                ctx.wait(v, u_tok)
                ctx.wait(v, tguard)
                _e4m3_round(ctx, v, ctx.LNJ2[:, 0:D], xqt[:], ctx.II2[:, 0:D],
                            ctx.II3[:, 0:D], ctx.II4[:, 0:D], ctx.LNJ[:, 0:D])
            xq_toks.append(ctx.tok("V"))

            tw = []
            for d in range(ND):
                def mk(d=d, tt=tt):
                    def mm(t, i):
                        pb = (ctx.P0 if i % 2 == 0 else ctx.P1)[:].bitcast(BF16)
                        return t.transpose(pb[:, 0:128], xqt[:, d * 128:(d + 1) * 128],
                                           ctx.identb[:])

                    def ev(e, i):
                        pb = (ctx.P0 if i % 2 == 0 else ctx.P1)[:].bitcast(BF16)
                        return ecopy(e, xqfm[:, d * T + tt * 128:d * T + tt * 128 + 128],
                                     pb[:, 0:128])
                    return mm, ev
                tw.append(mk())
            mm_pipeline(ctx, tw, evac="V", t_prewait=[xq_toks[-1]])
        xq_done = ctx.tok("V")


        ctx.dump("xqfm2", xqfm[:], xq_done)
        ctx.dump("wqT2", state["wqT"][:], xq_done)
        nrounds = NT * 2
        t0 = ctx.cnt["T"]; s0 = ctx.cnt["S"]; v0 = ctx.cnt["V"]
        guards = ctx.guard_tokens()
        ctx.cnt["T"] = t0 + nrounds
        ctx.cnt["S"] = s0 + nrounds
        ctx.cnt["V"] = v0 + 2 * nrounds
        ctx.psum_guard["S"] = ctx.cnt["S"]

        @ctx.section("T")
        def _(t, t0=t0, s0=s0):
            ctx.wait(t, xq_done)
            ctx.wait(t, comb_tok)
            ctx.wait(t, state["w_done"][0])
            for tok in guards:
                ctx.wait(t, tok)
            for i in range(nrounds):
                tt, oc = i // 2, i % 2
                pb = ctx.P0 if i % 2 == 0 else ctx.P1
                if i >= 2:
                    t.wait_ge(ctx.sem["S"], s0 + i - 1)
                for d in range(ND):
                    r = t.matmul(pb[:], xqfm[:, d * T + tt * 128:d * T + tt * 128 + 128],
                                 state["wqT"][:, d * 1024 + oc * 512:d * 1024 + oc * 512 + 512],
                                 start=(d == 0), stop=(d == ND - 1))
                r.then_inc(ctx.sem["T"], 1)

        @ctx.section("S")
        def _(s, t0=t0, s0=s0, v0=v0):
            for i in range(nrounds):
                pb = ctx.P0 if i % 2 == 0 else ctx.P1
                s.wait_ge(ctx.sem["T"], t0 + i + 1)
                if i >= 2:
                    s.wait_ge(ctx.sem["V"], v0 + 2 * (i - 1))
                s.activation(ctx.LNJ2[:, (i % 2) * 512:(i % 2) * 512 + 512],
                             pb[:], AF.Identity, scale=comb[:, 0:1])                    .then_inc(ctx.sem["S"], 1)

        @ctx.section("V")
        def _(v, s0=s0, v0=v0):
            for i in range(nrounds):
                tt, oc = i // 2, i % 2
                v.wait_ge(ctx.sem["S"], s0 + i + 1)
                sl = ctx.LNJ2[:, (i % 2) * 512:(i % 2) * 512 + 512]
                xsl = ctx.X(tt)[:, oc * 512:(oc + 1) * 512]
                v.tensor_tensor(ctx.XLNB[:, 0:512], sl,
                                state["bias_bc"][:, oc * 512:(oc + 1) * 512], ALU.add)\
                    .then_inc(ctx.sem["V"], 1)
                v.wait_ge(ctx.sem["V"], v0 + 2 * i + 1)
                v.tensor_tensor(xsl, xsl, ctx.XLNB[:, 0:512], ALU.add)\
                    .then_inc(ctx.sem["V"], 1)
        ctx.x_tok = ctx.tok("V")


F16 = mybir.dt.float16


def emit_l0_attn_b(ctx, inp):
    nc = ctx.nc
    es = contextlib.ExitStack()
    with es:
        GNWC = ctx.sb("GNWC", [128, 8], F32, es)
        HK = ctx.sb("HK", [128, 2048], F32, es)
        HV = ctx.sb("HV", [128, 2048], F32, es)
        HKB = ctx.sb("HKB", [128, 512], BF16, es)
        HVB = ctx.sb("HVB", [128, 512], BF16, es)
        TRIF = ctx.sb("TRIF", [128, 128], F16, es)
        onescol = ctx.sb("onescol", [128, 1], F16, es)
        QF = ctx.sb("QF", [128, 1024], BF16, es)
        KF = ctx.sb("KF", [128, 1024], BF16, es)
        KT = ctx.sb("KT", [128, 1024], BF16, es)
        VT = ctx.sb("VT", [128, 1024], BF16, es)
        GH = ctx.sb("GH", [128, 1024], F16, es)
        EGP = ctx.sb("EGP", [128, 1024], BF16, es)
        EGM = ctx.sb("EGM", [128, 1024], BF16, es)
        SP = ctx.sb("SP", [128, 1024], BF16, es)
        SPF = ctx.sb("SPF", [128, 1024], BF16, es)
        PPF = ctx.sb("PPF", [128, 256], BF16, es)
        ATS = ctx.sb("ATS", [128, 128], BF16, es)
        PTS = ctx.sb("PTS", [128, 128], BF16, es)
        ZF = ctx.sb("ZF", [128, 256], F32, es)
        USCR = None
        PB_ = ctx.sb("PBu", [128, 256], BF16, es)
        zcols = ctx.sb("zcols", [128, 4], F32, es)
        LAMB = ctx.sb("LAMB", [128, 1024], F32, es)
        LAMR = LAMB[0:1, :]
        gnr = ctx.LNJ[0:1, 0:2048]
        LAMT = ctx.sb("LAMT", [128, 8], F32, es)
        OSW = ctx.sb("OSW", [128, 1024], BF16, es)
        RSQR = ctx.sb("RSQR", [1, 128], F32, es)
        RSQB = ctx.sb("RSQB", [128, 128], F32, es)
        FS = ctx.LNJ[:, 0:1024]
        GB = ctx.LNJ[:, 1024:2048]
        SQB = ctx.LNJ[:, 0:1024]
        STMP = ctx.LNJ[:, 1024:2048]
        ONC = OSW
        GNWB = None
        XSP = nc.dram_tensor("xsp_gsa", [T, D], F32)

        XRB = ctx.XRES[:].bitcast(BF16)
        wqT = XRB[:, 0:8192]
        wkT = XRB[:, 8192:16384]
        wvT = XRB[:, 16384:24576]
        wfT = XRB[:, 24576:32768]
        woT = ctx.XLNB[:, 0:8192]
        wstage = ctx.XLNB[:, 8192:16384]

        sg0 = ctx.start_guard()

        @ctx.section("Y")
        def _(sy):
            for tokk in sg0:
                ctx.wait(sy, tokk)
            ctx.dma(sy, XSP.ap().rearrange("(a p) d -> p a d", p=128),
                    ctx.XRES[:].rearrange("p (a d) -> p a d", a=NT))
            ctx.dma(sy, gnr[:, 0:D], inp["L0_attn_b_norm_weight"])
            ctx.dma(sy, gnr[:, D:2 * D], inp["L0_attn_b_gnorm_weight"])
        spill_tok = ctx.tok("LD")

        @ctx.section("G")
        def _(g):
            for tokk in sg0:
                ctx.wait(g, tokk)
            g.memset(HK[:], 0.0)
            g.memset(HV[:], 0.0)
            g.memset(onescol[:], 1.0)
            gi2 = ctx.II[:, 0:128]
            g.iota(gi2, [[1, 128]], base=0, channel_multiplier=-1)
            ctx.inc(g.tensor_single_scalar(TRIF[:], gi2, 0, ALU.is_ge), "G")
        ginit_tok = ctx.tok("G")

        def nb_mm(t, i):
            return t.matmul(ctx.PO[0:128, 0:512], ctx.onesf[0:1, :],
                            gnr[0:1, i * 512:(i + 1) * 512], start=True, stop=True)

        def nb_ev(e, i):
            return e.copy(LAMB[:, i * 512:(i + 1) * 512], ctx.PO[0:128, 0:512])
        nw_tok = mm_pipeline(ctx, [(nb_mm, nb_ev)] * 2, evac="S", nbuf=1,
                             t_prewait=[spill_tok])
        gc_work = []
        for d in range(8):
            def mk(d=d):
                def mm(t, i):
                    return t.transpose(ctx.P2[0:128, 0:1],
                                       gnr[0:1, D + d * 128:D + (d + 1) * 128],
                                       ctx.identf[0:1, 0:1])

                def ev(e, i):
                    return e.copy(GNWC[:, d:d + 1], ctx.P2[0:128, 0:1])
                return mm, ev
            gc_work.append(mk())
        gnc_tok = mm_pipeline(ctx, gc_work, evac="S", nbuf=1, t_prewait=[spill_tok])

        _, _, _, lntok = emit_ln(ctx, wait_tok=nw_tok, rms=True, gamma_bcast=LAMB[:])
        xfm_tok = emit_transpose_to_xfm(ctx, lntok)

        def prep_weight(wname, dst, stage, pre):
            @ctx.section("G")
            def _(g):
                for tokk in pre:
                    ctx.wait(g, tokk)
                ctx.dma(g, stage[:, 0:8192].rearrange("p (a c) -> p a c", a=8),
                        inp[wname].rearrange("(a p) c -> p a c", p=128))
            ldt = ctx.tok("LG")
            work = []
            for po in range(8):
                for dd in range(8):
                    def mk(po=po, dd=dd):
                        def mm(t, i):
                            pb = (ctx.P0 if i % 2 == 0 else ctx.P1)[:].bitcast(BF16)
                            return t.transpose(pb[:, 0:128],
                                               stage[:, po * 1024 + dd * 128:po * 1024 + dd * 128 + 128],
                                               ctx.identb[:])

                        def ev(e, i):
                            pb = (ctx.P0 if i % 2 == 0 else ctx.P1)[:].bitcast(BF16)
                            return e.copy(dst[:, dd * 1024 + po * 128:dd * 1024 + po * 128 + 128],
                                          pb[:, 0:128])
                        return mm, ev
                    work.append(mk())
            return mm_pipeline(ctx, work, evac="S", t_prewait=[ldt])

        wq_tok = prep_weight("L0_attn_b_w_q", wqT, wstage, [spill_tok, lntok, xfm_tok])
        wk_tok = prep_weight("L0_attn_b_w_k", wkT, wstage, [wq_tok])
        wv_tok = prep_weight("L0_attn_b_w_v", wvT, wstage, [wk_tok])
        wf_tok = prep_weight("L0_attn_b_w_f", wfT, wstage, [wv_tok])
        wo_tok = prep_weight("L0_attn_b_w_o", woT, wstage, [wf_tok])

        NCH = 16
        for c in range(NCH):
            tsl = slice(c * 128, (c + 1) * 128)

            xt = ctx.LNJ[:, 1024:2048]

            fw = []
            for half in range(2):
                def mk(half=half):
                    def mm(t, i):
                        pb = ctx.P0 if i % 2 == 0 else ctx.P1
                        for dd in range(ND):
                            r = t.matmul(pb[:], ctx.XF(dd)[:, tsl],
                                         wfT[:, dd * 1024 + half * 512:dd * 1024 + half * 512 + 512],
                                         start=(dd == 0), stop=(dd == ND - 1))
                        return r

                    def ev(e, i):
                        pb = ctx.P0 if i % 2 == 0 else ctx.P1
                        return e.activation(FS[:, half * 512:half * 512 + 512], pb[:],
                                            AF.Sigmoid)
                    return mm, ev
                fw.append(mk())
            pre = [wf_tok] if c == 0 else []
            fs_tok = mm_pipeline(ctx, fw, evac="S", t_prewait=pre)

            @ctx.section("S")
            def _(s, fs_tok=fs_tok, stg=("ST", ctx.cnt["ST"])):
                ctx.wait(s, fs_tok)
                ctx.wait(s, stg)
                s.activation(GB[:, 0:512], FS[:, 0:512], AF.Ln)
                ctx.inc(s.activation(GB[:, 512:1024], FS[:, 512:1024], AF.Ln), "S")
            lng_tok = ctx.tok("S")

            @ctx.section("V")
            def _(v, lng_tok=lng_tok, tg=("T", ctx.cnt["T"])):
                ctx.wait(v, lng_tok)
                ctx.wait(v, tg)
                ctx.inc(v.tensor_scalar(GH[:], GB[:], 1.0 / 16.0, None, ALU.mult), "V")
            gh_tok = ctx.tok("V")

            @ctx.section("T")
            def _(t, gh_tok=gh_tok, g=ctx.guard_tokens()):
                ctx.wait(t, gh_tok)
                for tokk in g:
                    ctx.wait(t, tokk)
                t.matmul(ctx.PO[:, 0:512], TRIF[:], GH[:, 0:512], start=True, stop=True)\
                    .then_inc(ctx.sem["T"], 1)
                t.matmul(ctx.PO[:, 512:1024], TRIF[:], GH[:, 512:1024], start=True, stop=True)\
                    .then_inc(ctx.sem["T"], 1)
                t.matmul(ctx.P2[0:1, 0:512], onescol[:], GH[:, 0:512], start=True, stop=True)\
                    .then_inc(ctx.sem["T"], 1)
                t.matmul(ctx.P2[0:1, 0:512].bitcast(F32) if False else ctx.PO[0:1, 1024:1536],
                         onescol[:], GH[:, 512:1024], start=True, stop=True)\
                    .then_inc(ctx.sem["T"], 1)
            ctx.cnt["T"] += 4
            gl_T = ctx.tok("T")

            @ctx.section("S")
            def _(s, gl_T=gl_T):
                ctx.wait(s, gl_T)
                s.activation(EGP[:, 0:512], ctx.PO[:, 0:512], AF.Exp)
                s.activation(EGP[:, 512:1024], ctx.PO[:, 512:1024], AF.Exp)
                s.activation(EGM[:, 0:512], ctx.PO[:, 0:512], AF.Exp, scale=-1.0)
                s.activation(EGM[:, 512:1024], ctx.PO[:, 512:1024], AF.Exp, scale=-1.0)
                s.activation(LAMR[0:1, 0:512], ctx.P2[0:1, 0:512], AF.Exp)
                ctx.inc(s.activation(LAMR[0:1, 512:1024], ctx.PO[0:1, 1024:1536], AF.Exp), "S")
            eg_tok = ctx.tok("S")
            ctx.psum_guard["S"] = ctx.cnt["S"]

            lb = []
            for half in range(2):
                def mk(half=half):
                    def mm(t, i):
                        return t.matmul(ctx.PO[0:128, half * 512:half * 512 + 512],
                                        ctx.onesf[0:1, :],
                                        LAMR[0:1, half * 512:half * 512 + 512],
                                        start=True, stop=True)

                    def ev(e, i):
                        return ecopy(e, LAMB[:, half * 512:half * 512 + 512],
                                      ctx.PO[0:128, half * 512:half * 512 + 512])
                    return mm, ev
                lb.append(mk())
            for dd in range(8):
                def mk(dd=dd):
                    def mm(t, i):
                        return t.transpose(ctx.P2[0:128, 0:1],
                                           LAMR[0:1, dd * 128:(dd + 1) * 128],
                                           ctx.identf[0:1, 0:1])

                    def ev(e, i):
                        return ecopy(e, LAMT[:, dd:dd + 1], ctx.P2[0:128, 0:1])
                    return mm, ev
                lb.append(mk())
            lam_tok = mm_pipeline(ctx, lb, evac="V", nbuf=1, t_prewait=[eg_tok])

            for wi, (wT, dstb) in enumerate([(wqT, QF), (wkT, KF)]):
                qw = []
                for dqt in range(8):
                    def mk(dqt=dqt, wT=wT, dstb=dstb):
                        def mm(t, i):
                            pb = ctx.P0 if i % 2 == 0 else ctx.P1
                            for dd in range(ND):
                                r = t.matmul(pb[:, 0:128], wT[:, dd * 1024 + dqt * 128:dd * 1024 + dqt * 128 + 128],
                                             ctx.XF(dd)[:, tsl],
                                             start=(dd == 0), stop=(dd == ND - 1))
                            return r

                        def ev(e, i):
                            pb = ctx.P0 if i % 2 == 0 else ctx.P1
                            return e.activation(dstb[:, dqt * 128:(dqt + 1) * 128],
                                                pb[:, 0:128], AF.Silu)
                        return mm, ev
                    qw.append(mk())
                pre = []
                if c == 0:
                    pre = [wq_tok if wi == 0 else wk_tok]
                mm_pipeline(ctx, qw, evac="S", t_prewait=pre)
            vw = []
            for half in range(2):
                def mk(half=half):
                    def mm(t, i):
                        pb = ctx.P0 if i % 2 == 0 else ctx.P1
                        for dd in range(ND):
                            r = t.matmul(pb[:], ctx.XF(dd)[:, tsl],
                                         wvT[:, dd * 1024 + half * 512:dd * 1024 + half * 512 + 512],
                                         start=(dd == 0), stop=(dd == ND - 1))
                        return r

                    def ev(e, i):
                        pb = ctx.P0 if i % 2 == 0 else ctx.P1
                        return e.activation(VT[:, half * 512:half * 512 + 512], pb[:], AF.Silu)
                    return mm, ev
                vw.append(mk())
            vt_tok = mm_pipeline(ctx, vw, evac="S",
                                 t_prewait=[wv_tok] if c == 0 else [])
            qkv_S = ctx.tok("S")

            @ctx.section("S")
            def _(s, gh_tok=gh_tok):
                ctx.wait(s, gh_tok)
                ctx.inc(s.activation(FS[:], GH[:], AF.Exp), "S")
            eg2 = ctx.tok("S")

            @ctx.section("V")
            def _(v, eg2=eg2, eg_tok=eg_tok):
                ctx.wait(v, eg2)
                ctx.wait(v, eg_tok)
                ctx.f(v, "V", v.tensor_scalar(STMP[:], FS[:], -1.0, 1.0, ALU.mult, ALU.add))
                ctx.inc(v.tensor_tensor(SP[:], STMP[:], EGM[:], ALU.mult), "V")
            sp_tok = ctx.tok("V")

            tw = []
            for dd in range(8):
                def mk(dd=dd):
                    def mm(t, i):
                        pb = (ctx.P0 if i % 2 == 0 else ctx.P1)[:].bitcast(BF16)
                        return t.transpose(pb[:, 0:128], SP[:, dd * 128:(dd + 1) * 128],
                                           ctx.identb[:])

                    def ev(e, i):
                        pb = (ctx.P0 if i % 2 == 0 else ctx.P1)[:].bitcast(BF16)
                        return ecopy(e, SPF[:, dd * 128:(dd + 1) * 128], pb[:, 0:128])
                    return mm, ev
                tw.append(mk())
            for dd in range(8):
                def mk(dd=dd):
                    def mm(t, i):
                        pb = (ctx.P0 if i % 2 == 0 else ctx.P1)[:].bitcast(BF16)
                        return t.transpose(pb[:, 0:128], KF[:, dd * 128:(dd + 1) * 128],
                                           ctx.identb[:])

                    def ev(e, i):
                        pb = (ctx.P0 if i % 2 == 0 else ctx.P1)[:].bitcast(BF16)
                        return ecopy(e, KT[:, dd * 128:(dd + 1) * 128], pb[:, 0:128])
                    return mm, ev
                tw.append(mk())
            tr_tok = mm_pipeline(ctx, tw, evac="V", t_prewait=[sp_tok, qkv_S])

            @ctx.section("Y")
            def _(sy, c=c, xt=xt, sp_tok=sp_tok):
                ctx.wait(sy, spill_tok)
                ctx.wait(sy, sp_tok)
                ctx.dma(sy, xt, XSP.ap()[c * 128:(c + 1) * 128, :])
            xld = ctx.tok("LD")

            pass

            for h in range(4):
                hs = slice(h * 256, (h + 1) * 256)
                @ctx.section("G")
                def _(g, h=h, pv=ctx.tok("V")):
                    ctx.wait(g, pv)
                    g.tensor_copy(HKB[:, 0:256], HK[:, h * 512:h * 512 + 256])
                    g.tensor_copy(HKB[:, 256:512], HK[:, h * 512 + 256:h * 512 + 512])
                    g.tensor_copy(HVB[:, 0:256], HV[:, h * 512:h * 512 + 256])
                    ctx.inc(g.tensor_copy(HVB[:, 256:512], HV[:, h * 512 + 256:h * 512 + 512]), "G")
                hb_tok = ctx.tok("G")

                @ctx.section("T")
                def _(t, h=h, tr_tok=tr_tok, g=ctx.guard_tokens()):
                    ctx.wait(t, tr_tok)
                    for tokk in g:
                        ctx.wait(t, tokk)
                    t.matmul(ctx.P2[:, 0:128], KF[:, h * 256:h * 256 + 128],
                             QF[:, h * 256:h * 256 + 128], start=True, stop=False)
                    ctx.inc(t.matmul(ctx.P2[:, 0:128], KF[:, h * 256 + 128:h * 256 + 256],
                                     QF[:, h * 256 + 128:h * 256 + 256],
                                     start=False, stop=True), "T")
                at_T = ctx.tok("T")

                @ctx.section("V")
                def _(v, at_T=at_T):
                    ctx.wait(v, at_T)
                    ctx.inc(v.tensor_tensor(ATS[:], ctx.P2[:, 0:128], ctx.tri01[:],
                                            ALU.mult), "V")
                at_V = ctx.tok("V")
                if c == 0 and h == 0:
                    ctx.dump("h0at", ATS[:], at_V, hold="T")

                @ctx.section("T")
                def _(t, h=h, at_V=at_V, hb_tok=hb_tok):
                    ctx.wait(t, at_V)
                    ctx.wait(t, hb_tok)
                    t.matmul(ctx.PO[:, 1024:1280], QF[:, h * 256:h * 256 + 128],
                             HKB[:, 0:256], start=True, stop=False)
                    t.matmul(ctx.PO[:, 1024:1280], QF[:, h * 256 + 128:h * 256 + 256],
                             HKB[:, 256:512], start=False, stop=False)
                    ctx.inc(t.matmul(ctx.PO[:, 1024:1280], ATS[:], SP[:, hs],
                                     start=False, stop=True), "T")
                z_T = ctx.tok("T")

                @ctx.section("V")
                def _(v, h=h, z_T=z_T):
                    ctx.wait(v, z_T)
                    ctx.f(v, "V", v.tensor_tensor(ZF[:], ctx.PO[:, 1024:1280], EGP[:, hs], ALU.mult))
                    ctx.f(v, "V", v.tensor_reduce(zcols[:, 0:1], ZF[:], AX.X, ALU.max))
                    ctx.inc(v.tensor_scalar(zcols[:, 1:2], zcols[:, 0:1], -1.0, None, ALU.mult), "V")
                zm_tok = ctx.tok("V")
                if c == 0 and h == 0:
                    ctx.dump("h0z", ZF[:], (zm_tok[0], zm_tok[1] - 2), hold="S")

                @ctx.section("S")
                def _(s, zm_tok=zm_tok):
                    ctx.wait(s, zm_tok)
                    ctx.inc(s.activation(ZF[:], ZF[:], AF.Exp, bias=zcols[:, 1:2]), "S")
                ze_tok = ctx.tok("S")

                @ctx.section("V")
                def _(v, h=h, ze_tok=ze_tok):
                    ctx.wait(v, ze_tok)
                    ctx.f(v, "V", v.tensor_reduce(zcols[:, 2:3], ZF[:], AX.X, ALU.add))
                    ctx.f(v, "V", v.reciprocal(zcols[:, 3:4], zcols[:, 2:3]))
                    ctx.f(v, "V", v.tensor_scalar(PB_[:], ZF[:], zcols[:, 3:4], None, ALU.mult))
                    ctx.inc(v.tensor_tensor(PB_[:], PB_[:], EGP[:, hs], ALU.mult), "V")
                pp_tok = ctx.tok("V")
                if c == 0 and h == 0:
                    ctx.dump("h0p", PB_[:], pp_tok, hold="T")

                ppw = []
                for m2 in range(2):
                    def mk(m2=m2):
                        def mm(t, i):
                            pb = (ctx.P0 if i % 2 == 0 else ctx.P1)[:].bitcast(BF16)
                            return t.transpose(pb[:, 0:128], PB_[:, m2 * 128:(m2 + 1) * 128],
                                               ctx.identb[:])

                        def ev(e, i):
                            pb = (ctx.P0 if i % 2 == 0 else ctx.P1)[:].bitcast(BF16)
                            return ecopy(e, PPF[:, m2 * 128:(m2 + 1) * 128], pb[:, 0:128])
                        return mm, ev
                    ppw.append(mk())
                ppf_tok = mm_pipeline(ctx, ppw, evac="V", t_prewait=[pp_tok])

                @ctx.section("T")
                def _(t, h=h, ppf_tok=ppf_tok, g=ctx.guard_tokens()):
                    ctx.wait(t, ppf_tok)
                    for tokk in g:
                        ctx.wait(t, tokk)
                    t.matmul(ctx.P2[:, 0:128], SPF[:, h * 256:h * 256 + 128],
                             PPF[:, 0:128], start=True, stop=False)
                    ctx.inc(t.matmul(ctx.P2[:, 0:128], SPF[:, h * 256 + 128:h * 256 + 256],
                                     PPF[:, 128:256], start=False, stop=True), "T")
                pt_T = ctx.tok("T")

                @ctx.section("V")
                def _(v, pt_T=pt_T):
                    ctx.wait(v, pt_T)
                    ctx.inc(v.tensor_tensor(PTS[:], ctx.P2[:, 0:128], ctx.tri01[:],
                                            ALU.mult), "V")
                pt_V = ctx.tok("V")
                if c == 0 and h == 0:
                    ctx.dump("h0pt", PTS[:], pt_V, hold="T")

                @ctx.section("T")
                def _(t, h=h, pt_V=pt_V, vt_tok=vt_tok):
                    ctx.wait(t, pt_V)
                    for vt2 in range(2):
                        po = ctx.PO[:, 1536 + vt2 * 128:1536 + (vt2 + 1) * 128]
                        t.matmul(po, HVB[:, vt2 * 128:vt2 * 128 + 128], PPF[:, 0:128],
                                 start=True, stop=False)
                        t.matmul(po, HVB[:, 256 + vt2 * 128:256 + vt2 * 128 + 128],
                                 PPF[:, 128:256], start=False, stop=False)
                        r = t.matmul(po, VT[:, h * 256 + vt2 * 128:h * 256 + vt2 * 128 + 128],
                                     PTS[:], start=False, stop=True)
                    ctx.inc(r, "T")
                o_T = ctx.tok("T")

                @ctx.section("S")
                def _(s, h=h, o_T=o_T):
                    ctx.wait(s, o_T)
                    s.activation(OSW[:, h * 256:h * 256 + 128],
                                 ctx.PO[:, 1536:1664], AF.Silu)
                    ctx.inc(s.activation(OSW[:, h * 256 + 128:h * 256 + 256],
                                         ctx.PO[:, 1664:1792], AF.Silu), "S")
                osw_tok = ctx.tok("S")
                ctx.psum_guard["S"] = ctx.cnt["S"]

                @ctx.section("T")
                def _(t, h=h, osw_tok=osw_tok, tr_tok=tr_tok, hb_tok=hb_tok,
                      g=ctx.guard_tokens()):
                    for tokk in g:
                        ctx.wait(t, tokk)
                    for k2 in range(2):
                        pb = ctx.P0 if k2 == 0 else ctx.P1
                        ctx.inc(t.matmul(pb[:, 0:256],
                                         KT[:, h * 256 + k2 * 128:h * 256 + k2 * 128 + 128],
                                         SP[:, hs], start=True, stop=True), "T")
                    for m2 in range(2):
                        pb = ctx.P2 if m2 == 0 else ctx.PO[:, 1024:1280]
                        ap = pb[:, 0:256] if m2 == 0 else pb
                        ctx.inc(t.matmul(ap, SP[:, h * 256 + m2 * 128:h * 256 + m2 * 128 + 128],
                                         VT[:, hs], start=True, stop=True), "T")
                st_T = ctx.tok("T")

                @ctx.section("V")
                def _(v, h=h, st_T=st_T, lam_tok=lam_tok):
                    ctx.wait(v, st_T)
                    ctx.wait(v, lam_tok)
                    for k2, pb in ((0, ctx.P0), (1, ctx.P1)):
                        sl = HK[:, h * 512 + k2 * 256:h * 512 + k2 * 256 + 256]
                        ctx.f(v, "V", v.tensor_tensor(ZF[:], sl, pb[:, 0:256], ALU.add))
                        ctx.inc(v.tensor_tensor(sl, ZF[:], LAMB[:, hs], ALU.mult), "V")
                        v.wait_ge(ctx.sem["V"], ctx.cnt["V"])
                    for m2 in range(2):
                        src = ctx.P2[:, 0:256] if m2 == 0 else ctx.PO[:, 1024:1280]
                        sl = HV[:, h * 512 + m2 * 256:h * 512 + m2 * 256 + 256]
                        ctx.f(v, "V", v.tensor_tensor(ZF[:], sl, src, ALU.add))
                        ctx.inc(v.tensor_scalar(sl, ZF[:], LAMT[:, 2 * h + m2:2 * h + m2 + 1],
                                                None, ALU.mult), "V")
                        v.wait_ge(ctx.sem["V"], ctx.cnt["V"])
                ctx.psum_guard["V"] = ctx.cnt["V"]

            sq_base = ctx.cnt["V"]

            @ctx.section("V")
            def _(v, osw_all=ctx.tok("S")):
                ctx.wait(v, osw_all)
                for dd in range(8):
                    v.tensor_tensor(SQB[:, dd * 128:(dd + 1) * 128],
                                    OSW[:, dd * 128:(dd + 1) * 128],
                                    OSW[:, dd * 128:(dd + 1) * 128], ALU.mult)\
                        .then_inc(ctx.sem["V"], 1)
            ctx.cnt["V"] += 8
            sq_tok = ctx.tok("V")

            @ctx.section("T")
            def _(t, sq_tok=sq_tok, g=ctx.guard_tokens()):
                ctx.wait(t, sq_tok)
                for tokk in g:
                    ctx.wait(t, tokk)
                for dd in range(8):
                    r = t.matmul(ctx.P2[0:1, 0:128], ctx.ccol(1.0),
                                 SQB[:, dd * 128:(dd + 1) * 128],
                                 start=(dd == 0), stop=(dd == 7))
                ctx.inc(r, "T")
            msq_T = ctx.tok("T")

            @ctx.section("S")
            def _(s, msq_T=msq_T):
                ctx.wait(s, msq_T)
                ctx.f(s, "S", s.activation(RSQR[:], ctx.P2[0:1, 0:128], AF.Ln,
                                           scale=1.0 / D, bias=ctx.ccol(1e-5)[0:1, :]))
                ctx.inc(s.activation(RSQR[:], RSQR[:], AF.Exp, scale=-0.5), "S")
            rsq_tok = ctx.tok("S")

            def rb_mm(t, i):
                return t.matmul(ctx.P2[0:128, 0:128], ctx.onesf[0:1, :], RSQR[0:1, :],
                                start=True, stop=True)

            def rb_ev(e, i):
                return ecopy(e, RSQB[:], ctx.P2[0:128, 0:128])
            rsb_tok = mm_pipeline(ctx, [(rb_mm, rb_ev)], evac="V", t_prewait=[rsq_tok])

            @ctx.section("V")
            def _(v, rsb_tok=rsb_tok):
                ctx.wait(v, rsb_tok)
                for dd in range(8):
                    v.scalar_tensor_tensor(ONC[:, dd * 128:(dd + 1) * 128],
                                           OSW[:, dd * 128:(dd + 1) * 128],
                                           GNWC[:, dd:dd + 1], RSQB[:],
                                           ALU.mult, ALU.mult).then_inc(ctx.sem["V"], 1)
            ctx.cnt["V"] += 8
            onc_tok = ctx.tok("V")

            yw = []
            for oc in range(2):
                def mk(oc=oc):
                    def mm(t, i):
                        po = ctx.PO[:, oc * 512:(oc + 1) * 512]
                        for dd in range(ND):
                            r = t.matmul(po, ONC[:, dd * 128:(dd + 1) * 128],
                                         woT[:, dd * 1024 + oc * 512:dd * 1024 + oc * 512 + 512],
                                         start=(dd == 0), stop=(dd == ND - 1))
                        return r

                    def ev(e, i):
                        po = ctx.PO[:, oc * 512:(oc + 1) * 512]
                        xsl = xt[:, oc * 512:(oc + 1) * 512]
                        return e.tensor_tensor(xsl, xsl, po, ALU.add)
                    return mm, ev
                yw.append(mk())
            pre_y = [onc_tok, gnc_tok, xld] + ([wo_tok] if c == 0 else [])
            y_tok = mm_pipeline(ctx, yw, evac="V", nbuf=2, t_prewait=pre_y)

            @ctx.section("Y")
            def _(sy, c=c, xt=xt, y_tok=y_tok):
                ctx.wait(sy, y_tok)
                sy.dma_start(XSP.ap()[c * 128:(c + 1) * 128, :], xt)\
                    .then_inc(ctx.sem["ST"], 16)
                ctx.cnt["ST"] += 16
            if c == 0:
                vtk = ctx.tok("V")
                for nm, ap in [("gh", GH[:]), ("egp", EGP[:]), ("egm", EGM[:]),
                               ("sp", SP[:]), ("qf", QF[:]), ("kf", KF[:]),
                               ("vt", VT[:]), ("osw", OSW[:]), ("hk", HK[:]),
                               ("hv", HV[:]), ("lamb", LAMB[:])]:
                    ctx.dump("g0" + nm, ap, vtk, hold="T")

        @ctx.section("Y")
        def _(sy):
            sy.wait_ge(ctx.sem["ST"], ctx.cnt["ST"])
            ctx.dma(sy, ctx.XRES[:].rearrange("p (a d) -> p a d", a=NT),
                    XSP.ap().rearrange("(a p) d -> p a d", p=128))
        ctx.x_tok = ctx.tok("LD")


def emit_l1_attn_a(ctx, inp):
    nc = ctx.nc
    ctx.barrier()
    es = contextlib.ExitStack()
    with es:
        WINt = ctx.sb("WINt", [128, 2 * 1024], BF16, es)
        TPACK = ctx.sb("TPACK", [64, 2048], F32, es)
        PPACK = ctx.sb("PPACK", [64, 2048], F32, es)
        DENP = ctx.sb("DENP", [64, 2048], F32, es)
        PWT = ctx.sb("PWT", [128, 512], F32, es)
        MQR = ctx.sb("MQR", [128, 512], BF16, es)
        MQI = ctx.sb("MQI", [128, 512], BF16, es)
        CARRE = ctx.sb("CARRE", [128, 66], F32, es)
        CARIM = ctx.sb("CARIM", [128, 66], F32, es)
        CARDN = ctx.sb("CARDN", [64, 1], F32, es)
        XTS = ctx.sb("XTS", [128, 512], F32, es)
        XTSB = XTS[:].bitcast(BF16)
        REN4 = XTSB[:, 0:512]
        IMN4 = XTSB[:, 512:1024]
        OHS = ctx.sb("OHS", [128, 1024], BF16, es)
        WREb = ctx.sb("WREb", [128, 32], BF16, es)
        WIMb = ctx.sb("WIMb", [128, 32], BF16, es)
        NSC = ctx.sb("NSC", [128, 2], F32, es)
        WCC = ctx.sb("WCC", [128, 68], F32, es)
        BCC = ctx.sb("BCC", [128, 17], F32, es)
        WLC = ctx.sb("WLC", [128, 3], F32, es)
        rowsA = ctx.sb("rowsA", [1, 416], F32, es)
        SSC = ctx.sb("SSC", [64, 1], F32, es)
        XSP2 = nc.dram_tensor("xsp_l1a", [T, D], F32)
        XVSP = nc.dram_tensor("xvsp_l1a", [T, D], F32)
        XRB2 = ctx.XRES[:].bitcast(BF16)
        THETB_F = ctx.XRES[:, 6144:10240]
        NUMR = XRB2[:, 4096:8192]
        NUMI = XRB2[:, 8192:12288]
        WOUT = XRB2[:, 12288:20480]
        Q1 = ctx.XRES[:, 10240:11264]
        POLY = ctx.XRES[:, 11264:12288]
        XV2 = ctx.XRES[:, 12288:13312]
        ZRAW = ctx.XRES[:, 13312:15360]
        CONV = ctx.LNJ[:, 0:2048]
        CONVB = XRB2[:, 30720:32768]
        xt = ctx.XRES[:, 15360:16384]

        sg0 = ctx.start_guard()

        @ctx.section("Y")
        def _(sy):
            for tokk in sg0:
                ctx.wait(sy, tokk)
            ctx.dma(sy, XSP2.ap().rearrange("(a p) d -> p a d", p=128),
                    ctx.XRES[:].rearrange("p (a d) -> p a d", a=NT))
            ctx.dma(sy, rowsA[:, 0:28], inp["L1_attn_a_decay_slopes"])
            ctx.dma(sy, rowsA[:, 28:32], inp["L1_attn_a_anchor_slopes"])
            ctx.dma(sy, rowsA[:, 32:64], inp["L1_attn_a_score_scale"])
            ctx.dma(sy, rowsA[:, 64:67], inp["L1_attn_a_deriv_logits"])
        rows_tok = ctx.tok("LD")
        spill_tok = rows_tok

        @ctx.section("Y")
        def _(sy):
            ctx.dma(sy, rowsA[:, 160:416], inp["L1_attn_a_norm_scale"])
        ns_tok = ctx.tok("LD")


        @ctx.section("G")
        def _(g):
            for tokk in sg0:
                ctx.wait(g, tokk)
            pass

        _, _, _, lntok = emit_ln(ctx, wait_tok=ctx.x_tok)
        xfm_tok = emit_transpose_to_xfm(ctx, lntok)

        WCR = ctx.XRES[0:4, 2048:4128]
        BCR = ctx.XRES[0:1, 4128:6208]

        @ctx.section("Y")
        def _(sy):
            ctx.wait(sy, spill_tok)
            ctx.wait(sy, lntok)
            ctx.dma(sy, WCR, inp["L1_attn_a_w_conv"])
            ctx.dma(sy, BCR, inp["L1_attn_a_b_conv"])
        wc_tok = ctx.tok("LD")

        ccw = []
        for ct in range(17):
            def mk(ct=ct):
                w = 128 if ct < 16 else 32

                def mm(t, i):
                    return t.transpose(ctx.P2[0:w, 0:4], WCR[:, ct * 128:ct * 128 + w],
                                       ctx.identf[0:4, 0:4])

                def ev(e, i):
                    return e.copy(WCC[0:w, ct * 4:ct * 4 + 4], ctx.P2[0:w, 0:4])
                return mm, ev
            ccw.append(mk())
        for ct in range(17):
            def mk(ct=ct):
                w = 128 if ct < 16 else 32

                def mm(t, i):
                    return t.transpose(ctx.P2[0:w, 0:1], BCR[:, ct * 128:ct * 128 + w],
                                       ctx.identf[0:1, 0:1])

                def ev(e, i):
                    return e.copy(BCC[0:w, ct:ct + 1], ctx.P2[0:w, 0:1])
                return mm, ev
            ccw.append(mk())
        def ss_mm1(t, i):
            return t.transpose(ctx.P2[0:28, 0:1], rowsA[0:1, 32:60], ctx.identf[0:1, 0:1])

        def ss_ev1(e, i):
            return e.copy(SSC[0:28, 0:1], ctx.P2[0:28, 0:1])

        def ss_mm2(t, i):
            return t.transpose(ctx.P2[0:4, 0:1], rowsA[0:1, 60:64], ctx.identf[0:1, 0:1])

        def ss_ev2(e, i):
            return e.copy(SSC[32:36, 0:1], ctx.P2[0:4, 0:1])

        def ns_mm1(t, i):
            return t.transpose(ctx.P2[0:128, 0:1], rowsA[0:1, 160:288], ctx.identf[0:1, 0:1])

        def ns_ev1(e, i):
            return e.copy(NSC[:, 0:1], ctx.P2[0:128, 0:1])

        def ns_mm2(t, i):
            return t.transpose(ctx.P2[0:128, 0:1], rowsA[0:1, 288:416], ctx.identf[0:1, 0:1])

        def ns_ev2(e, i):
            return e.copy(NSC[:, 1:2], ctx.P2[0:128, 0:1])
        ccw += [(ss_mm1, ss_ev1), (ss_mm2, ss_ev2), (ns_mm1, ns_ev1), (ns_mm2, ns_ev2)]
        cols_tok = mm_pipeline(ctx, ccw, evac="S", nbuf=1,
                               t_prewait=[wc_tok, rows_tok, ns_tok])
        ctx.dump("nsc0", NSC[:], cols_tok, hold=["S", "T", "V", "G"])
        ctx.dump("lnjns", ctx.LNJ[0:1, 0:256], cols_tok, hold=["S", "T", "V", "G"])

        @ctx.section("V")
        def _(v):
            ctx.wait(v, rows_tok)
            r = rowsA[0:1, 64:67]
            ctx.f(v, "V", v.tensor_reduce(rowsA[0:1, 70:71], r, AX.X, ALU.max))
            ctx.f(v, "V", v.tensor_scalar(rowsA[0:1, 71:72], rowsA[0:1, 70:71], -1.0, None, ALU.mult))
            ctx.inc(v.tensor_scalar(rowsA[0:1, 72:75], r, rowsA[0:1, 71:72], None, ALU.add), "V")
        wl1 = ctx.tok("V")

        @ctx.section("S")
        def _(s_):
            ctx.wait(s_, wl1)
            ctx.inc(s_.activation(rowsA[0:1, 72:75], rowsA[0:1, 72:75], AF.Exp), "S")
        wl2 = ctx.tok("S")

        @ctx.section("V")
        def _(v):
            ctx.wait(v, wl2)
            ctx.f(v, "V", v.tensor_reduce(rowsA[0:1, 75:76], rowsA[0:1, 72:75], AX.X, ALU.add))
            ctx.f(v, "V", v.reciprocal(rowsA[0:1, 76:77], rowsA[0:1, 75:76]))
            ctx.inc(v.tensor_scalar(rowsA[0:1, 72:75], rowsA[0:1, 72:75],
                                    rowsA[0:1, 76:77], None, ALU.mult), "V")
        wl3 = ctx.tok("V")

        def wl_mm(t, i):
            return t.matmul(ctx.P2[0:128, 0:3], ctx.onesf[0:1, :], rowsA[0:1, 72:75],
                            start=True, stop=True)

        def wl_ev(e, i):
            return e.copy(WLC[:], ctx.P2[0:128, 0:3])
        wl_tok = mm_pipeline(ctx, [(wl_mm, wl_ev)], evac="S", t_prewait=[wl3])

        @ctx.section("S")
        def _(s_):
            ctx.wait(s_, rows_tok)
            ctx.f(s_, "S", s_.activation(rowsA[0:1, 80:112], rowsA[0:1, 0:32], AF.Abs))
            ctx.f(s_, "S", s_.activation(rowsA[0:1, 80:112], rowsA[0:1, 80:112], AF.Exp, scale=-1.0))
            ctx.inc(s_.activation(rowsA[0:1, 80:112], rowsA[0:1, 80:112], AF.Ln,
                                  bias=ctx.ccol(1.0)[0:1, :]), "S")
        sp2 = ctx.tok("S")

        @ctx.section("V")
        def _(v):
            ctx.wait(v, sp2)
            ctx.f(v, "V", v.tensor_scalar(rowsA[0:1, 112:144], rowsA[0:1, 0:32], 0.0, None, ALU.max))
            ctx.inc(v.tensor_tensor(rowsA[0:1, 0:32], rowsA[0:1, 112:144],
                                    rowsA[0:1, 80:112], ALU.add), "V")
        sp_tok2 = ctx.tok("V")

        DROW = ctx.LNJ[0:1, 0:2048]
        IROWv = ctx.XRES[0:1, 10240:12288].bitcast(I32)

        for phase in range(2):
            @ctx.section("G")
            def _(g, phase=phase, pv=(ctx.tok("S"), ctx.tok("V"))):
                ctx.wait(g, pv[0])
                ctx.wait(g, pv[1])
                if phase == 0:
                    g.iota(IROWv, [[-1, 2048]], base=2047, channel_multiplier=0)
                else:
                    g.iota(IROWv, [[1, 2048]], base=0, channel_multiplier=0)
                g.wait_ge(ctx.sem["G"], ctx.cnt["G"]) if False else None
                ctx.inc(g.tensor_copy(DROW, IROWv), "G")
            io_tok = ctx.tok("G")

            tb0 = ctx.cnt["T"]
            sb0 = ctx.cnt["S"]
            twg = ctx.guard_tokens()
            ctx.cnt["T"] += 4
            ctx.cnt["S"] += 4
            ctx.psum_guard["S"] = ctx.cnt["S"]

            @ctx.section("T")
            def _(t, phase=phase, io_tok=io_tok, tb0=tb0, twg=twg):
                ctx.wait(t, io_tok)
                ctx.wait(t, sp_tok2)
                for tokk in twg:
                    ctx.wait(t, tokk)
                rows = rowsA[0:1, 0:28] if phase == 0 else rowsA[0:1, 28:32]
                for half in range(4):
                    nr = 28 if phase == 0 else 4
                    t.matmul(ctx.PO[0:nr, half * 512:(half + 1) * 512], rows,
                             DROW[0:1, half * 512:(half + 1) * 512], start=True, stop=True)\
                        .then_inc(ctx.sem["T"], 1)

            @ctx.section("S")
            def _(s_, phase=phase, tb0=tb0, sb0=sb0):
                nr = 28 if phase == 0 else 4
                r0 = 0 if phase == 0 else 32
                for half in range(4):
                    s_.wait_ge(ctx.sem["T"], tb0 + half + 1)
                    s_.activation(TPACK[r0:r0 + nr, half * 512:(half + 1) * 512],
                                  ctx.PO[0:nr, half * 512:(half + 1) * 512], AF.Exp,
                                  scale=-1.0).then_inc(ctx.sem["S"], 1)
        tw_tok = ctx.tok("S")

        def emit_ctile(ct, consumer):
            w = 128 if ct < 16 else 32

            @ctx.section("G")
            def _(g, ct=ct, tg=("T", ctx.cnt["T"])):
                ctx.wait(g, tg)
                ctx.dma(g, WINt[:, (ct % 2) * 1024:(ct % 2) * 1024 + 8 * w]
                        .rearrange("p (a c) -> p a c", a=8),
                        inp["L1_attn_a_W_in"][:, ct * 128:ct * 128 + w]
                        .rearrange("(a p) c -> p a c", p=128))
            wld = ctx.tok("LG")

            zw = []
            for tch in range(4):
                def mk(tch=tch, ct=ct, w=w):
                    def mm(t, i):
                        pb = ctx.P0 if i % 2 == 0 else ctx.P1
                        for dd in range(ND):
                            r = t.matmul(pb[:, 0:w] if False else pb[0:w, 0:512],
                                         WINt[:, (ct % 2) * 1024 + dd * w:(ct % 2) * 1024 + (dd + 1) * w],
                                         ctx.XF(dd)[:, tch * 512:(tch + 1) * 512],
                                         start=(dd == 0), stop=(dd == ND - 1))
                        return r

                    def ev(e, i):
                        pb = ctx.P0 if i % 2 == 0 else ctx.P1
                        return e.copy(ctx.LNJ[0:w, tch * 512:(tch + 1) * 512] if False
                                      else ZRAW[0:w, tch * 512:(tch + 1) * 512],
                                      pb[0:w, 0:512])
                    return mm, ev
                zw.append(mk())
            z_tok = mm_pipeline(ctx, zw, evac="S", t_prewait=[wld, xfm_tok, cols_tok, spill_tok])

            @ctx.section("V")
            def _(v, ct=ct, w=w, z_tok=z_tok):
                ctx.wait(v, z_tok)
                wc = lambda k: WCC[0:w, ct * 4 + k:ct * 4 + k + 1]
                bc = BCC[0:w, ct:ct + 1]
                ctx.f(v, "V", v.tensor_scalar(CONV[0:w, :], ZRAW[0:w, :], wc(3), bc, ALU.mult, ALU.add))
                ctx.f(v, "V", v.scalar_tensor_tensor(CONV[0:w, 1:2048], ZRAW[0:w, 0:2047],
                                                     wc(2), CONV[0:w, 1:2048], ALU.mult, ALU.add))
                ctx.f(v, "V", v.scalar_tensor_tensor(CONV[0:w, 2:2048], ZRAW[0:w, 0:2046],
                                                     wc(1), CONV[0:w, 2:2048], ALU.mult, ALU.add))
                ctx.inc(v.scalar_tensor_tensor(CONV[0:w, 3:2048], ZRAW[0:w, 0:2045],
                                               wc(0), CONV[0:w, 3:2048], ALU.mult, ALU.add), "V")
            conv_tok = ctx.tok("V")
            consumer(ct, conv_tok)


        xv_st = []

        def xval_consumer(ct, conv_tok):
            ld0 = ctx.cnt["LD"]
            s0 = ctx.cnt["S"]
            tw2 = []
            for tt in range(NT):
                def mk(tt=tt, ld0=ld0):
                    def mm(t, i):
                        pb = ctx.P0 if i % 2 == 0 else ctx.P1
                        return t.transpose(pb[0:128, 0:128], CONV[:, tt * 128:(tt + 1) * 128],
                                           ctx.identf[:])

                    def ev(e, i):
                        g = i // 2
                        if i % 2 == 0 and g >= 2:
                            e.wait_ge(ctx.sem["LD"], ld0 + 16 * (g - 1))
                        pb = ctx.P0 if i % 2 == 0 else ctx.P1
                        o = (g % 2) * 256 + (i % 2) * 128
                        return e.copy(XTS[:, o:o + 128], pb[0:128, 0:128])
                    return mm, ev
                tw2.append(mk())
            prew = [conv_tok] + ([xv_st[-1]] if xv_st else [])
            mm_pipeline(ctx, tw2, evac="S", t_prewait=prew)

            for g in range(8):
                @ctx.section("Y")
                def _(sy, ct=ct, g=g, s0=s0):
                    sy.wait_ge(ctx.sem["S"], s0 + 2 * (g + 1))
                    ctx.dma(sy, XVSP.ap()[g * 256:(g + 1) * 256, ct * 128:(ct + 1) * 128]
                            .rearrange("(a p) f -> p a f", p=128),
                            XTS[:, (g % 2) * 256:(g % 2) * 256 + 256]
                            .rearrange("p (a f) -> p a f", f=128))
            xv_st.append(ctx.tok("LD"))

        def sraw_consumer(ct, conv_tok):
            @ctx.section("G")
            def _(g, conv_tok=conv_tok):
                ctx.wait(g, conv_tok)
                ctx.dma(g, PPACK[32:36, :], CONV[28:32, :])
            sra_tok = ctx.tok("LG")

            @ctx.section("V")
            def _(v, conv_tok=conv_tok, sra_tok=sra_tok):
                ctx.wait(v, conv_tok)
                ctx.wait(v, sra_tok)
                ctx.f(v, "V", v.tensor_copy(PPACK[0:28, :], CONV[0:28, :]))
                ctx.f(v, "V", v.tensor_scalar(PPACK[0:28, :], PPACK[0:28, :],
                                              SSC[0:28, 0:1], None, ALU.mult))
                ctx.f(v, "V", v.tensor_scalar(PPACK[32:36, :], PPACK[32:36, :],
                                              SSC[32:36, 0:1], None, ALU.mult))
                ctx.f(v, "V", v.tensor_scalar(PPACK[0:28, :], PPACK[0:28, :], 20.0, -20.0,
                                              ALU.min, ALU.max))
                ctx.inc(v.tensor_scalar(PPACK[32:36, :], PPACK[32:36, :], 20.0, -20.0,
                                        ALU.min, ALU.max), "V")
            pcl = ctx.tok("V")

            @ctx.section("S")
            def _(s_, pcl=pcl):
                ctx.wait(s_, pcl)
                s_.activation(PPACK[0:28, :], PPACK[0:28, :], AF.Exp)
                ctx.inc(s_.activation(PPACK[32:36, :], PPACK[32:36, :], AF.Exp), "S")
            pe = ctx.tok("S")

            @ctx.section("V")
            def _(v, pe=pe):
                ctx.wait(v, pe)
                ctx.wait(v, tw_tok)
                ctx.f(v, "V", v.tensor_tensor(PPACK[0:28, :], PPACK[0:28, :],
                                              TPACK[0:28, :], ALU.mult))
                ctx.inc(v.tensor_tensor(PPACK[32:36, :], PPACK[32:36, :],
                                        TPACK[32:36, :], ALU.mult), "V")
            pw_tok = ctx.tok("V")

            pww = []
            for tt in range(NT):
                def mk(tt=tt):
                    def mm(t, i):
                        return t.transpose(ctx.P2[0:128, 0:28],
                                           PPACK[0:28, tt * 128:(tt + 1) * 128],
                                           ctx.identf[0:28, 0:28])

                    def ev(e, i):
                        return e.copy(PWT[:, tt * 32:tt * 32 + 28], ctx.P2[0:128, 0:28])
                    return mm, ev
                pww.append(mk())
            for tt in range(NT):
                def mk(tt=tt):
                    def mm(t, i):
                        return t.transpose(ctx.P2[0:128, 0:4],
                                           PPACK[32:36, tt * 128:(tt + 1) * 128],
                                           ctx.identf[32:36, 32:36])

                    def ev(e, i):
                        return e.copy(PWT[:, tt * 32 + 28:tt * 32 + 32], ctx.P2[0:128, 0:4])
                    return mm, ev
                pww.append(mk())
            mm_pipeline(ctx, pww, evac="S", nbuf=1, t_prewait=[pw_tok])

        for ct in list(range(8)) + [16]:
            emit_ctile(ct, xval_consumer if ct < 8 else sraw_consumer)
        xvt_tok = (ctx.tok("S"), ctx.tok("V"))

        ctx.dump("pwt", PWT[:], xvt_tok[1], hold="G")
        ctx.dump("tpk", TPACK[:], xvt_tok[1], hold="G")
        ctx.dump("ppk", PPACK[:], xvt_tok[1], hold="G")
        SELS = TPACK[:].bitcast(BF16)[0:32, 0:4096]
        ISELv = ctx.XRES[0:32, 2048:6144].bitcast(I32)
        IOHv = ctx.XRES[0:128, 11264:12288].bitcast(I32)

        @ctx.section("G")
        def _(g, pv=(ctx.tok("S"), ctx.tok("V"))):
            ctx.wait(g, pv[0])
            ctx.wait(g, pv[1])
            g.iota(ISELv, [[1, 32], [0, 128]], base=0, channel_multiplier=-1)
            g.wait_ge(ctx.sem["G"], ctx.cnt["G"])
            g.tensor_single_scalar(SELS, ISELv, 0, ALU.is_equal)
            g.iota(IOHv, [[-1, 32], [1, 32]], base=0, channel_multiplier=0)
            ctx.inc(g.tensor_single_scalar(OHS[:], IOHv, 0, ALU.is_equal), "G")
        sel_tok = ctx.tok("G")
        @ctx.section("Y")
        def _(sy):
            ctx.wait(sy, cols_tok)
            ctx.wait(sy, spill_tok)
            ctx.dma(sy, THETB_F[:],
                    inp["L1_attn_a_theta"].rearrange("(o k) f -> o (k f)", o=1)
                    .broadcast_to((128, 4096)))
        th_tok = ctx.tok("LD")


        TRIFF = ctx.sb("TRIFF", [128, 128], F32, es)
        SINB = ctx.sb("SINB", [128, 512], BF16, es)
        COSB = ctx.sb("COSB", [128, 512], BF16, es)
        FPB = ctx.sb("FPB", [64, 2048], BF16, es)
        MSQT = ctx.II[:].bitcast(F32)[0:32, 0:128]
        WREf = MQR[:].bitcast(F32)[:, 0:64]

        @ctx.section("Y")
        def _(sy):
            ctx.dma(sy, WREf[:, 0:32], inp["L1_attn_a_W_re"])
            ctx.dma(sy, WREf[:, 32:64], inp["L1_attn_a_W_im"])
        wre_ld = ctx.tok("LD")

        @ctx.section("S")
        def _(s_):
            ctx.wait(s_, wre_ld)
            s_.copy(WREb[:], WREf[:, 0:32])
            ctx.inc(s_.copy(WIMb[:], WREf[:, 32:64]), "S")
        wre_tok = ctx.tok("S")

        @ctx.section("G")
        def _(g):
            gi3 = ctx.II[:, 128:256]
            g.iota(gi3, [[1, 128]], base=0, channel_multiplier=-1)
            g.tensor_single_scalar(TRIFF[:], gi3, 0, ALU.is_ge)
            g.memset(CARRE[:], 0.0)
            g.memset(CARIM[:], 0.0)
            ctx.inc(g.memset(CARDN[:], 0.0), "G")
        carr_init = ctx.tok("G")

        @ctx.section("V")
        def _(v):
            ctx.wait(v, wl_tok)
            ctx.inc(v.tensor_scalar(WLC[:, 2:3], WLC[:, 2:3], -1.0, None, ALU.mult), "V")
        wlneg_tok = ctx.tok("V")

        ld_xv = []
        xvf_done = []
        prev_msq = [None]
        prev_perdd = [None, None]
        for tt in range(NT):
            tcol = slice(tt * 128, (tt + 1) * 128)

            @ctx.section("Y")
            def _(sy, tt=tt):
                if tt == 0:
                    for tk in xv_st:
                        ctx.wait(sy, tk)
                if tt >= 2:
                    ctx.wait(sy, xvf_done[tt - 2])
                ctx.dma(sy, ctx.XRES[:, (tt % 2) * 1024:(tt % 2) * 1024 + 1024],
                        XVSP.ap()[tt * 128:(tt + 1) * 128, :])
            ld_xv.append(ctx.tok("LD"))

            @ctx.section("T")
            def _(t, tt=tt, g=ctx.guard_tokens(), pw=xvt_tok, pm=prev_msq[0]):
                if tt == 0:
                    ctx.wait(t, pw[0])
                    ctx.wait(t, pw[1])
                    ctx.wait(t, carr_init)
                    ctx.wait(t, wre_tok)
                    ctx.wait(t, wlneg_tok)
                    ctx.wait(t, sel_tok)
                    ctx.wait(t, th_tok)
                    for tokk in g:
                        ctx.wait(t, tokk)
                else:
                    ctx.wait(t, pm)
                ctx.inc(t.matmul(ctx.P2[0:32, 0:128], PWT[:, tt * 32:(tt + 1) * 32],
                                 TRIFF[:], start=True, stop=True), "T")
            dc_T = ctx.tok("T")

            @ctx.section("V")
            def _(v, tt=tt, dc_T=dc_T):
                ctx.wait(v, dc_T)
                ctx.f(v, "V", v.tensor_scalar(DENP[0:32, tcol], ctx.P2[0:32, 0:128],
                                              CARDN[0:32, 0:1], None, ALU.add))
                ctx.inc(v.tensor_copy(CARDN[0:32, 0:1], DENP[0:32, tt * 128 + 127:tt * 128 + 128]), "V")
            den_tok = ctx.tok("V")

            ZRB = ZRAW[:].bitcast(BF16)
            SINP = ZRB[:, 0:1024]
            COSP = ZRB[:, 1024:2048]
            MQRP = ZRB[:, 2048:3072]
            MQIP = ZRB[:, 3072:4096]
            XV2B = ctx.XRES[:, 15360:16384]
            Q1B = ctx.LNJ[:, 0:1024]
            POLYB = ctx.LNJ[:, 1024:2048]
            sb0 = ctx.cnt["S"]
            ctx.cnt["S"] += 8
            ab_toks = [None] * 4
            cst_toks = [None] * 4
            mq_toks = [None] * 4
            cs_drain = [None, None]

            def emit_rr(pp, tt=tt):
                par = pp % 2
                xvp = ctx.XRES[:, (tt % 2) * 1024 + pp * 256:(tt % 2) * 1024 + (pp + 1) * 256]
                XVq = XV2 if par == 0 else XV2B
                POq = POLY if par == 0 else POLYB
                Q1q = Q1 if par == 0 else Q1B

                @ctx.section("V")
                def _(v, tt=tt, pp=pp, xvp=xvp, XVq=XVq, POq=POq, Q1q=Q1q):
                    if pp == 0:
                        ctx.wait(v, den_tok)
                        ctx.wait(v, ld_xv[tt])
                    ctx.f(v, "V", v.tensor_tensor(XVq[:, 0:256], xvp, xvp, ALU.mult))
                    ctx.f(v, "V", v.tensor_scalar(POq[:, 0:256], xvp, WLC[:, 1:2],
                                                  WLC[:, 0:1], ALU.mult, ALU.add))
                    ctx.f(v, "V", v.scalar_tensor_tensor(POq[:, 0:256], XVq[:, 0:256],
                                                         WLC[:, 2:3], POq[:, 0:256],
                                                         ALU.mult, ALU.add))
                    pwv = PWT[:, tt * 32 + pp * 8:tt * 32 + (pp + 1) * 8]
                    ctx.f(v, "V", v.tensor_tensor(
                        Q1q.rearrange("p (a b c) -> p a b c", a=8, b=32),
                        POq[:, 0:256].rearrange("p (a b) -> p a b", a=8)
                        .rearrange("p a (b o) -> p a b o", o=1)
                        .broadcast_to((128, 8, 32, 4)),
                        pwv.rearrange("p (a o) -> p a o", o=1)
                        .rearrange("p (a z) o -> p a z o", z=1)
                        .broadcast_to((128, 8, 32, 4)), ALU.mult))
                    ctx.f(v, "V", v.tensor_tensor(
                        XVq.rearrange("p (a b c) -> p a b c", a=8, b=32),
                        xvp.rearrange("p (a b) -> p a b", a=8)
                        .rearrange("p a (b o) -> p a b o", o=1)
                        .broadcast_to((128, 8, 32, 4)),
                        THETB_F[:, pp * 1024:(pp + 1) * 1024]
                        .rearrange("p (a b c) -> p a b c", a=8, b=32), ALU.mult))
                    ctx.f(v, "V", v.tensor_scalar(POq[:], XVq[:], 1.0 / (2 * PI),
                                                  None, ALU.mult))
                    ctx.f(v, "V", v.tensor_copy(ctx.II[:, 0:512], POq[:, 0:512]))
                    ctx.f(v, "V", v.tensor_copy(POq[:, 0:512], ctx.II[:, 0:512]))
                    ctx.f(v, "V", v.tensor_copy(ctx.II[:, 0:512], POq[:, 512:1024]))
                    ctx.f(v, "V", v.tensor_copy(POq[:, 512:1024], ctx.II[:, 0:512]))
                    ctx.f(v, "V", v.scalar_tensor_tensor(XVq[:], POq[:], -2 * PI,
                                                         XVq[:], ALU.mult, ALU.add))
                    ctx.f(v, "V", v.tensor_scalar(POq[:].bitcast(I32), XVq[:].bitcast(I32),
                                                  0x7FFFFFFF, None, ALU.bitwise_and))
                    ctx.inc(v.tensor_scalar(POq[:], POq[:], -1.0, PI / 2,
                                            ALU.mult, ALU.add), "V")
                ab_toks[pp] = ctx.tok("V")

            def emit_mq(pp, tt=tt):
                par = pp % 2
                Q1q = Q1 if par == 0 else Q1B

                @ctx.section("V")
                def _(v, pp=pp, Q1q=Q1q):
                    v.wait_ge(ctx.sem["S"], sb0 + 2 * (pp + 1))
                    if pp >= 1:
                        ctx.wait(v, cst_toks[pp - 1])
                    v.tensor_tensor(MQRP[:], Q1q, COSP[:], ALU.mult)\
                        .then_inc(ctx.sem["V"], 1)
                    ctx.cnt["V"] += 1
                    ctx.inc(v.tensor_tensor(MQIP[:], Q1q, SINP[:], ALU.mult), "V")
                mq_toks[pp] = ctx.tok("V")
                mq_tok = mq_toks[pp]

                t0b = ctx.cnt["T"]
                v0b = ctx.cnt["V"]
                ctx.cnt["T"] += 4
                ctx.cnt["V"] += 8
                cst_toks[pp] = ("T", t0b + 4)
                cs_drain[0] = cs_drain[1]
                cs_drain[1] = ("V", v0b + 8)
                banks = [ctx.P0, ctx.P1, ctx.P2]

                @ctx.section("T")
                def _(t, tt=tt, pp=pp, mq_tok=mq_tok, t0b=t0b, v0b=v0b,
                      pv=cs_drain[0], g=ctx.guard_tokens()):
                    ctx.wait(t, mq_tok)
                    if pv is not None:
                        ctx.wait(t, pv)
                    for tokk in g:
                        ctx.wait(t, tokk)
                    for g4 in range(4):
                        im, half = g4 % 2, g4 // 2
                        src = MQIP if im else MQRP
                        pb = banks[g4 % 3]
                        if g4 >= 3:
                            t.wait_ge(ctx.sem["V"], v0b + 2)
                        for j in range(4):
                            cb2 = half * 4 + j
                            r = t.matmul(pb[:, j * 128:(j + 1) * 128],
                                         src[:, cb2 * 128:(cb2 + 1) * 128],
                                         ctx.tri01[:], start=True, stop=True)
                        r.then_inc(ctx.sem["T"], 1)

                @ctx.section("V")
                def _(v, tt=tt, pp=pp, t0b=t0b, v0b=v0b):
                    for g4 in range(4):
                        im, half = g4 % 2, g4 // 2
                        pb = banks[g4 % 3]
                        dstb = NUMI if im else NUMR
                        blk0 = pp * 8 + half * 4
                        carrs = (CARIM if im else CARRE)[:, blk0:blk0 + 4]
                        pbv = pb[:, 0:512].rearrange("p (c t) -> p c t", t=128)
                        v.wait_ge(ctx.sem["T"], t0b + g4 + 1)
                        v.tensor_tensor(
                            dstb[:, blk0 * 128:(blk0 + 4) * 128]
                            .rearrange("p (c t) -> p c t", t=128),
                            pbv,
                            carrs.rearrange("p (c o) -> p c o", o=1)
                            .broadcast_to((128, 4, 128)), ALU.add)\
                            .then_inc(ctx.sem["V"], 1)
                        v.tensor_tensor(
                            carrs.rearrange("p (c o) -> p c o", o=1),
                            carrs.rearrange("p (c o) -> p c o", o=1),
                            pbv[:, :, 127:128], ALU.add)\
                            .then_inc(ctx.sem["V"], 1)

            for pp in range(4):
                emit_rr(pp)
                if pp >= 1:
                    emit_mq(pp - 1)

            @ctx.section("S")
            def _(s_, ab=list(ab_toks), mq=list(mq_toks), tt=tt, sb0=sb0):
                for pp in range(4):
                    par = pp % 2
                    XVq = XV2 if par == 0 else XV2B
                    POq = POLY if par == 0 else POLYB
                    ctx.wait(s_, ab[pp])
                    if pp >= 1:
                        ctx.wait(s_, mq[pp - 1])
                    s_.activation(SINP[:], XVq[:], AF.Sin).then_inc(ctx.sem["S"], 1)
                    s_.activation(COSP[:], POq[:], AF.Sin).then_inc(ctx.sem["S"], 1)

            emit_mq(3)
            cs_tok = ctx.tok("V")
            xvf_done.append(cs_tok)
            ctx.psum_guard["V"] = ctx.cnt["V"]

            @ctx.section("V")
            def _(v, tt=tt, cs_tok=cs_tok):
                ctx.wait(v, cs_tok)
                ctx.f(v, "V", v.tensor_scalar(DENP[0:32, tcol], DENP[0:32, tcol],
                                              1e-4, None, ALU.max))
                ctx.inc(v.reciprocal(DENP[0:32, tcol], DENP[0:32, tcol]), "V")
            inv_tok = ctx.tok("V")

            t0c = ctx.cnt["T"]
            v0c = ctx.cnt["V"]
            ctx.cnt["T"] += 16
            ctx.cnt["V"] += 16

            @ctx.section("V")
            def _(v, tt=tt, t0c=t0c, v0c=v0c, cs_tok=cs_tok):
                ctx.wait(v, cs_tok)
                for i in range(16):
                    grp, im = i // 2, i % 2
                    srcb = (NUMI if im else NUMR)[:, grp * 512:(grp + 1) * 512]
                    dstq = MQRP[:, 0:512] if i % 2 == 0 else MQIP[:, 0:512]
                    if i >= 2:
                        v.wait_ge(ctx.sem["T"], t0c + i - 1)
                    v.tensor_tensor(dstq[:], srcb, srcb, ALU.mult)\
                        .then_inc(ctx.sem["V"], 1)

            @ctx.section("T")
            def _(t, tt=tt, t0c=t0c, v0c=v0c, den_tok=den_tok):
                ctx.wait(t, den_tok)
                for i in range(16):
                    grp = i // 2
                    srcq = MQRP[:, 0:512] if i % 2 == 0 else MQIP[:, 0:512]
                    t.wait_ge(ctx.sem["V"], v0c + i + 1)
                    for j in range(4):
                        kk = grp * 4 + j
                        r = t.matmul(ctx.P2[0:32, 0:128], OHS[:, kk * 32:(kk + 1) * 32],
                                     srcq[:, j * 128:(j + 1) * 128],
                                     start=(i == 0 and j == 0), stop=(i == 15 and j == 3))
                    r.then_inc(ctx.sem["T"], 1)
            s2_T = ctx.tok("T")

            @ctx.section("V")
            def _(v, tt=tt, s2_T=s2_T, inv_tok=inv_tok):
                ctx.wait(v, s2_T)
                ctx.wait(v, inv_tok)
                ctx.f(v, "V", v.tensor_tensor(MSQT[:], DENP[0:32, tcol], DENP[0:32, tcol],
                                              ALU.mult))
                ctx.inc(v.tensor_tensor(MSQT[:], MSQT[:], ctx.P2[0:32, 0:128], ALU.mult), "V")
            msq_V = ctx.tok("V")
            prev_msq[0] = msq_V

            @ctx.section("S")
            def _(s_, tt=tt, msq_V=msq_V):
                ctx.wait(s_, msq_V)
                ctx.f(s_, "S", s_.activation(MSQT[:], MSQT[:], AF.Ln, scale=1.0 / 256.0,
                                             bias=ctx.ccol(1e-5)[0:32, :]))
                ctx.inc(s_.activation(MSQT[:], MSQT[:], AF.Exp, scale=-0.5), "S")
            rsq_S = ctx.tok("S")

            @ctx.section("V")
            def _(v, tt=tt, rsq_S=rsq_S):
                ctx.wait(v, rsq_S)
                ctx.inc(v.tensor_tensor(FPB[0:32, tcol], DENP[0:32, tcol], MSQT[:],
                                        ALU.mult), "V")
            f_tok = ctx.tok("V")

            sb2 = ctx.cnt["S"]
            ctx.cnt["S"] += 32
            yk_toks = []
            for dd in range(8):
                @ctx.section("T")
                def _(t, dd=dd, f_tok=f_tok, pv=prev_perdd[0], ps=prev_perdd[1],
                      tt=tt, sb2=sb2):
                    ctx.wait(t, f_tok)
                    if dd == 0 and pv is not None:
                        ctx.wait(t, pv)
                    for j in range(4):
                        kk = dd * 4 + j
                        r = t.matmul(ctx.PO[0:128, j * 128:(j + 1) * 128],
                                     SELS[:, kk * 128:(kk + 1) * 128],
                                     FPB[0:32, tcol], start=True, stop=True)
                    ctx.inc(r, "T")
                fb_T = ctx.tok("T")

                @ctx.section("V")
                def _(v, dd=dd, fb_T=fb_T):
                    ctx.wait(v, fb_T)
                    v.scalar_tensor_tensor(REN4[:], NUMR[:, dd * 512:(dd + 1) * 512],
                                           NSC[:, 0:1], ctx.PO[0:128, 0:512],
                                           ALU.mult, ALU.mult).then_inc(ctx.sem["V"], 1)
                    ctx.cnt["V"] += 1
                    ctx.inc(v.scalar_tensor_tensor(IMN4[:], NUMI[:, dd * 512:(dd + 1) * 512],
                                                   NSC[:, 1:2], ctx.PO[0:128, 0:512],
                                                   ALU.mult, ALU.mult), "V")
                ri_V = ctx.tok("V")

                @ctx.section("T")
                def _(t, dd=dd, ri_V=ri_V, ps=prev_perdd[1], sb2=sb2):
                    ctx.wait(t, ri_V)
                    if dd == 0:
                        if ps is not None:
                            ctx.wait(t, ps)
                    else:
                        t.wait_ge(ctx.sem["S"], sb2 + 4 * dd)
                    for j in range(4):
                        t.matmul(ctx.PO[0:32, 512 + j * 128:512 + (j + 1) * 128],
                                 WREb[:], REN4[:, j * 128:(j + 1) * 128],
                                 start=True, stop=False)
                        r = t.matmul(ctx.PO[0:32, 512 + j * 128:512 + (j + 1) * 128],
                                     WIMb[:], IMN4[:, j * 128:(j + 1) * 128],
                                     start=False, stop=True)
                    ctx.inc(r, "T")
                yk_toks.append(ctx.tok("T"))

            @ctx.section("S")
            def _(s_, tt=tt, yk=list(yk_toks), sb2=sb2):
                for dd in range(8):
                    ctx.wait(s_, yk[dd])
                    for j in range(4):
                        s_.copy(ctx.XLNB[j * 32:(j + 1) * 32,
                                         tt * 1024 + dd * 128:tt * 1024 + (dd + 1) * 128],
                                ctx.PO[0:32, 512 + j * 128:512 + (j + 1) * 128])\
                            .then_inc(ctx.sem["S"], 1)
            prev_perdd[1] = ("S", sb2 + 32)
            prev_perdd[0] = ctx.tok("V")
            ctx.psum_guard["S"] = ctx.cnt["S"]
            ctx.psum_guard["V"] = ctx.cnt["V"]

        ctx.dump("sels", TPACK[:], ctx.tok("G"), hold="G")
        ctx.dump("ych", ctx.XLNB[:], ctx.tok("S"), hold=["S", "V", "T", "G"])
        ctx.dump("denp", DENP[:], ctx.tok("V"), hold="V")
        ctx.dump("fpb", FPB[:], ctx.tok("V"), hold="V")
        ctx.dump("numr", NUMR, ctx.tok("V"), hold="V")
        ctx.dump("numi", NUMI, ctx.tok("V"), hold="V")
        def gate_consumer(ct, conv_tok):
            dd = ct - 8

            @ctx.section("S")
            def _(s_, conv_tok=conv_tok):
                ctx.wait(s_, conv_tok)
                ctx.inc(s_.activation(CONVB[:], CONV[:], AF.Silu), "S")
            sg_tok = ctx.tok("S")

            @ctx.section("V")
            def _(v, dd=dd, sg_tok=sg_tok):
                ctx.wait(v, sg_tok)
                yv = ctx.XLNB[:, dd * 128:dd * 128 + 15 * 1024 + 128]\
                    .rearrange("p (a c) -> p a c", c=128) if False else \
                    ctx.XLNB[:].rearrange("p (a c) -> p a c", c=1024)[:, :, dd * 128:(dd + 1) * 128]
                cv = CONVB[:].rearrange("p (a c) -> p a c", c=128)
                ctx.inc(v.tensor_tensor(yv, yv, cv, ALU.mult), "V")
        for ct in range(8, 16):
            emit_ctile(ct, gate_consumer)
        yg_tok = ctx.tok("V")

        @ctx.section("G")
        def _(g):
            ctx.wait(g, ctx.tok("T"))
            ctx.wait(g, spill_tok)
            ctx.dma(g, WOUT.rearrange("p (a c) -> p a c", a=8),
                    inp["L1_attn_a_W_out"].rearrange("(a p) c -> p a c", p=128))
        wo_ld = ctx.tok("LG")

        for tt in range(NT):
            @ctx.section("Y")
            def _(sy, tt=tt, yg_tok=yg_tok, pv=ctx.tok("V")):
                ctx.wait(sy, spill_tok)
                ctx.wait(sy, yg_tok if tt == 0 else pv)
                ctx.dma(sy, xt, XSP2.ap()[tt * 128:(tt + 1) * 128, :])
            xld = ctx.tok("LD")

            ow = []
            for oc in range(2):
                def mk(tt=tt, oc=oc):
                    def mm(t, i):
                        pb = ctx.P0 if i % 2 == 0 else ctx.P1
                        for dd in range(ND):
                            r = t.matmul(pb[:], ctx.XLNB[:, tt * 1024 + dd * 128:tt * 1024 + (dd + 1) * 128],
                                         WOUT[:, dd * 1024 + oc * 512:dd * 1024 + oc * 512 + 512],
                                         start=(dd == 0), stop=(dd == ND - 1))
                        return r

                    def ev(e, i):
                        pb = ctx.P0 if i % 2 == 0 else ctx.P1
                        xsl = xt[:, oc * 512:(oc + 1) * 512]
                        return e.tensor_tensor(xsl, xsl, pb[:], ALU.add)
                    return mm, ev
                ow.append(mk())
            pre_o = [xld, yg_tok] + ([wo_ld] if tt == 0 else [])
            yo_tok = mm_pipeline(ctx, ow, evac="V", t_prewait=pre_o)

            @ctx.section("Y")
            def _(sy, tt=tt, yo_tok=yo_tok):
                ctx.wait(sy, yo_tok)
                sy.dma_start(XSP2.ap()[tt * 128:(tt + 1) * 128, :], xt)\
                    .then_inc(ctx.sem["ST"], 16)
                ctx.cnt["ST"] += 16

        @ctx.section("Y")
        def _(sy):
            sy.wait_ge(ctx.sem["ST"], ctx.cnt["ST"])
            ctx.dma(sy, ctx.XRES[:].rearrange("p (a d) -> p a d", a=NT),
                    XSP2.ap().rearrange("(a p) d -> p a d", p=128))
        ctx.x_tok = ctx.tok("LD")


def emit_l1_ffn(ctx, inp, state):
    ctx.barrier()
    es = contextlib.ExitStack()
    with es:
        if "wqT" not in state:
            state["wqT"] = ctx.sb("wqT1", [128, 8192], BF16, es)
            state["wsb"] = ctx.sb("wsb1", [128, D], F32, es)
            state["bias_bc"] = ctx.sb("biasbc1", [128, D], F32, es)
            state["wsdr"] = ctx.nc.dram_tensor(f"wsdr{ctx.uid}", [8, 128], F32)
        II3 = ctx.XLNB[:].bitcast(I32)[:, 4096:5120]
        LNJ2 = ctx.XLNB[:].bitcast(F32)[:, 5120:6144]

        if "w_done" not in state:
            wf = ctx.sb("wf", [128, 4096], F32, es)
            wq8 = ctx.XLNB[:, 0:8192]
            wcol = ctx.sb("wcol", [128, 32], F32, es)
            WS8 = ctx.sb("WS8", [8, 128], F32, es)
            wsrc = inp["L1_ffn_weight"].rearrange("(a p) c -> p a c", p=128)
            sg0 = ctx.start_guard()

            @ctx.section("Y")
            def _(sy):
                for tokk in sg0:
                    ctx.wait(sy, tokk)
                ctx.dma(sy, state["bias_bc"][:],
                        inp["L1_ffn_bias"].broadcast_to((128, D)))
                ctx.dma(sy, wf[:].rearrange("p (a c) -> p a c", a=4), wsrc[:, 0:4, :])
            wld0 = ctx.tok("LD")

            @ctx.section("V")
            def _(v):
                ctx.wait(v, wld0)
                for d in range(4):
                    ctx.inc(v.tensor_reduce(wcol[:, d:d + 1], wf[:, d * 1024:(d + 1) * 1024],
                                            AX.X, ALU.max, apply_absolute_value=True), "V")
                v.wait_ge(ctx.sem["V"], ctx.cnt["V"])
            wv0 = ctx.tok("V")

            @ctx.section("Y")
            def _(sy):
                ctx.wait(sy, wv0)
                ctx.dma(sy, wf[:].rearrange("p (a c) -> p a c", a=4), wsrc[:, 4:8, :])
            wld1 = ctx.tok("LD")

            @ctx.section("V")
            def _(v):
                ctx.wait(v, wld1)
                for d in range(4):
                    ctx.inc(v.tensor_reduce(wcol[:, 4 + d:5 + d], wf[:, d * 1024:(d + 1) * 1024],
                                            AX.X, ALU.max, apply_absolute_value=True), "V")
                v.wait_ge(ctx.sem["V"], ctx.cnt["V"])
                ctx.f(v, "V", v.tensor_scalar(wcol[:, 8:16], wcol[:, 0:8], 1e-8, None, ALU.max))
                ctx.f(v, "V", v.reciprocal(wcol[:, 16:24], wcol[:, 8:16]))
                ctx.f(v, "V", v.tensor_scalar(wcol[:, 16:24], wcol[:, 16:24], 127.0, None, ALU.mult))
                ctx.inc(v.tensor_scalar(wcol[:, 24:32], wcol[:, 8:16], 1.0 / 16129.0,
                                        None, ALU.mult), "V")
            wv_tok = ctx.tok("V")

            def ws_mm(t, i):
                return t.transpose(ctx.P2[0:8, 0:128], wcol[:, 24:32], ctx.identf[:])

            def ws_ev(e, i):
                return e.copy(WS8[:], ctx.P2[0:8, 0:128])
            ws_tok = mm_pipeline(ctx, [(ws_mm, ws_ev)], evac="S", t_prewait=[wv_tok])

            @ctx.section("Y")
            def _(sy):
                ctx.wait(sy, ws_tok)
                ctx.dma(sy, state["wsdr"].ap(), WS8[:])
            wsd_tok = ctx.tok("LD")

            @ctx.section("Y")
            def _(sy):
                ctx.wait(sy, wsd_tok)
                ctx.dma(sy, state["wsb"][:],
                        state["wsdr"].ap().rearrange("(o a) j -> o (a j)", o=1)
                        .broadcast_to((128, D)))
            wsb_tok = ctx.tok("LD")

            for half in range(2):
                @ctx.section("Y")
                def _(sy, half=half, vg=ctx.tok("V")):
                    ctx.wait(sy, vg)
                    ctx.dma(sy, wf[:].rearrange("p (a c) -> p a c", a=4),
                            wsrc[:, 4 * half:4 * half + 4, :])
                wldq = ctx.tok("LD")

                @ctx.section("V")
                def _(v, half=half, wldq=wldq, tguard=("T", ctx.cnt["T"])):
                    ctx.wait(v, wldq)
                    ctx.wait(v, tguard)
                    for hh in range(4):
                        a = half * 4 + hh
                        sl = slice(a * 1024, (a + 1) * 1024)
                        lsl = slice(hh * 1024, (hh + 1) * 1024)
                        ctx.f(v, "V", v.tensor_scalar(LNJ2[:, 0:1024], wf[:, lsl],
                                                      wcol[:, 16 + a:17 + a], None, ALU.mult))
                        ctx.f(v, "V", v.tensor_copy(II3[:, 0:1024], LNJ2[:, 0:1024]))
                        ctx.f(v, "V", v.tensor_copy(LNJ2[:, 0:1024], II3[:, 0:1024]))
                        ctx.f(v, "V", v.tensor_scalar(wq8[:, sl], LNJ2[:, 0:1024],
                                                      127.0, -128.0, ALU.min, ALU.max))
            wq8_tok = ctx.tok("V")

            wtw = []
            for po in range(8):
                for d in range(8):
                    def mk(po=po, d=d):
                        def mm(t, i):
                            pb = (ctx.P0 if i % 2 == 0 else ctx.P1)[:].bitcast(BF16)
                            return t.transpose(pb[:, 0:128],
                                               wq8[:, po * 1024 + d * 128:po * 1024 + d * 128 + 128],
                                               ctx.identb[:])

                        def ev(e, i):
                            pb = (ctx.P0 if i % 2 == 0 else ctx.P1)[:].bitcast(BF16)
                            return e.copy(state["wqT"][:, d * 1024 + po * 128:d * 1024 + po * 128 + 128],
                                          pb[:, 0:128])
                        return mm, ev
                    wtw.append(mk())
            mm_pipeline(ctx, wtw, evac="S", t_prewait=[wq8_tok])
            state["w_done"] = (ctx.tok("S"), wsb_tok)

        rstd, negb, mean, _ = emit_ln(ctx, wait_tok=ctx.x_tok, apply=False)
        amscr = ctx.sb("amscr", [128, 48], F32, es)
        mx = amscr[:, 0:NT]
        mn = amscr[:, NT:2 * NT]
        am = amscr[:, 2 * NT:3 * NT]
        sA = ctx.sb("sAq", [128, NT], F32, es)
        sB = ctx.sb("sBq", [128, NT], F32, es)
        xqt = ctx.sb("xqtq", [128, D], BF16, es)
        vt_ln = ctx.tok("V")

        @ctx.section("V")
        def _(v):
            ctx.wait(v, ctx.x_tok)
            ctx.wait(v, vt_ln)
            for tt in range(NT):
                ctx.inc(v.tensor_reduce(mx[:, tt:tt + 1], ctx.X(tt), AX.X, ALU.max), "V")
            for tt in range(NT):
                ctx.inc(v.tensor_reduce(mn[:, tt:tt + 1], ctx.X(tt), AX.X, ALU.min), "V")
            v.wait_ge(ctx.sem["V"], ctx.cnt["V"])
            ctx.f(v, "V", v.tensor_tensor(mx[:], mx[:], mean[:], ALU.subtract))
            ctx.f(v, "V", v.tensor_tensor(mn[:], mean[:], mn[:], ALU.subtract))
            ctx.f(v, "V", v.tensor_tensor(mx[:], mx[:], mn[:], ALU.max))
            ctx.f(v, "V", v.tensor_tensor(am[:], mx[:], rstd[:], ALU.mult))
            ctx.f(v, "V", v.tensor_scalar(am[:], am[:], 1e-8, None, ALU.max))
            ctx.f(v, "V", v.reciprocal(sA[:], am[:]))
            ctx.f(v, "V", v.tensor_scalar(sA[:], sA[:], 127.0, None, ALU.mult))
            ctx.f(v, "V", v.tensor_tensor(sB[:], negb[:], sA[:], ALU.mult))
            ctx.inc(v.tensor_tensor(sA[:], sA[:], rstd[:], ALU.mult), "V")
        sab_tok = ctx.tok("V")
        ctx.dump("amq", amscr[:], sab_tok)

        xqfm = ctx.XFM
        xq_toks = []
        for tt in range(NT):
            @ctx.section("S")
            def _(s, tt=tt):
                if tt == 0:
                    ctx.wait(s, sab_tok)
                else:
                    ctx.wait(s, xq_toks[tt - 1])
                ctx.inc(s.activation(LNJ2[:, 0:D], ctx.X(tt), AF.Identity,
                                     bias=sB[:, tt:tt + 1], scale=sA[:, tt:tt + 1]), "S")
            u_tok = ctx.tok("S")

            @ctx.section("V")
            def _(v, u_tok=u_tok, tt=tt, tguard=("T", ctx.cnt["T"])):
                ctx.wait(v, u_tok)
                ctx.wait(v, tguard)
                ctx.f(v, "V", v.tensor_copy(II3[:, 0:D], LNJ2[:, 0:D]))
                ctx.f(v, "V", v.tensor_copy(LNJ2[:, 0:D], II3[:, 0:D]))
                ctx.inc(v.tensor_scalar(xqt[:], LNJ2[:, 0:D], 127.0, -128.0,
                                        ALU.min, ALU.max), "V")
            xq_toks.append(ctx.tok("V"))

            tw = []
            for d in range(ND):
                def mk(d=d, tt=tt):
                    def mm(t, i):
                        pb = (ctx.P0 if i % 2 == 0 else ctx.P1)[:].bitcast(BF16)
                        return t.transpose(pb[:, 0:128], xqt[:, d * 128:(d + 1) * 128],
                                           ctx.identb[:])

                    def ev(e, i):
                        pb = (ctx.P0 if i % 2 == 0 else ctx.P1)[:].bitcast(BF16)
                        return ecopy(e, xqfm[:, d * T + tt * 128:d * T + tt * 128 + 128],
                                     pb[:, 0:128])
                    return mm, ev
                tw.append(mk())
            mm_pipeline(ctx, tw, evac="V", t_prewait=[xq_toks[-1]])
        xq_done = ctx.tok("V")
        ctx.dump("xqfmq", xqfm[:], xq_done)

        nrounds = NT * 2
        t0 = ctx.cnt["T"]; s0 = ctx.cnt["S"]; v0 = ctx.cnt["V"]
        guards = ctx.guard_tokens()
        ctx.cnt["T"] = t0 + nrounds
        ctx.cnt["S"] = s0 + nrounds
        ctx.cnt["V"] = v0 + 3 * nrounds
        ctx.psum_guard["S"] = ctx.cnt["S"]
        ctx.psum_guard["V"] = ctx.cnt["V"]

        @ctx.section("T")
        def _(t, t0=t0, s0=s0):
            ctx.wait(t, xq_done)
            ctx.wait(t, state["w_done"][0])
            for tok in guards:
                ctx.wait(t, tok)
            for i in range(nrounds):
                tt, oc = i // 2, i % 2
                pb = ctx.P0 if i % 2 == 0 else ctx.P1
                if i >= 2:
                    t.wait_ge(ctx.sem["S"], s0 + i - 1)
                for d in range(ND):
                    r = t.matmul(pb[:], xqfm[:, d * T + tt * 128:d * T + tt * 128 + 128],
                                 state["wqT"][:, d * 1024 + oc * 512:d * 1024 + oc * 512 + 512],
                                 start=(d == 0), stop=(d == ND - 1))
                r.then_inc(ctx.sem["T"], 1)

        @ctx.section("S")
        def _(s, t0=t0, s0=s0, v0=v0):
            for i in range(nrounds):
                tt = i // 2
                pb = ctx.P0 if i % 2 == 0 else ctx.P1
                s.wait_ge(ctx.sem["T"], t0 + i + 1)
                if i >= 2:
                    s.wait_ge(ctx.sem["V"], v0 + 3 * (i - 1))
                s.activation(LNJ2[:, (i % 2) * 512:(i % 2) * 512 + 512],
                             pb[:], AF.Identity, scale=am[:, tt:tt + 1])\
                    .then_inc(ctx.sem["S"], 1)

        @ctx.section("V")
        def _(v, s0=s0, v0=v0):
            ctx.wait(v, state["w_done"][1])
            for i in range(nrounds):
                tt, oc = i // 2, i % 2
                v.wait_ge(ctx.sem["S"], s0 + i + 1)
                sl = LNJ2[:, (i % 2) * 512:(i % 2) * 512 + 512]
                xsl = ctx.X(tt)[:, oc * 512:(oc + 1) * 512]
                v.tensor_tensor(sl, sl, state["wsb"][:, oc * 512:(oc + 1) * 512],
                                ALU.mult).then_inc(ctx.sem["V"], 1)
                v.wait_ge(ctx.sem["V"], v0 + 3 * i + 1)
                v.tensor_tensor(sl, sl, state["bias_bc"][:, oc * 512:(oc + 1) * 512],
                                ALU.add).then_inc(ctx.sem["V"], 1)
                v.wait_ge(ctx.sem["V"], v0 + 3 * i + 2)
                v.tensor_tensor(xsl, xsl, sl, ALU.add).then_inc(ctx.sem["V"], 1)
        ctx.x_tok = ctx.tok("V")


def _out_pieces(p, shift, Tp, Tn=2048):
    pieces = []
    run = None
    for j in range(120):
        tpr = p * 120 + j
        ok = tpr < Tp
        t = (tpr + shift) % Tp if ok else 0
        ok = ok and t < Tn
        if ok and run is not None and t == run[2] + (j - run[0]):
            run = (run[0], j + 1, run[2])
        elif ok:
            if run:
                pieces.append((run[0], run[1], run[2]))
            run = (j, j + 1, t)
        else:
            if run:
                pieces.append((run[0], run[1], run[2]))
            run = None
    if run:
        pieces.append((run[0], run[1], run[2]))
    return pieces


def emit_l1_attn_b(ctx, inp):
    nc = ctx.nc
    ctx.barrier()
    es = contextlib.ExitStack()
    with es:
        DWNT = ctx.sb("DWNT", [128, 8 * 512], BF16, es)
        UPT = ctx.sb("UPT", [128, 4 * 1024], BF16, es)
        NW1B = ctx.sb("NW1B", [128, 512], F32, es)
        NW2B = ctx.sb("NW2B", [128, 1024], F32, es)
        DBB = ctx.sb("DBB", [128, 512], F32, es)
        UBB = ctx.sb("UBB", [128, 1024], F32, es)
        XIT = ctx.sb("XITb", [128, 2048], BF16, es)

        sg0 = ctx.start_guard()

        @ctx.section("Y")
        def _(sy):
            for tokk in sg0:
                ctx.wait(sy, tokk)
            ctx.dma(sy, NW1B[:], inp["L1_attn_b_norm1_w"].broadcast_to((128, 512)))
            ctx.dma(sy, NW2B[:], inp["L1_attn_b_norm2_w"].broadcast_to((128, D)))
            ctx.dma(sy, DBB[:], inp["L1_attn_b_down_b"].broadcast_to((128, 512)))
            ctx.dma(sy, UBB[:], inp["L1_attn_b_up_b"].broadcast_to((128, D)))

        @ctx.section("G")
        def _(g):
            for tokk in sg0:
                ctx.wait(g, tokk)
            load_w(ctx, g, inp["L1_attn_b_down_w"], DWNT)
            load_w(ctx, g, inp["L1_attn_b_up_w"], UPT)
        dwn_ld = ctx.tok("LD")
        dwn_lg = ctx.tok("LG")

        _, _, _, lntok = emit_ln(ctx, wait_tok=ctx.x_tok)
        xfm_tok = emit_transpose_to_xfm(ctx, lntok)

        RS1 = ctx.LNrstd
        xi_apply = []
        for tt in range(NT):
            @ctx.section("T")
            def _(t, tt=tt, pg=ctx.guard_tokens()):
                if tt == 0:
                    ctx.wait(t, xfm_tok)
                    ctx.wait(t, dwn_ld)
                    ctx.wait(t, dwn_lg)
                    for tokk in pg:
                        ctx.wait(t, tokk)
                if tt >= 2:
                    ctx.wait(t, xi_apply[tt - 2])
                pb = ctx.P0 if tt % 2 == 0 else ctx.P1
                for d in range(ND):
                    r = t.matmul(pb[:], ctx.XF(d)[:, tt * 128:(tt + 1) * 128],
                                 DWNT[:, d * 512:(d + 1) * 512],
                                 start=(d == 0), stop=(d == ND - 1))
                ctx.inc(r, "T")
            dn_T = ctx.tok("T")

            @ctx.section("V")
            def _(v, tt=tt, dn_T=dn_T):
                ctx.wait(v, dn_T)
                pb = ctx.P0 if tt % 2 == 0 else ctx.P1
                ctx.inc(v.tensor_tensor(ctx.LNJ[:, (tt % 2) * 512:(tt % 2) * 512 + 512],
                                        pb[:, 0:512], DBB[:], ALU.add), "V")
            ev_V = ctx.tok("V")

            @ctx.section("V")
            def _(v, tt=tt, ev_V=ev_V):
                ctx.wait(v, ev_V)
                sq = ctx.LNJ[:, 1024 + (tt % 2) * 512:1024 + (tt % 2) * 512 + 512]
                xd = ctx.LNJ[:, (tt % 2) * 512:(tt % 2) * 512 + 512]
                ctx.f(v, "V", v.tensor_tensor(sq, xd, xd, ALU.mult))
                ctx.f(v, "V", v.tensor_reduce(ctx.LNsq[:, tt:tt + 1], sq, AX.X, ALU.add))
                ctx.f(v, "V", v.tensor_scalar(RS1[:, tt:tt + 1], ctx.LNsq[:, tt:tt + 1],
                                              1.0 / 512.0, 1e-6, ALU.mult, ALU.add))
                ctx.inc(v.reciprocal(RS1[:, tt:tt + 1], RS1[:, tt:tt + 1]), "V")
            ms_V = ctx.tok("V")

            @ctx.section("S")
            def _(s_, tt=tt, ms_V=ms_V):
                ctx.wait(s_, ms_V)
                ctx.inc(s_.activation(RS1[:, tt:tt + 1], RS1[:, tt:tt + 1], AF.Sqrt), "S")
            rs_S = ctx.tok("S")

            @ctx.section("V")
            def _(v, tt=tt, rs_S=rs_S, tg=ctx.tok("T")):
                ctx.wait(v, rs_S)
                ctx.wait(v, tg)
                ctx.inc(v.scalar_tensor_tensor(XIT[:, (tt % 2) * 512:(tt % 2) * 512 + 512],
                                               ctx.LNJ[:, (tt % 2) * 512:(tt % 2) * 512 + 512],
                                               RS1[:, tt:tt + 1], NW1B[:],
                                               ALU.mult, ALU.mult), "V")
            xi_apply.append(ctx.tok("V"))

            tw = []
            for j in range(4):
                def mk(j=j, tt=tt):
                    def mm(t, i):
                        pb = ctx.P2[:].bitcast(BF16)
                        return t.transpose(pb[:, 0:128],
                                           XIT[:, (tt % 2) * 512 + j * 128:(tt % 2) * 512 + (j + 1) * 128],
                                           ctx.identb[:])

                    def ev(e, i):
                        pb = ctx.P2[:].bitcast(BF16)
                        return ecopy(e, ctx.XF(4 + j)[:, tt * 128:(tt + 1) * 128], pb[:, 0:128])
                    return mm, ev
                tw.append(mk())
            mm_pipeline(ctx, tw, evac="V", nbuf=1, t_prewait=[xi_apply[-1]])
        xif_tok = ctx.tok("V")
        ctx.dump("xif", ctx.XFM[:, 8192:16384], xif_tok)

        es2 = contextlib.ExitStack()
        with es2:
            XROLL = ctx.sb("XROLL", [128, 2 * 2176], BF16, es2)
            QKVT = ctx.sb("QKVT", [128, 2 * 768], BF16, es2)
            PROJT = ctx.sb("PROJT", [128, 2 * 256], BF16, es2)
            BQB = ctx.sb("BQB", [128, 768], F32, es2)
            PBROW = ctx.sb("PBROW", [1, 256], F32, es2)
            PBC = ctx.sb("PBC", [128, 2], F32, es2)
            QKVS = ctx.sb("QKVS", [120, 768], BF16, es2)
            KT = ctx.sb("KTb", [32, 120], BF16, es2)
            QT = ctx.sb("QTb", [32, 120], BF16, es2)
            VA = ctx.sb("VAb", [120, 33], BF16, es2)
            SSb = ctx.sb("SSb", [120, 120], F32, es2)
            PPb = ctx.sb("PPb", [120, 120], BF16, es2)
            BT8 = ctx.sb("BT8", [120, 8 * 120], F32, es2)
            AMb = ctx.sb("AMb", [120, 120], F32, es2)
            MSb = ctx.sb("MSb", [120, 120], F32, es2)
            QMK = ctx.sb("QMK", [120, 120], I32, es2)
            T1b = ctx.sb("T1b", [120, 120], F32, es2)
            T2b = ctx.sb("T2b", [120, 120], F32, es2)
            SLI = ctx.sb("SLI", [1, 120], I32, es2)
            SLF = ctx.sb("SLF", [1, 120], F32, es2)
            SLP = ctx.sb("SLP", [120, 1], F32, es2)
            RPBB = ctx.sb("RPBB", [128, 312], F32, es2)
            OPACK = ctx.sb("OPACK", [120, 256], BF16, es2)
            ONT = ctx.sb("ONT", [128, 2 * 120], BF16, es2)
            RC = ctx.sb("RCb", [120, 1], F32, es2)

            @ctx.section("G")
            def _(g):
                ctx.inc(g.memset(VA[:, 32:33], 1.0), "G")
            va1_tok = ctx.tok("G")

            prev_end = (xif_tok, ctx.tok("V"), ctx.tok("T"))
            for br in range(2):
                w = 10 if br == 0 else 20
                shift = 5 if br == 0 else 10
                Tp = 2050 if br == 0 else 2060
                spp = 12 if br == 0 else 6
                R = 2 * w - 1
                nm = "1" if br == 0 else "2"
                hscale = 32.0 ** -0.5

                @ctx.section("Y")
                def _(sy, nm=nm, R=R, prev_end=prev_end):
                    ctx.wait(sy, prev_end[2])
                    ctx.dma(sy, BQB[:], inp[f"L1_attn_b_qkv{nm}_b"].broadcast_to((128, 768)))
                    ctx.dma(sy, PBROW[:], inp[f"L1_attn_b_proj{nm}_b"])
                    ctx.dma(sy, RPBB[:, 0:R * 8],
                            inp[f"L1_attn_b_rpb{nm}"].rearrange("(o r) h -> o (r h)", o=1)
                            .broadcast_to((128, R * 8)))

                @ctx.section("G")
                def _(g, nm=nm, prev_end=prev_end):
                    ctx.wait(g, prev_end[2])
                    load_w(ctx, g, inp[f"L1_attn_b_qkv{nm}_w"], QKVT)
                    load_w(ctx, g, inp[f"L1_attn_b_proj{nm}_w"], PROJT)
                br_ld = ctx.tok("LD")
                br_lg = ctx.tok("LG")

                pbw = []
                for oc in range(2):
                    def mk(oc=oc):
                        def mm(t, i):
                            return t.transpose(ctx.P2[0:128, 0:1],
                                               PBROW[0:1, oc * 128:(oc + 1) * 128],
                                               ctx.identf[0:1, 0:1])

                        def ev(e, i):
                            return e.copy(PBC[:, oc:oc + 1], ctx.P2[0:128, 0:1])
                        return mm, ev
                    pbw.append(mk())
                pbc_tok = mm_pipeline(ctx, pbw, evac="S", nbuf=1,
                                      t_prewait=[br_ld, prev_end[0]])

                @ctx.section("G")
                def _(g, prev_end=prev_end):
                    ctx.wait(g, prev_end[1])
                    ctx.inc(g.iota(QMK[:], [[1, 120]], base=0, channel_multiplier=0), "G")
                gf_tok = ctx.tok("G")

                @ctx.section("V")
                def _(v, w=w, shift=shift, gf_tok=gf_tok):
                    ctx.wait(v, gf_tok)
                    ctx.f(v, "V", v.tensor_single_scalar(MSb[:], QMK[:], w - shift, ALU.is_lt))
                    ctx.inc(v.tensor_single_scalar(SSb[:], QMK[:], w, ALU.is_lt), "V")
                fcmp_V = ctx.tok("V")

                @ctx.section("G")
                def _(g, fcmp_V=fcmp_V):
                    ctx.wait(g, fcmp_V)
                    ctx.inc(g.iota(QMK[:], [[0, 120]], base=0, channel_multiplier=1), "G")
                gp_tok = ctx.tok("G")

                @ctx.section("V")
                def _(v, w=w, shift=shift, gp_tok=gp_tok):
                    ctx.wait(v, gp_tok)
                    ctx.f(v, "V", v.tensor_single_scalar(T1b[:], QMK[:], w - shift, ALU.is_lt))
                    ctx.f(v, "V", v.tensor_single_scalar(T2b[:], QMK[:], w, ALU.is_lt))
                    ctx.f(v, "V", v.tensor_tensor(AMb[:], T1b[:], MSb[:], ALU.subtract))
                    ctx.f(v, "V", v.tensor_tensor(AMb[:], AMb[:], AMb[:], ALU.mult))
                    ctx.f(v, "V", v.tensor_tensor(AMb[:], AMb[:], SSb[:], ALU.mult))
                    ctx.f(v, "V", v.tensor_tensor(AMb[:], AMb[:], T2b[:], ALU.mult))
                    ctx.inc(v.tensor_scalar(AMb[:], AMb[:], -100.0, None, ALU.mult), "V")
                am_V = ctx.tok("V")

                @ctx.section("G")
                def _(g, spp=spp, w=w, am_V=am_V):
                    ctx.wait(g, am_V)
                    g.iota(QMK[:], [[1, 120]], base=0, channel_multiplier=-1)
                    g.iota(SLI[:], [[1, spp], [0, w]], base=0, channel_multiplier=0)
                    g.wait_ge(ctx.sem["G"], ctx.cnt["G"])
                    ctx.inc(g.tensor_copy(SLF[:], SLI[:]), "G")
                qmk_tok = ctx.tok("G")

                def slb_mm(t, i):
                    return t.matmul(ctx.P2[0:120, 0:120], ctx.onesf[0:1, 0:120],
                                    SLF[0:1, :], start=True, stop=True)

                def slb_ev(e, i):
                    return e.copy(MSb[:], ctx.P2[0:120, 0:120])
                slb_tok = mm_pipeline(ctx, [(slb_mm, slb_ev)], evac="S",
                                      nbuf=1, t_prewait=[qmk_tok])

                def slp_mm(t, i):
                    return t.transpose(ctx.P2[0:120, 0:1], SLF[0:1, :],
                                       ctx.identf[0:1, 0:1])

                def slp_ev(e, i):
                    return e.copy(SLP[:], ctx.P2[0:120, 0:1])
                slp_tok = mm_pipeline(ctx, [(slp_mm, slp_ev)], evac="S",
                                      nbuf=1, t_prewait=[slb_tok])

                @ctx.section("V")
                def _(v, R=R, w=w, slp_tok=slp_tok, br_ld=br_ld):
                    ctx.wait(v, slp_tok)
                    ctx.wait(v, br_ld)
                    ctx.f(v, "V", v.tensor_scalar(MSb[:], MSb[:], SLP[:, 0:1], None, ALU.is_equal))
                    for h in range(8):
                        ctx.f(v, "V", v.tensor_scalar(BT8[:, h * 120:(h + 1) * 120], MSb[:],
                                                      60000.0, -60000.0, ALU.mult, ALU.add))
                    for r in range(R):
                        ctx.f(v, "V", v.tensor_single_scalar(MSb[:], QMK[:], r - w + 1,
                                                             ALU.is_equal))
                        for h in range(8):
                            ctx.f(v, "V", v.scalar_tensor_tensor(
                                BT8[:, h * 120:(h + 1) * 120], MSb[:],
                                RPBB[0:120, r * 8 + h:r * 8 + h + 1],
                                BT8[:, h * 120:(h + 1) * 120], ALU.mult, ALU.add))
                btd_tok = ctx.tok("V")
                if br == 0:
                    ctx.dump("bt8", BT8[:], btd_tok)
                    ctx.dump("am", AMb[:], btd_tok)

                @ctx.section("V")
                def _(v, br=br, shift=shift, Tp=Tp, prev_end=prev_end):
                    ctx.wait(v, xif_tok)
                    ctx.wait(v, prev_end[2])
                    for kt in range(2):
                        src = ctx.XF(4 + br * 2 + kt)
                        dst = XROLL[:, kt * 2176:(kt + 1) * 2176]
                        n0 = 2048 - shift
                        ctx.f(v, "V", v.tensor_copy(dst[:, 0:n0], src[:, shift:2048]))
                        ctx.f(v, "V", v.memset(dst[:, n0:Tp - shift], 0.0))
                        ctx.f(v, "V", v.tensor_copy(dst[:, Tp - shift:Tp], src[:, 0:shift]))
                        ctx.inc(v.memset(dst[:, Tp:2176], 0.0), "V")
                xr_tok = ctx.tok("V")
                if br == 0:
                    ctx.dump("xroll", XROLL[:], xr_tok)

                prev_qkvs = [None]
                for p in range(18):
                    pieces = _out_pieces(p, shift, Tp)
                    is_last = (p == 17)

                    @ctx.section("T")
                    def _(t, p=p, xr_tok=xr_tok, br_ld=br_ld, prev_end=prev_end,
                          pv=prev_qkvs[0], pg=ctx.guard_tokens()):
                        if p == 0:
                            ctx.wait(t, xr_tok)
                            ctx.wait(t, br_ld)
                            ctx.wait(t, br_lg)
                            ctx.wait(t, prev_end[0])
                            for tokk in pg:
                                ctx.wait(t, tokk)
                        else:
                            ctx.wait(t, pv)
                        for cc in range(2):
                            csl = slice(cc * 512, 512 + cc * 256)
                            for kt in range(2):
                                r = t.matmul(ctx.PO[0:120, cc * 512:512 + cc * 256],
                                             XROLL[:, kt * 2176 + p * 120:kt * 2176 + (p + 1) * 120],
                                             QKVT[:, kt * 768 + cc * 512:kt * 768 + 512 + cc * 256],
                                             start=(kt == 0), stop=(kt == 1))
                            ctx.inc(r, "T")
                    qkv_T = ctx.tok("T")

                    @ctx.section("V")
                    def _(v, qkv_T=qkv_T):
                        ctx.wait(v, qkv_T)
                        ctx.inc(v.tensor_tensor(QKVS[:], ctx.PO[0:120, 0:768],
                                                BQB[0:120, :], ALU.add), "V")
                    qkvs_V = ctx.tok("V")
                    prev_qkvs[0] = qkvs_V

                    tb = ctx.cnt["T"]
                    vb = ctx.cnt["V"]
                    sbp = ctx.cnt["S"]
                    ssn = 2 if is_last else 1
                    vper = 2 + ssn + 1 + 2
                    ctx.cnt["T"] += 32
                    ctx.cnt["V"] += 8 * vper
                    ctx.cnt["S"] += 8

                    @ctx.section("T")
                    def _(t, tb=tb, vb=vb, sbp=sbp, vper=vper, qkvs_V=qkvs_V,
                          pg=ctx.guard_tokens()):
                        ctx.wait(t, qkvs_V)
                        for tokk in pg:
                            ctx.wait(t, tokk)
                        for h in range(8):
                            if h > 0:
                                t.wait_ge(ctx.sem["V"], vb + vper * (h - 1) + 2)
                            t.transpose(ctx.PB[0:32, 0:120],
                                        QKVS[0:120, h * 32:(h + 1) * 32],
                                        ctx.identb[0:120, 0:120])\
                                .then_inc(ctx.sem["T"], 1)
                            t.transpose(ctx.PB[0:32, 128:248],
                                        QKVS[0:120, 256 + h * 32:256 + (h + 1) * 32],
                                        ctx.identb[0:120, 0:120])\
                                .then_inc(ctx.sem["T"], 1)
                            t.wait_ge(ctx.sem["V"], vb + vper * h + 2)
                            t.matmul(ctx.P0[0:120, 0:120], KT[:], QT[:],
                                     start=True, stop=True).then_inc(ctx.sem["T"], 1)
                            t.wait_ge(ctx.sem["S"], sbp + h + 1)
                            if h > 0:
                                t.wait_ge(ctx.sem["V"], vb + vper * h)
                            t.matmul(ctx.P1[0:120, 0:33], PPb[:], VA[:],
                                     start=True, stop=True).then_inc(ctx.sem["T"], 1)

                    @ctx.section("V")
                    def _(v, tb=tb, vb=vb, sbp=sbp, vper=vper, is_last=is_last,
                          hscale=hscale):
                        for h in range(8):
                            v.wait_ge(ctx.sem["T"], tb + 4 * h + 2)
                            v.tensor_scalar(QT[:], ctx.PB[0:32, 0:120], hscale,
                                            None, ALU.mult).then_inc(ctx.sem["V"], 1)
                            v.tensor_copy(KT[:], ctx.PB[0:32, 128:248])\
                                .then_inc(ctx.sem["V"], 1)
                            v.wait_ge(ctx.sem["T"], tb + 4 * h + 3)
                            if h == 0:
                                ctx.wait(v, btd_tok)
                                v.wait_ge(ctx.sem["T"], tb)
                                v.wait_ge(ctx.sem["S"], sbp)
                            else:
                                v.wait_ge(ctx.sem["S"], sbp + h)
                                v.wait_ge(ctx.sem["T"], tb + 4 * (h - 1) + 4)
                            if is_last:
                                v.tensor_tensor(SSb[:], ctx.P0[0:120, 0:120],
                                                BT8[:, h * 120:(h + 1) * 120], ALU.add)\
                                    .then_inc(ctx.sem["V"], 1)
                                v.wait_ge(ctx.sem["V"], vb + vper * h + 3)
                                v.tensor_tensor(SSb[:], SSb[:], AMb[:], ALU.add)\
                                    .then_inc(ctx.sem["V"], 1)
                            else:
                                v.tensor_tensor(SSb[:], ctx.P0[0:120, 0:120],
                                                BT8[:, h * 120:(h + 1) * 120], ALU.add)\
                                    .then_inc(ctx.sem["V"], 1)
                            v.tensor_copy(VA[0:120, 0:32],
                                          QKVS[0:120, 512 + h * 32:512 + (h + 1) * 32])\
                                .then_inc(ctx.sem["V"], 1)
                            v.wait_ge(ctx.sem["T"], tb + 4 * h + 4)
                            v.reciprocal(RC[:], ctx.P1[0:120, 32:33])\
                                .then_inc(ctx.sem["V"], 1)
                            v.wait_ge(ctx.sem["V"], vb + vper * h + 2 + ssn + 2)
                            v.tensor_scalar(OPACK[:, h * 32:(h + 1) * 32],
                                            ctx.P1[0:120, 0:32], RC[:, 0:1],
                                            None, ALU.mult).then_inc(ctx.sem["V"], 1)

                    @ctx.section("S")
                    def _(s_, tb=tb, vb=vb, sbp=sbp, vper=vper, ssn=ssn):
                        for h in range(8):
                            s_.wait_ge(ctx.sem["V"], vb + vper * h + 2 + ssn + 1)
                            if h > 0:
                                s_.wait_ge(ctx.sem["T"], tb + 4 * (h - 1) + 4)
                            else:
                                s_.wait_ge(ctx.sem["T"], tb)
                            s_.activation(PPb[:], SSb[:], AF.Exp)\
                                .then_inc(ctx.sem["S"], 1)

                    @ctx.section("T")
                    def _(t, rc_V=("V", vb + 8 * vper), kte=("V", vb + 7 * vper + 2)):
                        ctx.wait(t, rc_V)
                        ctx.wait(t, kte)
                        pbt = ctx.PB
                        t.transpose(pbt[0:128, 256:376], OPACK[:, 0:128],
                                    ctx.identb[0:120, 0:120]).then_inc(ctx.sem["T"], 1)
                        ctx.cnt["T"] += 1
                        ctx.inc(t.transpose(pbt[0:128, 376:496], OPACK[:, 128:256],
                                            ctx.identb[0:120, 0:120]), "T")
                    opt_T = ctx.tok("T")

                    @ctx.section("V")
                    def _(v, opt_T=opt_T):
                        ctx.wait(v, opt_T)
                        v.tensor_copy(ONT[:, 0:120], ctx.PB[0:128, 256:376])\
                            .then_inc(ctx.sem["V"], 1)
                        ctx.cnt["V"] += 1
                        ctx.inc(v.tensor_copy(ONT[:, 120:240], ctx.PB[0:128, 376:496]), "V")
                    ont_V = ctx.tok("V")

                    @ctx.section("T")
                    def _(t, ont_V=ont_V, pbc_tok=pbc_tok, vg2=ctx.tok("V")):
                        ctx.wait(t, ont_V)
                        ctx.wait(t, pbc_tok)
                        ctx.wait(t, vg2)
                        for oc in range(2):
                            for kt in range(2):
                                r = t.matmul(ctx.P2[0:128, oc * 120:(oc + 1) * 120],
                                             PROJT[:, kt * 256 + oc * 128:kt * 256 + (oc + 1) * 128],
                                             ONT[:, kt * 120:(kt + 1) * 120],
                                             start=(kt == 0), stop=(kt == 1))
                            ctx.inc(r, "T")
                    pj_T = ctx.tok("T")

                    @ctx.section("V")
                    def _(v, p=p, br=br, pj_T=pj_T, pieces=pieces):
                        ctx.wait(v, pj_T)
                        n = 0
                        for oc in range(2):
                            for (j0, j1, t0) in pieces:
                                r = v.tensor_scalar(
                                    ctx.XF(br * 2 + oc)[:, t0:t0 + (j1 - j0)],
                                    ctx.P2[0:128, oc * 120 + j0:oc * 120 + j1],
                                    PBC[:, oc:oc + 1], None, ALU.add)
                                r.then_inc(ctx.sem["V"], 1)
                                n += 1
                        ctx.cnt["V"] += n
                    of_S = ctx.tok("V")
                    ctx.psum_guard["S"] = ctx.cnt["S"]
                    ctx.psum_guard["V"] = ctx.cnt["V"]

                prev_end = (of_S, ctx.tok("V"), ctx.tok("T"))
                if br == 0:
                    ctx.dump("qkvs17", QKVS[:], prev_end[1])
            branches_done = prev_end
            ctx.dump("of", ctx.XFM[:, 0:8192], branches_done[0])

        up_rounds = []
        for tt in range(NT):
            for oc in range(2):
                def mk(tt=tt, oc=oc):
                    def mm(t, i):
                        pb = ctx.P0 if i % 2 == 0 else ctx.P1
                        for kt in range(4):
                            r = t.matmul(pb[:], ctx.XF(kt)[:, tt * 128:(tt + 1) * 128],
                                         UPT[:, kt * 1024 + oc * 512:kt * 1024 + oc * 512 + 512],
                                         start=(kt == 0), stop=(kt == 3))
                        return r

                    def ev(e, i):
                        pb = ctx.P0 if i % 2 == 0 else ctx.P1
                        t1 = ctx.LNJ[:, 1024 + (i % 2) * 512:1024 + (i % 2) * 512 + 512]
                        xsl = ctx.X(tt)[:, oc * 512:(oc + 1) * 512]
                        lsl = ctx.XL(tt)[:, oc * 512:(oc + 1) * 512]
                        e.tensor_tensor(t1, pb[:, 0:512], UBB[:, oc * 512:(oc + 1) * 512], ALU.add)
                        e.tensor_tensor(xsl, xsl, t1, ALU.add)
                        e.tensor_tensor(xsl, xsl, lsl, ALU.add)
                        return e.tensor_tensor(lsl, lsl, t1, ALU.add)
                    return mm, ev
                up_rounds.append(mk())
        up_tok = mm_pipeline(ctx, up_rounds, evac="V",
                             t_prewait=[branches_done[0], dwn_ld, dwn_lg])
        ctx.dump("xp", ctx.XLNB[:], up_tok)
        ctx.dump("xres_up", ctx.XRES[:], up_tok)

        RS2 = ctx.LNnegb
        h_apply = []
        for tt in range(NT):
            @ctx.section("V")
            def _(v, tt=tt, up_tok=up_tok):
                if tt == 0:
                    ctx.wait(v, up_tok)
                sq = ctx.LNJ[:, (tt % 2) * 1024:(tt % 2) * 1024 + 1024]
                ctx.f(v, "V", v.tensor_tensor(sq, ctx.XL(tt), ctx.XL(tt), ALU.mult))
                ctx.f(v, "V", v.tensor_reduce(ctx.LNsq[:, tt:tt + 1], sq, AX.X, ALU.add))
                ctx.f(v, "V", v.tensor_scalar(RS2[:, tt:tt + 1], ctx.LNsq[:, tt:tt + 1],
                                              1.0 / 1024.0, 1e-6, ALU.mult, ALU.add))
                ctx.inc(v.reciprocal(RS2[:, tt:tt + 1], RS2[:, tt:tt + 1]), "V")
            ms2_V = ctx.tok("V")

            @ctx.section("S")
            def _(s_, tt=tt, ms2_V=ms2_V):
                ctx.wait(s_, ms2_V)
                ctx.inc(s_.activation(RS2[:, tt:tt + 1], RS2[:, tt:tt + 1], AF.Sqrt), "S")
            rs2_S = ctx.tok("S")

            @ctx.section("V")
            def _(v, tt=tt, rs2_S=rs2_S, tg=ctx.tok("T")):
                ctx.wait(v, rs2_S)
                ctx.wait(v, tg)
                ctx.inc(v.scalar_tensor_tensor(XIT[:, (tt % 2) * 1024:(tt % 2) * 1024 + 1024],
                                               ctx.XL(tt), RS2[:, tt:tt + 1], NW2B[:],
                                               ALU.mult, ALU.mult), "V")
            h_apply.append(ctx.tok("V"))

            tw = []
            for j in range(ND):
                def mk(j=j, tt=tt):
                    def mm(t, i):
                        pb = (ctx.P0 if i % 2 == 0 else ctx.P1)[:].bitcast(BF16)
                        return t.transpose(pb[:, 0:128],
                                           XIT[:, (tt % 2) * 1024 + j * 128:(tt % 2) * 1024 + (j + 1) * 128],
                                           ctx.identb[:])

                    def ev(e, i):
                        pb = (ctx.P0 if i % 2 == 0 else ctx.P1)[:].bitcast(BF16)
                        return ecopy(e, ctx.XF(j)[:, tt * 128:(tt + 1) * 128], pb[:, 0:128])
                    return mm, ev
                tw.append(mk())
            mm_pipeline(ctx, tw, evac="V", t_prewait=[h_apply[-1]])
        hf_tok = ctx.tok("V")
        ctx.dump("hf", ctx.XFM[:], hf_tok)
        ctx.dump("rs2", RS2[:], hf_tok)
        ctx.dump("msq2", ctx.LNsq[:], hf_tok)

        W1T = ctx.XLNB[:, 0:16384]
        W2Q = ctx.sb("W2Q", [128, 2 * 4 * 1024], BF16, es)
        B1ROW = ctx.sb("B1ROW", [1, 2048], F32, es)
        B1C = ctx.sb("B1C", [128, 16], F32, es)
        B2BC = ctx.sb("B2BC", [128, 1024], F32, es)
        G1 = ctx.sb("G1", [128, 256], BF16, es)

        @ctx.section("Y")
        def _(sy):
            ctx.wait(sy, branches_done[0])
            ctx.wait(sy, branches_done[1])
            ctx.wait(sy, branches_done[2])
            ctx.dma(sy, B1ROW[:], inp["L1_attn_b_ffn_b1"])
            ctx.dma(sy, B2BC[:], inp["L1_attn_b_ffn_b2"].broadcast_to((128, D)))

        @ctx.section("G")
        def _(g, hf_tok=hf_tok, vg=ctx.tok("V")):
            ctx.wait(g, hf_tok)
            ctx.wait(g, vg)
            load_w(ctx, g, inp["L1_attn_b_ffn_w1"], W1T)
            load_w(ctx, g, inp["L1_attn_b_ffn_w2"][0:512, :], W2Q[:, 0:4096])
        w1_ld = ctx.tok("LD")
        w1_lg = ctx.tok("LG")

        b1w = []
        for gt in range(16):
            def mk(gt=gt):
                def mm(t, i):
                    return t.transpose(ctx.P2[0:128, 0:1],
                                       B1ROW[0:1, gt * 128:(gt + 1) * 128],
                                       ctx.identf[0:1, 0:1])

                def ev(e, i):
                    return e.copy(B1C[:, gt:gt + 1], ctx.P2[0:128, 0:1])
                return mm, ev
            b1w.append(mk())
        b1c_tok = mm_pipeline(ctx, b1w, evac="S", nbuf=1, t_prewait=[w1_ld])

        @ctx.section("V")
        def _(v, w1_ld=w1_ld):
            ctx.wait(v, w1_ld)
            for tt in range(NT):
                for oc in range(2):
                    xsl = ctx.X(tt)[:, oc * 512:(oc + 1) * 512]
                    v.tensor_tensor(xsl, xsl, B2BC[:, oc * 512:(oc + 1) * 512], ALU.add)\
                        .then_inc(ctx.sem["V"], 1)
                    ctx.cnt["V"] += 1
        b2_V = ctx.tok("V")

        for q in range(4):
            if q < 3:
                @ctx.section("G")
                def _(g, q=q, tg=ctx.tok("T")):
                    ctx.wait(g, tg)
                    load_w(ctx, g, inp["L1_attn_b_ffn_w2"][(q + 1) * 512:(q + 2) * 512, :],
                           W2Q[:, ((q + 1) % 2) * 4096:((q + 1) % 2) * 4096 + 4096])
                w2_lds = ctx.tok("LG")
            for c in range(8):
                tb = ctx.cnt["T"]
                sb = ctx.cnt["S"]
                vb = ctx.cnt["V"]
                guards = ctx.guard_tokens()
                ctx.cnt["T"] += 8
                ctx.cnt["S"] += 4
                ctx.cnt["V"] += 4
                ctx.psum_guard["V"] = ctx.cnt["V"]

                @ctx.section("T")
                def _(t, q=q, c=c, tb=tb, sb=sb, vb=vb, guards=guards):
                    if c == 0:
                        ctx.wait(t, w1_lg if q == 0 else ("LG", ctx.cnt["LG"]))
                        ctx.wait(t, b1c_tok)
                        ctx.wait(t, hf_tok)
                        ctx.wait(t, b2_V)
                        for tokk in guards:
                            ctx.wait(t, tokk)
                    for gtq in range(4):
                        gt = q * 4 + gtq
                        pb = ctx.P0 if gtq % 2 == 0 else ctx.P1
                        if gtq >= 2:
                            t.wait_ge(ctx.sem["S"], sb + gtq - 1)
                        for d in range(ND):
                            r = t.matmul(pb[0:128, 0:256],
                                         W1T[:, d * 2048 + gt * 128:d * 2048 + (gt + 1) * 128],
                                         ctx.XF(d)[:, c * 256:(c + 1) * 256],
                                         start=(d == 0), stop=(d == ND - 1))
                        r.then_inc(ctx.sem["T"], 1)
                        t.wait_ge(ctx.sem["S"], sb + gtq + 1)
                        if gtq == 0:
                            t.wait_ge(ctx.sem["V"], vb)
                        for sg in range(2):
                            for oc2 in range(2):
                                rr = t.matmul(ctx.PO[0:128, sg * 1024 + oc2 * 512:sg * 1024 + oc2 * 512 + 512],
                                              G1[:, sg * 128:(sg + 1) * 128],
                                              W2Q[:, (q % 2) * 4096 + gtq * 1024 + oc2 * 512:(q % 2) * 4096 + gtq * 1024 + oc2 * 512 + 512],
                                              start=(gtq == 0), stop=(gtq == 3))
                        rr.then_inc(ctx.sem["T"], 1)

                @ctx.section("S")
                def _(s_, q=q, c=c, tb=tb, sb=sb):
                    for gtq in range(4):
                        gt = q * 4 + gtq
                        pb = ctx.P0 if gtq % 2 == 0 else ctx.P1
                        s_.wait_ge(ctx.sem["T"], tb + 2 * gtq + 1)
                        s_.activation(G1[:], pb[0:128, 0:256], AF.Gelu,
                                      bias=B1C[:, gt:gt + 1]).then_inc(ctx.sem["S"], 1)

                @ctx.section("V")
                def _(v, q=q, c=c, tb=tb, vb=vb):
                    v.wait_ge(ctx.sem["T"], tb + 8)
                    for sg in range(2):
                        for oc2 in range(2):
                            xsl = ctx.X(2 * c + sg)[:, oc2 * 512:(oc2 + 1) * 512]
                            v.tensor_tensor(xsl, xsl,
                                            ctx.PO[0:128, sg * 1024 + oc2 * 512:sg * 1024 + oc2 * 512 + 512],
                                            ALU.add).then_inc(ctx.sem["V"], 1)
        ctx.x_tok = ctx.tok("V")
        ctx.dump("b2bc", B2BC[:], ctx.x_tok)
        ctx.dump("ubb", UBB[:], ctx.x_tok)
        ctx.dump("b1c", B1C[:], ctx.x_tok)
        ctx.dump("dbb", DBB[:], ctx.x_tok)
